# version 06
# baseline (speedup 1.0000x reference)
import math
from contextlib import ExitStack

import numpy as np
import ml_dtypes

import concourse.bass as bass
import concourse.mybir as mybir
from concourse.bass_utils import run_bass_kernel_spmd

F32 = mybir.dt.float32
BF16 = mybir.dt.bfloat16
AF = mybir.ActivationFunctionType
ALU = mybir.AluOpType
AX = mybir.AxisListType

ENGS = ("pe", "dve", "act", "pool", "sp")
BLK = {"pe": "tensor", "dve": "vector", "act": "scalar", "pool": "gpsimd", "sp": "sync"}
QUEUES = {"qsp": "sp", "qact": "act", "qpool": "pool"}
NSLOT = 6
QSLOTS = [f"{q}{i}" for q in QUEUES for i in range(NSLOT)]

N_CORES = 8
D = 1024
TFULL = 4096
THALF = 2048
NCHUNK = 8


class Prog:

    def __init__(self, name="prog"):
        self.nc = bass.Bass("TRN2", target_bir_lowering=False)
        self.es = ExitStack()
        self.name = name
        self.phases = []
        self.cur = {e: [] for e in ENGS}
        allkeys = list(ENGS) + QSLOTS + ["cc"]
        self.n = {k: 0 for k in allkeys}
        self.items = {e: [] for e in ENGS}
        self.clock = {e: {k: -1 for k in allkeys} for e in ENGS}
        self.acc = {}
        self.qrr = {q: 0 for q in QUEUES}
        self.sems = {}
        for k in allkeys:
            self.sems[k] = self.es.enter_context(self.nc.semaphore("sem_" + k))
        self.tagged = {e: [] for e in ENGS}
        self._built = False
        self.sb_tensors = {}
        self.dram_tensors = {}
        self._scratch_ctr = 0

    def push_scope(self, s):
        self.scope = s

    def pop_scope(self):
        self.scope = ""

    def sb(self, name, shape, dtype):
        name = getattr(self, "scope", "") + name
        t = self.es.enter_context(self.nc.sbuf_tensor(name, list(shape), dtype))
        self.sb_tensors[name] = t
        return t

    def psum(self, name, shape, dtype=F32):
        t = self.es.enter_context(self.nc.psum_tensor(name, list(shape), dtype))
        return t

    def dram_in(self, name, shape, dtype=F32):
        if name in self.dram_tensors:
            return self.dram_tensors[name]
        t = self.nc.dram_tensor(name, list(shape), dtype, kind="ExternalInput")
        self.dram_tensors[name] = t
        return t

    def dram_out(self, name, shape, dtype=F32):
        t = self.nc.dram_tensor(name, list(shape), dtype, kind="ExternalOutput")
        self.dram_tensors[name] = t
        return t

    def dram_tmp(self, name, shape, dtype=F32):
        name = getattr(self, "scope", "") + name
        if name in self.dram_tensors:
            return self.dram_tensors[name]
        t = self.nc.dram_tensor(name, list(shape), dtype)
        self.dram_tensors[name] = t
        return t

    def _res(self, ap):
        if ap is None:
            return None
        return ap.name if hasattr(ap, "name") else str(ap)

    def _rec(self, res):
        r = self.acc.get(res)
        if r is None:
            r = {"w": None, "r": {}}
            self.acc[res] = r
        return r

    def _need(self, consumer, key, idx):
        if key == consumer and key == "pe":
            return
        if self.clock[consumer][key] >= idx:
            return
        if key in ENGS:
            it = self.items[key][idx]
            if not it["tag"]:
                it["tag"] = True
                self.tagged[key].append(idx)
        self.cur[consumer].append({"kind": "wait", "key": key, "idx": idx})
        self.clock[consumer][key] = idx

    def _deps_and_record(self, me_key, consumer_eng, reads, writes):
        deps = []
        for ap in reads:
            res = self._res(ap)
            if res is None:
                continue
            rec = self._rec(res)
            if rec["w"] is not None:
                deps.append(rec["w"])
        for ap in writes:
            res = self._res(ap)
            if res is None:
                continue
            rec = self._rec(res)
            if rec["w"] is not None:
                deps.append(rec["w"])
            deps.extend(rec["r"].items())
        best = {}
        for k, idx in deps:
            best[k] = max(best.get(k, -1), idx)
        for k, idx in sorted(best.items()):
            self._need(consumer_eng, k, idx)
        idx_me = self.n[me_key]
        for ap in reads:
            res = self._res(ap)
            if res is not None:
                self._rec(res)["r"][me_key] = idx_me
        for ap in writes:
            res = self._res(ap)
            if res is not None:
                rec = self._rec(res)
                rec["w"] = (me_key, idx_me)
                rec["r"] = {}
        return idx_me

    def op(self, eng, fn, reads=(), writes=()):
        assert eng in ENGS
        self._deps_and_record(eng, eng, list(reads), list(writes))
        item = {"kind": "op", "fn": fn, "tag": False, "eng": eng}
        self.cur[eng].append(item)
        self.items[eng].append(item)
        self.n[eng] += 1

    def dma(self, q, out_ap, in_ap, **kw):
        assert q in QUEUES
        eng = QUEUES[q]
        slot = f"{q}{self.qrr[q]}"
        self.qrr[q] = (self.qrr[q] + 1) % NSLOT
        if self.n[slot] > 0:
            self._need(eng, slot, self.n[slot] - 1)
        self._deps_and_record(slot, eng, [in_ap], [out_ap])
        sem = self.sems[slot]
        fn = lambda e, o=out_ap, i=in_ap, kw=kw: e.dma_start(out=o, in_=i, **kw).then_inc(sem, 16)
        item = {"kind": "op", "fn": fn, "tag": False, "eng": eng}
        self.cur[eng].append(item)
        self.n[slot] += 1

    def idma_gather(self, out_ap, in_view, idx_ap):
        q = "qpool"
        eng = "pool"
        slot = f"{q}{self.qrr[q]}"
        self.qrr[q] = (self.qrr[q] + 1) % NSLOT
        if self.n[slot] > 0:
            self._need(eng, slot, self.n[slot] - 1)
        self._deps_and_record(slot, eng, [in_view, idx_ap], [out_ap])
        sem = self.sems[slot]
        fn = lambda e, o=out_ap, i=in_view, ix=idx_ap: \
            e.indirect_dma_start(o, None, i, bass.IndirectOffsetOnAxis(ix, 0)).then_inc(sem, 16)
        item = {"kind": "op", "fn": fn, "tag": False, "eng": eng}
        self.cur[eng].append(item)
        self.n[slot] += 1

    def cc(self, kind, op, groups, ins, outs):
        eng = "pool"
        self._deps_and_record("cc", eng, list(ins), list(outs))
        sem = self.sems["cc"]
        fn = lambda e, k=kind, o=op, g=groups, i=list(ins), u=list(outs): \
            e.collective_compute(k, o, g, ins=[a.opt() for a in i], outs=[a.opt() for a in u]).then_inc(sem, 1)
        item = {"kind": "op", "fn": fn, "tag": False, "eng": eng}
        self.cur[eng].append(item)
        self.n["cc"] += 1

    def barrier(self):
        if any(self.cur[e] for e in ENGS):
            self.phases.append(self.cur)
            self.cur = {e: [] for e in ENGS}
        for e in ENGS:
            for f in ENGS:
                self.clock[e][f] = self.n[f] - 1

    def mm(self, bank, lhsT, rhs, start=True, stop=True, tile_position=None):
        self.op("pe", lambda t, b=bank, l=lhsT, r=rhs, s=start, p=stop, tp=tile_position:
                t.matmul(b, l, r, start=s, stop=p, tile_position=tp),
                reads=[lhsT, rhs], writes=[bank])

    def transp(self, bank, in_, ident):
        self.op("pe", lambda t, b=bank, i=in_, d=ident: t.transpose(b, i, d),
                reads=[in_, ident], writes=[bank])

    def tt(self, out, a, b, aluop, eng="dve"):
        self.op(eng, lambda v, o=out, x=a, y=b, p=aluop: v.tensor_tensor(o, x, y, p),
                reads=[a, b], writes=[out])

    def ts(self, out, a, s1, s2, op0, op1=None, eng="dve", accum_out=None):
        rd = [a] + ([s1] if hasattr(s1, "name") else []) + ([s2] if hasattr(s2, "name") else [])
        wr = [out] + ([accum_out] if accum_out is not None else [])
        if op1 is None:
            self.op(eng, lambda v, o=out, x=a, u=s1, p=op0, ac=accum_out:
                    v.tensor_scalar(o, x, u, None, p, accum_out=ac), reads=rd, writes=wr)
        else:
            self.op(eng, lambda v, o=out, x=a, u=s1, w=s2, p=op0, q=op1, ac=accum_out:
                    v.tensor_scalar(o, x, u, w, p, q, accum_out=ac), reads=rd, writes=wr)

    def stt(self, out, in0, scalar, in1, op0, op1, eng="dve"):
        rd = [in0, in1] + ([scalar] if hasattr(scalar, "name") else [])
        self.op(eng, lambda v, o=out, a=in0, s=scalar, b=in1, p=op0, q=op1:
                v.scalar_tensor_tensor(o, a, s, b, p, q), reads=rd, writes=[out])

    def act(self, out, in_, func, bias=0.0, scale=1.0, accum_out=None, eng="act"):
        rd = [in_] + ([bias] if hasattr(bias, "name") else []) + ([scale] if hasattr(scale, "name") else [])
        wr = [out] + ([accum_out] if accum_out is not None else [])
        self.op(eng, lambda s, o=out, i=in_, f=func, b=bias, c=scale, ac=accum_out:
                s.activation(o, i, f, bias=b, scale=c, accum_out=ac), reads=rd, writes=wr)

    def red(self, out, in_, aluop=ALU.max, negate=False, eng="dve"):
        self.op(eng, lambda v, o=out, i=in_, p=aluop, n=negate:
                v.tensor_reduce(o, i, AX.X, p, negate=n), reads=[in_], writes=[out])

    def scan(self, out, d0, d1, init, op0, op1):
        self.op("dve", lambda v, o=out, a=d0, b=d1, i=init, p=op0, q=op1:
                v.tensor_tensor_scan(o, a, b, i, p, q), reads=[d0, d1], writes=[out])

    def copy(self, out, in_, eng="dve"):
        if eng == "act":
            self.act(out, in_, AF.Copy)
        else:
            self.op(eng, lambda v, o=out, i=in_: v.tensor_copy(o, i), reads=[in_], writes=[out])

    def memset(self, ap, c, eng="pool"):
        self.op(eng, lambda g, a=ap, x=c: g.memset(a, x), reads=[], writes=[ap])

    def iota(self, ap, pattern, base=0, channel_multiplier=0):
        self.op("pool", lambda g, a=ap, p=pattern, b=base, c=channel_multiplier:
                g.iota(a, p, base=b, channel_multiplier=c, allow_small_or_imprecise_dtypes=True),
                reads=[], writes=[ap])

    def affsel(self, out, in_, pattern, cmp, fill, base=0, channel_multiplier=0):
        self.op("pool", lambda g, o=out, i=in_, p=pattern, c=cmp, f=fill, b=base, m=channel_multiplier:
                g.affine_select(o, i, p, c, f, base=b, channel_multiplier=m),
                reads=[in_], writes=[out])

    def std_init(self):
        if hasattr(self, "ones1"):
            return
        sc = getattr(self, "scope", "")
        self.scope = ""
        self.ones1 = self.sb("std_ones1", [1, 128], F32)
        self.ones1b = self.sb("std_ones1b", [1, 128], BF16)
        self.onec = self.sb("std_onec", [128, 1], F32)
        self.ones128 = self.sb("std_ones128", [128, 1], BF16)
        self.identb = self.sb("std_identb", [128, 128], BF16)
        self.memset(self.ones1[:, :], 1.0)
        self.memset(self.ones1b[:, :], 1.0)
        self.memset(self.onec[:, :], 1.0)
        self.memset(self.ones128[:, :], 1.0)
        self.memset(self.identb[:, :], 1.0)
        self.affsel(self.identb[:, :], self.identb[:, :], [[1, 128]], ALU.is_equal,
                    0.0, base=0, channel_multiplier=-1)
        self.scope = sc

    def init_banks(self):
        if hasattr(self, "bks"):
            return
        self.bks = [self.psum(f"bank{i}", [128, 512], F32) for i in range(6)]
        self.tb = [self.psum(f"tbank{i}", [128, 1024], BF16) for i in range(2)]

    def _units(self, shape, dtype):
        p, c = shape
        bytes_per_part = c * (2 if dtype == BF16 else 4)
        u = max(1, -(-bytes_per_part // 2048))
        return 1 if u == 1 else (2 if u == 2 else 4)

    def acq(self, n, shape, dtype):
        if not hasattr(self, "_free"):
            self._free = {1: [], 2: [], 4: []}
            self._slabs = []
            self._viewmap = {}
        out = []
        for _ in range(n):
            u = self._units(shape, dtype)
            blk = self._take(u)
            slab, off = blk
            ap = self._slabs[slab][:, 512 * off: 512 * (off + u)]
            if dtype == BF16:
                ap = ap.bitcast(BF16)
            ap = ap[0:shape[0], 0:shape[1]]
            self._viewmap[id(ap)] = (blk, u)
            out.append(ap)
        return out

    def _take(self, u):
        if self._free[u]:
            return self._free[u].pop()
        if u < 4:
            big = self._take(2 * u)
            slab, off = big
            self._free[u].append((slab, off + u))
            return (slab, off)
        t = self.sb(f"slab{len(self._slabs)}", [128, 2048], F32)
        self._slabs.append(t)
        return (len(self._slabs) - 1, 0)

    def acq1(self, shape, dtype):
        return self.acq(1, shape, dtype)[0]

    def rel(self, tiles):
        if not isinstance(tiles, (list, tuple)):
            tiles = [tiles]
        for t in tiles:
            assert id(t) in self._viewmap, f"rel of unknown/already-released tile {t}"
            blk, u = self._viewmap.pop(id(t))
            self._free_blk(blk, u)

    def _free_blk(self, blk, u):
        if u < 4:
            slab, off = blk
            buddy = (slab, off ^ u)
            if buddy in self._free[u]:
                self._free[u].remove(buddy)
                self._free_blk((slab, min(off, off ^ u)), 2 * u)
                return
        self._free[u].append(blk)

    def constcol(self, value, dtype=F32):
        key = (float(value), dtype)
        if not hasattr(self, "_ccols"):
            self._ccols = {}
        if key not in self._ccols:
            t = self.sb(f"cc_{len(self._ccols)}", [128, 1], dtype)
            self.memset(t[:, :], float(value))
            self._ccols[key] = t
        return self._ccols[key][:, 0:1]

    def bcast_row(self, bank, row_ap, nrows=128, scale=None):
        self.mm(bank, self.ones1[0:1, 0:nrows], row_ap)

    def finish(self):
        if getattr(self, "defer_finish", False):
            self.barrier()
            return
        self.barrier()
        for q in QSLOTS:
            if self.n[q] > 0:
                cnt = self.n[q]
                self.cur["sp"].append({"kind": "rawwait", "key": q, "val": 16 * cnt})
        if self.n["cc"] > 0:
            self.cur["sp"].append({"kind": "rawwait", "key": "cc", "val": self.n["cc"]})
        self.phases.append(self.cur)
        self.cur = {e: [] for e in ENGS}

    def _resolve(self, it):
        key = it["key"]
        if it["kind"] == "rawwait":
            return self.sems[key], it["val"]
        idx = it["idx"]
        if key in ENGS:
            tg = self.tagged_sorted[key]
            import bisect
            v = bisect.bisect_right(tg, idx)
            assert v > 0
            return self.sems[key], v
        elif key != "cc":
            return self.sems[key], 16 * (idx + 1)
        else:
            return self.sems[key], idx + 1

    def build(self):
        assert not self._built
        self._built = True
        self.tagged_sorted = {e: sorted(self.tagged[e]) for e in ENGS}
        for ph in self.phases:
            with self.nc.Block() as block:
                for ename in ENGS:
                    items = ph[ename]
                    if not items:
                        continue
                    def replay(engobj, items=items, ename=ename):
                        for it in items:
                            if it["kind"] in ("wait", "rawwait"):
                                sem, v = self._resolve(it)
                                engobj.wait_ge(sem, v)
                            else:
                                ins = it["fn"](engobj)
                                if it["tag"]:
                                    ins.then_inc(self.sems[it["eng"]], 1)
                    getattr(block, BLK[ename])(replay)
        return self.nc



def load_wt(P, dram, name, kchunks, cols, dtype=BF16, q="qpool", rowoff=0):
    tiles = P.acq(kchunks, (128, cols), dtype)
    for k in range(kchunks):
        P.dma(q, tiles[k][:, :], dram.ap()[rowoff + 128 * k: rowoff + 128 * (k + 1), :])
    return tiles


def sb_tiles(P, name, n, shape, dtype):
    return [P.sb(f"{name}_{i}", list(shape), dtype) for i in range(n)]


def emit_load_x(P, x, xdram, T):
    for c in range(8):
        P.dma("qsp", x[c][:, :], xdram.ap()[128 * c:128 * (c + 1), :])


def emit_store_x(P, x, xdram, T):
    for c in range(8):
        P.dma("qsp", xdram.ap()[128 * c:128 * (c + 1), :], x[c][:, :])


def emit_ln(P, x, xout, T, pre="ln", eps=1e-5, xout32=None, gcol=None, bcol=None, keep_rows=False):
    sq = P.acq(2, (128, 512), F32)
    if keep_rows:
        mu = P.acq1((1, T), F32)
        rs = P.acq1((1, T), F32)
    else:
        mu_t = P.acq1((128, 512), F32)
        rs_t = P.acq1((128, 512), F32)
        mu, rs = mu_t, rs_t
    tmp = P.acq1((128, 512), F32)
    b_s, b_q, b_m, b_r = P.bks[0], P.bks[1], P.bks[2], P.bks[3]
    onec = P.onec
    for g in range(T // 512):
        sl = slice(512 * g, 512 * (g + 1))
        msl = sl if keep_rows else slice(0, 512)
        for c in range(8):
            P.mm(b_s[0:1, 0:512], onec[:, 0:1], x[c][:, sl], start=(c == 0), stop=(c == 7))
            P.act(sq[c % 2][:, :], x[c][:, sl], AF.Square)
            P.mm(b_q[0:1, 0:512], onec[:, 0:1], sq[c % 2][:, :], start=(c == 0), stop=(c == 7))
        P.ts(mu[0:1, msl], b_s[0:1, 0:512], 1.0 / 1024.0, None, ALU.mult)
        P.ts(rs[0:1, msl], b_q[0:1, 0:512], 1.0 / 1024.0, None, ALU.mult)
        P.tt(tmp[0:1, 0:512], mu[0:1, msl], mu[0:1, msl], ALU.mult)
        P.tt(rs[0:1, msl], rs[0:1, msl], tmp[0:1, 0:512], ALU.subtract)
        P.ts(rs[0:1, msl], rs[0:1, msl], eps, None, ALU.add)
        P.act(rs[0:1, msl], rs[0:1, msl], AF.Sqrt)
        P.op("dve", lambda v, o=rs[0:1, msl], i=rs[0:1, msl]: v.reciprocal(o, i),
             reads=[rs], writes=[rs])
        P.mm(b_m[:, 0:512], P.ones1[0:1, :], mu[0:1, msl])
        P.mm(b_r[:, 0:512], P.ones1[0:1, :], rs[0:1, msl])
        for c in range(8):
            P.tt(tmp[:, 0:512], x[c][:, sl], b_m[:, 0:512], ALU.subtract)
            if gcol is None:
                P.tt(xout[c][:, sl], tmp[:, 0:512], b_r[:, 0:512], ALU.mult)
                if xout32 is not None:
                    P.copy(xout32[c][:, sl], xout[c][:, sl], eng="act")
            else:
                P.tt(tmp[:, 0:512], tmp[:, 0:512], b_r[:, 0:512], ALU.mult)
                if bcol is None:
                    P.ts(xout[c][:, sl], tmp[:, 0:512], gcol[:, c:c + 1], None, ALU.mult)
                else:
                    P.ts(xout[c][:, sl], tmp[:, 0:512], gcol[:, c:c + 1], bcol[:, c:c + 1],
                         ALU.mult, ALU.add)
    P.rel(sq + [tmp])
    if not keep_rows:
        P.rel([mu_t, rs_t])
        return None, None
    return mu, rs


def gemm_to_dram(P, name, terms, out_dram, Mch, N, scale=1.0, tmp=None):
    own_tmp = tmp is None
    if own_tmp:
        tmp = P.acq1((128, N), BF16)
    for m in range(Mch):
        for g in range(N // 512):
            sl = slice(512 * g, 512 * (g + 1))
            bank = P.bks[4 + (g % 2)]
            first = True
            for ti, (lt, rt) in enumerate(terms):
                for k in range(len(lt)):
                    P.mm(bank[:, 0:512], lt[k][:, 128 * m:128 * (m + 1)], rt[k][:, sl],
                         start=first, stop=(k == len(lt) - 1 and ti == len(terms) - 1))
                    first = False
            if scale == 1.0:
                P.copy(tmp[:, sl], bank[:, 0:512], eng="act")
            else:
                P.act(tmp[:, sl], bank[:, 0:512], AF.Copy, scale=P.constcol(scale))
        P.dma("qsp", out_dram.ap()[128 * m:128 * (m + 1), :], tmp[:, 0:N])
        P.barrier()
    if own_tmp:
        P.rel(tmp)



def emit_l1f(P, T=THALF, d_xin=None, d_xout=None):
    d_xin = d_xin or P.dram_in("x_in", [D, T])
    d_xout = d_xout or P.dram_out("x_out", [D, T])
    dC = P.dram_in("dftC", [D, D])
    dS = P.dram_in("dftS", [D, D])
    dWR = P.dram_in("w_freq_real", [D, D])
    dWI = P.dram_in("w_freq_imag", [D, D])
    dbias = P.dram_in("bias_chunks", [128, 8])
    dB1 = P.dram_tmp("scrB1", [D, D], BF16)
    dB2 = P.dram_tmp("scrB2", [D, D], BF16)
    dM = P.dram_tmp("scrM", [D, D], BF16)

    P.std_init()
    P.init_banks()
    x = P.acq(8, (128, T), F32)
    emit_load_x(P, x, d_xin, T)
    xln = P.acq(8, (128, T), BF16)
    emit_ln(P, x, xln, T)
    P.barrier()
    tA = load_wt(P, dC, "tA", 8, D)
    tB = load_wt(P, dS, "tB", 8, D)
    tC_ = load_wt(P, dWR, "tC", 8, D)
    tD_ = load_wt(P, dWI, "tD", 8, D)
    P.barrier()
    gemm_to_dram(P, "gB1", [(tC_, tA), (tD_, tB)], dB1, 8, D)
    for k in range(8):
        P.act(tB[k][:, :], tB[k][:, :], AF.Copy, scale=P.constcol(-1.0))
    gemm_to_dram(P, "gB2", [(tD_, tA), (tC_, tB)], dB2, 8, D)
    P.barrier()
    for k in range(8):
        P.dma("qpool", tC_[k][:, :], dB1.ap()[128 * k:128 * (k + 1), :])
        P.dma("qpool", tD_[k][:, :], dB2.ap()[128 * k:128 * (k + 1), :])
    gemm_to_dram(P, "gM", [(tC_, tA), (tD_, tB)], dM, 8, D, scale=1.0 / 1024.0)
    P.barrier()
    for k in range(8):
        P.dma("qpool", tA[k][:, :], dM.ap()[128 * k:128 * (k + 1), :])
    bcol = P.sb("bcol", [128, 8], F32)
    P.dma("qsp", bcol[:, :], dbias.ap())
    hb = P.sb("hb", [128, 8], F32)
    P.act(hb[:, :], bcol[:, :], AF.Copy, scale=P.constcol(0.5))
    P.barrier()
    scr = P.acq(2, (128, 512), F32)
    for oc in range(8):
        for k in range(8):
            for g in range(T // 512):
                bank = P.bks[g % 4]
                P.mm(bank[:, 0:512], tA[k][:, 128 * oc:128 * (oc + 1)],
                     xln[k][:, 512 * g:512 * (g + 1)], start=(k == 0), stop=(k == 7))
        for g in range(T // 512):
            sl = slice(512 * g, 512 * (g + 1))
            bank = P.bks[g % 4]
            s = scr[g % 2]
            P.ts(s[:, :], bank[:, 0:512], 0.5, hb[:, oc:oc + 1], ALU.mult, ALU.add)
            P.tt(x[oc][:, sl], x[oc][:, sl], s[:, :], ALU.add)
    P.barrier()
    emit_store_x(P, x, d_xout, T)
    P.rel(x + xln + tA + tB + tC_ + tD_ + scr)
    P.finish()


def bcast_to_sbuf(P, row_ap, out_tile, W):
    for g in range((W + 511) // 512):
        sl = slice(512 * g, min(512 * (g + 1), W))
        n = sl.stop - sl.start
        bank = P.bks[4 + (g % 2)]
        P.mm(bank[:, 0:n], P.ones1[0:1, :], row_ap[0:1, sl])
        P.copy(out_tile[:, sl], bank[:, 0:n], eng="act")


def identf_init(P):
    if hasattr(P, "identf"):
        return
    sc = getattr(P, "scope", "")
    P.scope = ""
    P.identf = P.sb("std_identf", [128, 128], F32)
    P.memset(P.identf[:, :], 1.0)
    P.affsel(P.identf[:, :], P.identf[:, :], [[1, 128]], ALU.is_equal,
             0.0, base=0, channel_multiplier=-1)
    P.scope = sc



def emit_l3f(P, T=THALF, d_xin=None, d_xout=None):
    HID = 1408
    NE = 8
    d_xin = d_xin or P.dram_in("x_in", [D, T])
    d_xout = d_xout or P.dram_out("x_out", [D, T])
    d_gateT = P.dram_in("l3_gateT", [D, NE])
    d_ebias = P.dram_in("l3_ebias", [1, NE])
    d_upT = P.dram_in("l3_upcatT", [D, 9 * HID])
    d_dnT = P.dram_in("l3_dncatT", [9 * HID, D])
    d_wscr = P.dram_tmp("wscr", [128, 128])

    P.std_init()
    P.init_banks()
    identf_init(P)
    x = P.acq(8, (128, T), F32)
    emit_load_x(P, x, d_xin, T)
    xlnb = P.acq(8, (128, T), BF16)
    mu, rs = emit_ln(P, x, xlnb, T, keep_rows=True)
    P.barrier()
    gT = load_wt(P, d_gateT, "gT", 8, NE, dtype=F32, q="qsp")
    ebr = P.sb("ebr", [1, NE], F32)
    P.dma("qsp", ebr[:, :], d_ebias.ap())
    eb = P.sb("eb", [128, NE], F32)
    bank = P.bks[5]
    P.mm(bank[:, 0:NE], P.ones1[0:1, :], ebr[0:1, :])
    P.copy(eb[:, :], bank[:, 0:NE], eng="act")
    Brow = P.sb("Brow", [1, NE], F32)
    for k in range(8):
        P.mm(bank[0:1, 0:NE], P.onec[:, 0:1], gT[k][:, :], start=(k == 0), stop=(k == 7))
    P.copy(Brow[0:1, :], bank[0:1, 0:NE])
    wbuf = P.sb("wbuf", [128, 128], F32)
    sc = P.acq1((128, 512), F32)
    t1 = P.acq1((128, 512), F32)
    t2 = P.acq1((128, 512), F32)
    col = P.acq1((128, 1), F32)
    for ti in range(T // 128):
        tsl = slice(128 * ti, 128 * (ti + 1))
        b = P.bks[ti % 2]
        for k in range(8):
            P.mm(b[:, 0:NE], x[k][:, tsl], gT[k][:, :],
                 start=(k == 0), stop=(k == 7))
        bmb = P.bks[2 + (ti % 2)]
        P.mm(bmb[:, 0:NE], mu[0:1, tsl], Brow[0:1, :])
        P.mm(P.bks[4][:, 0:1], rs[0:1, tsl], P.ones1[0:1, 0:1])
        P.copy(col[:, 0:1], P.bks[4][:, 0:1])
        P.copy(sc[:, 0:NE], b[:, 0:NE], eng="act")
        P.tt(sc[:, 0:NE], sc[:, 0:NE], bmb[:, 0:NE], ALU.subtract)
        P.ts(sc[:, 0:NE], sc[:, 0:NE], col[:, 0:1], None, ALU.mult)
        P.act(sc[:, 0:NE], sc[:, 0:NE], AF.Sigmoid)
        P.tt(t1[:, 0:NE], sc[:, 0:NE], eb[:, 0:NE], ALU.add)
        P.red(col[:, 0:1], t1[:, 0:NE], ALU.max)
        P.ts(t2[:, 0:NE], t1[:, 0:NE], col[:, 0:1], None, ALU.is_ge)
        P.ts(t2[:, 8:8 + NE], t2[:, 0:NE], -1e30, None, ALU.mult)
        P.tt(t1[:, 0:NE], t1[:, 0:NE], t2[:, 8:8 + NE], ALU.add)
        P.red(col[:, 0:1], t1[:, 0:NE], ALU.max)
        P.ts(t1[:, 0:NE], t1[:, 0:NE], col[:, 0:1], None, ALU.is_ge)
        P.tt(t2[:, 0:NE], t2[:, 0:NE], t1[:, 0:NE], ALU.add)
        P.tt(t2[:, 0:NE], t2[:, 0:NE], sc[:, 0:NE], ALU.mult)
        P.tt(wbuf[:, 8 * ti:8 * ti + NE], t2[:, 0:NE], t2[:, 0:NE], ALU.mult)
    P.rel([sc, t1, t2, col])
    P.rel([mu, rs])
    P.rel(gT)
    wrowb = P.acq1((1, T), F32)
    bank = P.bks[4]
    P.transp(bank[:, 0:128], wbuf[:, :], P.identf[:, :])
    wtr = P.acq1((128, 512), F32)
    P.copy(wtr[:, 0:128], bank[:, 0:128])
    P.dma("qsp", d_wscr.ap(), wtr[:, 0:128])
    P.rel(wtr)
    P.barrier()
    v = d_wscr.ap().rearrange("(a e) p -> e a p", e=NE)
    wsqb = P.acq1((128, T), F32)
    hbuf = [P.acq(4, (128, T), BF16), P.acq(4, (128, T), BF16)]
    upb = [P.acq(8, (128, 512), BF16), P.acq(8, (128, 512), BF16)]
    dnb = [P.acq(4, (128, 1024), BF16), P.acq(4, (128, 1024), BF16)]
    scr = P.acq(2, (128, 512), F32)
    BLKS = [(0, 512), (512, 1024), (1024, 1408)]
    DGRP = [(0, 4), (4, 8), (8, 11)]

    def load_up(buf, e, blk):
        c0, c1 = BLKS[blk]
        for k in range(8):
            P.dma("qpool", buf[k][:, 0:c1 - c0],
                  d_upT.ap()[128 * k:128 * (k + 1), HID * e + c0:HID * e + c1])

    def load_dn(buf, e, gi):
        i0, i1 = DGRP[gi]
        for j in range(i1 - i0):
            gx = 11 * e + i0 + j
            P.dma("qpool", buf[j][:, :], d_dnT.ap()[128 * gx:128 * (gx + 1), :])

    up_all = [(e, blk) for e in range(9) for blk in range(3)]
    dn_all = [(e, gi) for e in range(9) for gi in range(3)]
    load_up(upb[0], *up_all[0])
    load_dn(dnb[0], *dn_all[0])
    up_i = dn_i = 0
    for e in range(9):
        if e < NE:
            P.dma("qsp", wrowb[0:1, :].rearrange("x (a p) -> x a p", a=T // 128),
                  v[e:e + 1, :, :])
            bcast_to_sbuf(P, wrowb[0:1, :], wsqb, T)
        for blk, (c0, c1) in enumerate(BLKS):
            w = c1 - c0
            nic = (w + 127) // 128
            hb = hbuf[up_i % 2]
            up_sl = upb[up_i % 2]
            if up_i + 1 < len(up_all):
                load_up(upb[(up_i + 1) % 2], *up_all[up_i + 1])
            up_i += 1
            for ic in range(nic):
                for k in range(8):
                    for g in range(T // 512):
                        b = P.bks[g % 4]
                        P.mm(b[:, 0:512], up_sl[k][:, 128 * ic:128 * (ic + 1)],
                             xlnb[k][:, 512 * g:512 * (g + 1)], start=(k == 0), stop=(k == 7))
                for g in range(T // 512):
                    sl = slice(512 * g, 512 * (g + 1))
                    b = P.bks[g % 4]
                    s = scr[g % 2]
                    P.ts(s[:, :], b[:, 0:512], 0.0, None, ALU.max)
                    P.act(s[:, :], s[:, :], AF.Square)
                    if e < NE:
                        P.tt(hb[ic][:, sl], s[:, :], wsqb[:, sl], ALU.mult)
                    else:
                        P.copy(hb[ic][:, sl], s[:, :])
            dn4 = dnb[dn_i % 2]
            if dn_i + 1 < len(dn_all):
                load_dn(dnb[(dn_i + 1) % 2], *dn_all[dn_i + 1])
            dn_i += 1
            for oc in range(8):
                for j in range(nic):
                    for g in range(T // 512):
                        b = P.bks[g % 4]
                        P.mm(b[:, 0:512], dn4[j][:, 128 * oc:128 * (oc + 1)],
                             hb[j][:, 512 * g:512 * (g + 1)],
                             start=(j == 0), stop=(j == nic - 1))
                for g in range(T // 512):
                    sl = slice(512 * g, 512 * (g + 1))
                    P.stt(x[oc][:, sl], P.bks[g % 4][:, 0:512], 0.5, x[oc][:, sl],
                          ALU.mult, ALU.add)
    assert up_i == len(up_all) and dn_i == len(dn_all)
    P.rel([wsqb, wrowb] + scr + upb[0] + upb[1] + dnb[0] + dnb[1] + hbuf[0] + hbuf[1])
    P.barrier()
    emit_store_x(P, x, d_xout, T)
    P.rel(x + xlnb)
    P.finish()


PAIRS = [[0, 1], [2, 3], [4, 5], [6, 7]]


def proj_fm(P, wT, xb, T, evac, ocs=range(8), bk0=0):
    ng = T // 512
    for i, oc in enumerate(ocs):
        for g0 in range(0, ng, 4):
            gs = range(g0, min(g0 + 4, ng))
            for k in range(8):
                for g in gs:
                    b = P.bks[bk0 + (g % 4)]
                    P.mm(b[:, 0:512], wT[k][:, 128 * oc:128 * (oc + 1)],
                         xb[k][:, 512 * g:512 * (g + 1)], start=(k == 0), stop=(k == 7))
            for g in gs:
                evac(oc, g, P.bks[bk0 + (g % 4)])


def proj_tm(P, wT, xb, T, NO, evac, bk0=0):
    ng = NO // 512
    for ti in range(T // 128):
        boff = bk0 + ((ti % 2) * ng if ng <= 2 else 0)
        for k in range(8):
            for og in range(ng):
                b = P.bks[boff + (og % 4)]
                P.mm(b[:, 0:512], xb[k][:, 128 * ti:128 * (ti + 1)],
                     wT[k][:, 512 * og:512 * (og + 1)], start=(k == 0), stop=(k == 7))
        for og in range(ng):
            evac(ti, og, P.bks[boff + (og % 4)])


def elu1_evac(P, dst_ap, src_ap, s1, s2, bias_col=None, bias_tile=None):
    if bias_col is not None:
        P.ts(s1[:, :], src_ap, bias_col, None, ALU.add)
    else:
        P.tt(s1[:, :], src_ap, bias_tile, ALU.add)
    P.ts(s2[:, :], s1[:, :], 0.0, None, ALU.min)
    P.act(s2[:, :], s2[:, :], AF.Exp)
    P.ts(s1[:, :], s1[:, :], 0.0, None, ALU.max)
    P.tt(dst_ap, s1[:, :], s2[:, :], ALU.add)



def emit_l0b(P, T=THALF, n_pair=2, d_xin=None, d_xout=None):
    d_xin = d_xin or P.dram_in("x_in", [D, T])
    d_xout = d_xout or P.dram_out("x_out", [D, T])
    wqT = P.dram_in("l0b_wqT", [D, D])
    wkT = P.dram_in("l0b_wkT", [D, D])
    wvT = P.dram_in("l0b_wvT", [D, D])
    woT = P.dram_in("l0b_woT", [D, D])
    d_bq = P.dram_in("l0b_bq", [128, 8])
    d_bo = P.dram_in("l0b_bo", [128, 8])
    d_bkv = P.dram_in("l0b_bkvrow", [2, D])
    b_kv1 = P.dram_tmp("b_kv1", [128, 17 * 64])
    b_kv2 = P.dram_tmp("b_kv2", [128, 17 * 64])

    P.std_init()
    P.init_banks()
    x = P.acq(8, (128, T), F32)
    emit_load_x(P, x, d_xin, T)
    xln = P.acq(8, (128, T), BF16)
    emit_ln(P, x, xln, T)
    P.rel(x)
    P.barrier()
    bqc = P.sb("bqc", [128, 8], F32)
    boc = P.sb("boc", [128, 8], F32)
    P.dma("qsp", bqc[:, :], d_bq.ap())
    P.dma("qsp", boc[:, :], d_bo.ap())
    bkr = P.acq1((1, D), F32)
    bvr = P.acq1((1, D), F32)
    P.dma("qsp", bkr[0:1, :], d_bkv.ap()[0:1, :])
    P.dma("qsp", bvr[0:1, :], d_bkv.ap()[1:2, :])
    bkt = P.acq1((128, 1024), F32)
    bvt = P.acq1((128, 1024), F32)
    bcast_to_sbuf(P, bkr[0:1, :], bkt, D)
    bcast_to_sbuf(P, bvr[0:1, :], bvt, D)
    s1 = P.acq1((128, 512), F32)
    s2 = P.acq1((128, 512), F32)
    wt = load_wt(P, wqT, "wq", 8, D)
    qp = P.acq(8, (128, T), BF16)
    proj_fm(P, wt, xln, T, lambda oc, g, b:
            elu1_evac(P, qp[oc][:, 512 * g:512 * (g + 1)], b[:, 0:512], s1, s2,
                      bias_col=bqc[:, oc:oc + 1]))
    P.barrier()
    for k in range(8):
        P.dma("qpool", wt[k][:, :], wkT.ap()[128 * k:128 * (k + 1), :])
    kp = P.acq(T // 128, (128, 1024), BF16)
    def kevac(ti, og, b):
        sl = slice(512 * og, 512 * (og + 1))
        elu1_evac(P, kp[ti][:, sl], b[:, 0:512], s1, s2, bias_tile=bkt[:, sl])
    proj_tm(P, wt, xln, T, D, kevac)
    P.barrier()
    for k in range(8):
        P.dma("qpool", wt[k][:, :], wvT.ap()[128 * k:128 * (k + 1), :])
    vt = P.acq(T // 128, (128, 1024), BF16)
    def vevac(ti, og, b):
        sl = slice(512 * og, 512 * (og + 1))
        P.tt(vt[ti][:, sl], b[:, 0:512], bvt[:, sl], ALU.add)
    proj_tm(P, wt, xln, T, D, vevac)
    P.rel(xln)
    P.barrier()
    kvs = P.acq1((128, 1088), F32)
    for hp in range(8):
        bkv = P.bks[hp % 4]
        bks_ = P.bks[4]
        for h2 in range(2):
            h = 2 * hp + h2
            hs = slice(64 * h, 64 * (h + 1))
            tp = (0, 64 * h2)
            for ti in range(T // 128):
                P.mm(bkv[64 * h2:64 * (h2 + 1), 0:64], kp[ti][:, hs], vt[ti][:, hs],
                     start=(ti == 0), stop=(ti == T // 128 - 1), tile_position=tp)
                P.mm(bks_[64 * h2:64 * (h2 + 1), hp:hp + 1], kp[ti][:, hs], P.ones128[:, 0:1],
                     start=(ti == 0), stop=(ti == T // 128 - 1), tile_position=tp)
            P.copy(kvs[64 * h2:64 * (h2 + 1), 136 * hp:136 * hp + 64], bkv[64 * h2:64 * (h2 + 1), 0:64])
            P.copy(kvs[64 * h2:64 * (h2 + 1), 136 * hp + 128 + h2:136 * hp + 129 + h2],
                   bks_[64 * h2:64 * (h2 + 1), hp:hp + 1])
    P.barrier()
    P.dma("qpool", b_kv1.ap(), kvs[:, :])
    P.cc("AllReduce", ALU.add, PAIRS[:n_pair // 2] if n_pair < 8 else PAIRS,
         [b_kv1.ap()], [b_kv2.ap()])
    P.dma("qpool", kvs[:, :], b_kv2.ap())
    kvb = P.acq1((128, 1088), BF16)
    P.copy(kvb[:, :], kvs[:, :], eng="act")
    P.barrier()
    at = P.acq(8, (128, T), BF16)
    zs = P.acq1((128, 512), F32)
    for h in range(16):
        hp, h2 = h // 2, h % 2
        ch, po = h // 2 % 4 * 0 + h // 2, 64 * (h % 2)
        ch = h // 2
        kvsl = kvb[64 * h2:64 * (h2 + 1), 136 * hp:136 * hp + 64]
        kssl = kvb[64 * h2:64 * (h2 + 1), 136 * hp + 128 + h2:136 * hp + 129 + h2]
        for g in range(T // 512):
            sl = slice(512 * g, 512 * (g + 1))
            qsl = qp[ch][po:po + 64, sl]
            bz = P.bks[4 + (g % 2)]
            P.mm(bz[0:1, 0:512], kssl, qsl)
            P.ts(zs[0:1, 0:512], bz[0:1, 0:512], 1e-6, None, ALU.add)
            P.op("dve", lambda v, o=zs[0:1, 0:512], i=zs[0:1, 0:512]: v.reciprocal(o, i),
                 reads=[zs], writes=[zs])
            bo_ = P.bks[g % 2]
            P.mm(bo_[po:po + 64, 0:512], kvsl, qsl, tile_position=(64 * h2, po))
            bb = P.bks[2 + (g % 2)]
            P.mm(bb[0:64, 0:512], P.ones1[0:1, 0:64], zs[0:1, 0:512])
            P.copy(at[ch][po:po + 64, sl], bo_[po:po + 64, 0:512], eng="act")
            P.tt(at[ch][po:po + 64, sl], at[ch][po:po + 64, sl], bb[0:64, 0:512], ALU.mult)
    P.rel([kvs, kvb, zs] + kp + vt + qp)
    P.barrier()
    x2 = P.acq(8, (128, T), F32)
    emit_load_x(P, x2, d_xin, T)
    for k in range(8):
        P.dma("qpool", wt[k][:, :], woT.ap()[128 * k:128 * (k + 1), :])
    def oevac(oc, g, b):
        sl = slice(512 * g, 512 * (g + 1))
        P.ts(s1[:, :], b[:, 0:512], boc[:, oc:oc + 1], None, ALU.add)
        P.tt(x2[oc][:, sl], x2[oc][:, sl], s1[:, :], ALU.add)
    proj_fm(P, wt, at, T, oevac)
    P.barrier()
    emit_store_x(P, x2, d_xout, T)
    P.rel(x2 + at + wt + [s1, s2, bkt, bvt, bkr, bvr])
    P.finish()



def emit_l2b(P, T=THALF, n_pair=2, d_xin=None, d_xout=None):
    H, J, Dh = 16, 64, 64
    d_xin = d_xin or P.dram_in("x_in", [D, T])
    d_xout = d_xout or P.dram_out("x_out", [D, T])
    wqT = P.dram_in("l2b_wqT", [D, D])
    wkT = P.dram_in("l2b_wkT", [D, D])
    wvT = P.dram_in("l2b_wvT", [D, D])
    woT = P.dram_in("l2b_woT", [D, D])
    bqc = P.dram_in("l2b_bq", [128, 8])
    bkc_d = P.dram_in("l2b_bk", [128, 8])
    boc_d = P.dram_in("l2b_bo", [128, 8])
    d_bv = P.dram_in("l2b_bvrow", [1, D])
    d_pjT = P.dram_in("l2b_projT", [8 * 128, J])
    b_st1 = P.dram_tmp("b_st1", [128, 16])
    b_st2 = P.dram_tmp("b_st2", [128, 16])
    b_str = P.dram_tmp("b_str", [16, 1])
    b_kv1 = P.dram_tmp("b_kv1", [128, 17 * 64])
    b_kv2 = P.dram_tmp("b_kv2", [128, 17 * 64])
    groups = PAIRS[:n_pair // 2] if n_pair < 8 else PAIRS
    nrm = float(Dh) ** (-0.25)
    ratio = float(J) ** (-0.5)
    EPSF = 1e-4

    P.std_init()
    P.init_banks()
    identf_init(P)
    x = P.acq(8, (128, T), F32)
    emit_load_x(P, x, d_xin, T)
    xln = P.acq(8, (128, T), BF16)
    emit_ln(P, x, xln, T)
    P.rel(x)
    P.barrier()
    bq = P.sb("bq", [128, 8], F32)
    bk = P.sb("bk", [128, 8], F32)
    bo = P.sb("bo", [128, 8], F32)
    P.dma("qsp", bq[:, :], bqc.ap())
    P.dma("qsp", bk[:, :], bkc_d.ap())
    P.dma("qsp", bo[:, :], boc_d.ap())
    bvr = P.acq1((1, D), F32)
    P.dma("qsp", bvr[0:1, :], d_bv.ap())
    bvt = P.acq1((128, 1024), F32)
    bcast_to_sbuf(P, bvr[0:1, :], bvt, D)
    s1 = P.acq1((128, 512), F32)
    s2 = P.acq1((128, 512), F32)
    pjT = load_wt(P, d_pjT, "pjT", 8, J)
    wt = load_wt(P, wqT, "wq", 8, D)
    qn = P.acq(8, (128, T), BF16)
    def nevac(dst, bias):
        def f(oc, g, b):
            P.ts(s1[:, :], b[:, 0:512], bias[:, oc:oc + 1], None, ALU.add)
            P.ts(dst[oc][:, 512 * g:512 * (g + 1)], s1[:, :], nrm, None, ALU.mult)
        return f
    proj_fm(P, wt, xln, T, nevac(qn, bq))
    P.barrier()
    for k in range(8):
        P.dma("qpool", wt[k][:, :], wkT.ap()[128 * k:128 * (k + 1), :])
    kn = P.acq(8, (128, T), BF16)
    proj_fm(P, wt, xln, T, nevac(kn, bk))
    P.barrier()
    for k in range(8):
        P.dma("qpool", wt[k][:, :], wvT.ap()[128 * k:128 * (k + 1), :])
    vt = P.acq(T // 128, (128, 1024), BF16)
    def vevac(ti, og, b):
        sl = slice(512 * og, 512 * (og + 1))
        P.tt(vt[ti][:, sl], b[:, 0:512], bvt[:, sl], ALU.add)
    proj_tm(P, wt, xln, T, D, vevac)
    P.rel(xln + [bvt, bvr])
    P.barrier()

    dra = P.acq1((1, T), F32)
    drb = P.acq1((1, T), F32)
    dca = P.acq1((128, 1), F32)
    def diag_rows(srcoc):
        for g in range(T // 512):
            sl = slice(512 * g, 512 * (g + 1))
            P.act(s1[:, :], srcoc[:, sl], AF.Square)
            ba = P.bks[4]
            P.mm(ba[0:1, 0:512], P.onec[0:64, 0:1], s1[0:64, :])
            P.ts(dra[0:1, sl], ba[0:1, 0:512], 0.5, None, ALU.mult)
            P.mm(ba[0:1, 0:512], P.onec[64:128, 0:1], s1[64:128, :], tile_position=(64, 0))
            P.ts(drb[0:1, sl], ba[0:1, 0:512], 0.5, None, ALU.mult)

    mxk = P.acq1((128, 16), F32)
    P.memset(mxk[:, :], -1e30)
    for h in range(H):
        oc, h2 = h // 2, h % 2
        pb = slice(64 * h2, 64 * (h2 + 1))
        for ti in range(T // 128):
            b = P.bks[ti % 4]
            P.mm(b[:, 0:64], kn[oc][pb, 128 * ti:128 * (ti + 1)], pjT[oc][pb, :],
                 tile_position=(64 * h2, 0))
            P.red(dca[:, 0:1], b[:, 0:64], ALU.max)
            P.tt(mxk[:, h:h + 1], mxk[:, h:h + 1], dca[:, 0:1], ALU.max)
    P.dma("qpool", b_st1.ap(), mxk[:, :])
    P.cc("AllReduce", ALU.max, groups, [b_st1.ap()], [b_st2.ap()])
    P.dma("qpool", mxk[:, :], b_st2.ap())
    bt = P.bks[4]
    P.transp(bt[0:16, 0:128], mxk[:, :], P.identf[:, :])
    P.copy(s1[0:16, 0:128], bt[0:16, 0:128])
    P.red(s2[0:16, 0:1], s1[0:16, 0:128], ALU.max)
    P.dma("qsp", b_str.ap(), s2[0:16, 0:1])
    strow = P.acq1((1, 16), F32)
    P.dma("qsp", strow[0:1, :], b_str.ap().rearrange("a b -> b a"))
    stabs = P.acq1((128, 16), F32)
    bt2 = P.bks[5]
    P.mm(bt2[:, 0:16], P.ones1[0:1, :], strow[0:1, :])
    P.copy(stabs[:, :], bt2[:, 0:16], eng="act")
    P.barrier()
    kp = P.acq(T // 128, (128, 1024), BF16)
    for h in range(H):
        oc, h2 = h // 2, h % 2
        if h2 == 0:
            diag_rows(kn[oc])
        drow = dra if h2 == 0 else drb
        pb = slice(64 * h2, 64 * (h2 + 1))
        for ti in range(T // 128):
            bd = P.bks[ti % 2]
            P.mm(bd[:, 0:64], kn[oc][pb, 128 * ti:128 * (ti + 1)], pjT[oc][pb, :],
                 tile_position=(64 * h2, 0))
            P.mm(P.bks[4][:, 0:1], drow[0:1, 128 * ti:128 * (ti + 1)], P.ones1[0:1, 0:1])
            P.copy(dca[:, 0:1], P.bks[4][:, 0:1])
            P.ts(s1[:, 0:64], bd[:, 0:64], dca[:, 0:1], stabs[:, h:h + 1],
                 ALU.subtract, ALU.subtract)
            P.act(s1[:, 0:64], s1[:, 0:64], AF.Exp)
            P.ts(kp[ti][:, 64 * h:64 * h + 64], s1[:, 0:64], ratio, EPSF, ALU.mult, ALU.add)
    P.rel(kn + [mxk, stabs, strow])
    P.barrier()
    qp = P.acq(8, (128, T), BF16)
    mq = P.acq1((128, 1), F32)
    for h in range(H):
        oc, h2 = h // 2, h % 2
        pb = slice(64 * h2, 64 * (h2 + 1))
        if h2 == 0:
            diag_rows(qn[oc])
        drow = dra if h2 == 0 else drb
        for ti in range(T // 128):
            b = P.bks[ti % 2]
            P.mm(b[:, 0:64], qn[oc][pb, 128 * ti:128 * (ti + 1)], pjT[oc][pb, :],
                 tile_position=(64 * h2, 0))
            P.mm(P.bks[4][:, 0:1], drow[0:1, 128 * ti:128 * (ti + 1)], P.ones1[0:1, 0:1])
            P.copy(dca[:, 0:1], P.bks[4][:, 0:1])
            P.red(mq[:, 0:1], b[:, 0:64], ALU.max)
            P.ts(s1[:, 0:64], b[:, 0:64], dca[:, 0:1], mq[:, 0:1], ALU.subtract, ALU.subtract)
            P.act(s1[:, 0:64], s1[:, 0:64], AF.Exp)
            qpt = P.acq1((128, 512), BF16)
            P.ts(qpt[:, 0:64], s1[:, 0:64], ratio, EPSF, ALU.mult, ALU.add)
            tb = P.tb[ti % 2]
            P.transp(tb[0:64, 0:128], qpt[:, 0:64], P.identb[:, :])
            P.copy(qp[oc][pb, 128 * ti:128 * (ti + 1)], tb[0:64, 0:128])
            P.rel(qpt)
    P.rel(qn + [dra, drb, dca, mq])
    P.barrier()
    kvs = P.acq1((128, 1088), F32)
    for hp in range(8):
        bkv = P.bks[hp % 4]
        bks_ = P.bks[4]
        for h2 in range(2):
            h = 2 * hp + h2
            hs = slice(64 * h, 64 * (h + 1))
            tp = (0, 64 * h2)
            for ti in range(T // 128):
                P.mm(bkv[64 * h2:64 * (h2 + 1), 0:64], kp[ti][:, hs], vt[ti][:, hs],
                     start=(ti == 0), stop=(ti == T // 128 - 1), tile_position=tp)
                P.mm(bks_[64 * h2:64 * (h2 + 1), hp:hp + 1], kp[ti][:, hs], P.ones128[:, 0:1],
                     start=(ti == 0), stop=(ti == T // 128 - 1), tile_position=tp)
            P.copy(kvs[64 * h2:64 * (h2 + 1), 136 * hp:136 * hp + 64], bkv[64 * h2:64 * (h2 + 1), 0:64])
            P.copy(kvs[64 * h2:64 * (h2 + 1), 136 * hp + 128 + h2:136 * hp + 129 + h2],
                   bks_[64 * h2:64 * (h2 + 1), hp:hp + 1])
    P.barrier()
    P.dma("qpool", b_kv1.ap(), kvs[:, :])
    P.cc("AllReduce", ALU.add, groups, [b_kv1.ap()], [b_kv2.ap()])
    P.dma("qpool", kvs[:, :], b_kv2.ap())
    kvb = P.acq1((128, 1088), BF16)
    P.copy(kvb[:, :], kvs[:, :], eng="act")
    P.barrier()
    at = P.acq(8, (128, T), BF16)
    zs = P.acq1((128, 512), F32)
    for h in range(16):
        hp, h2 = h // 2, h % 2
        ch, po = h // 2, 64 * (h % 2)
        kvsl = kvb[64 * h2:64 * (h2 + 1), 136 * hp:136 * hp + 64]
        kssl = kvb[64 * h2:64 * (h2 + 1), 136 * hp + 128 + h2:136 * hp + 129 + h2]
        for g in range(T // 512):
            sl = slice(512 * g, 512 * (g + 1))
            qsl = qp[ch][po:po + 64, sl]
            bz = P.bks[4 + (g % 2)]
            P.mm(bz[0:1, 0:512], kssl, qsl, tile_position=(64 * h2, 0))
            P.ts(zs[0:1, 0:512], bz[0:1, 0:512], 0.01, None, ALU.max)
            P.op("dve", lambda v, o=zs[0:1, 0:512], i=zs[0:1, 0:512]: v.reciprocal(o, i),
                 reads=[zs], writes=[zs])
            bo_ = P.bks[g % 2]
            P.mm(bo_[po:po + 64, 0:512], kvsl, qsl, tile_position=(64 * h2, po))
            bb = P.bks[2 + (g % 2)]
            P.mm(bb[0:64, 0:512], P.ones1[0:1, 0:64], zs[0:1, 0:512])
            P.copy(at[ch][po:po + 64, sl], bo_[po:po + 64, 0:512], eng="act")
            P.tt(at[ch][po:po + 64, sl], at[ch][po:po + 64, sl], bb[0:64, 0:512], ALU.mult)
    P.rel([kvs, kvb, zs] + kp + vt + qp)
    P.barrier()
    x2 = P.acq(8, (128, T), F32)
    emit_load_x(P, x2, d_xin, T)
    for k in range(8):
        P.dma("qpool", wt[k][:, :], woT.ap()[128 * k:128 * (k + 1), :])
    def oevac(oc, g, b):
        sl = slice(512 * g, 512 * (g + 1))
        P.ts(s1[:, :], b[:, 0:512], bo[:, oc:oc + 1], None, ALU.add)
        P.tt(x2[oc][:, sl], x2[oc][:, sl], s1[:, :], ALU.add)
    proj_fm(P, wt, at, T, oevac)
    P.barrier()
    emit_store_x(P, x2, d_xout, T)
    P.rel(x2 + at + wt + pjT + [s1, s2])
    P.finish()



def emit_l3b(P, T=THALF, n_pair=2, d_xin=None, d_xout=None):
    H, Dh, CH = 16, 64, 128
    NCH = T // CH
    d_xin = d_xin or P.dram_in("x_in", [D, T])
    d_xout = d_xout or P.dram_out("x_out", [D, T])
    dWq = P.dram_in("l3b_wq", [D, D])
    dWk = P.dram_in("l3b_wk", [D, D])
    dWv = P.dram_in("l3b_wv", [D, D])
    dWo = P.dram_in("l3b_wo", [D, D])
    dWg1 = P.dram_in("l3b_wg1", [D, 64])
    dWg2 = P.dram_in("l3b_wg2", [64, D])
    d_gam = P.dram_in("l3b_gamma", [128, 8])
    d_bet = P.dram_in("l3b_beta", [128, 8])
    d_maskT = P.dram_in("l3b_maskT", [128, 16 * 128])
    d_lamqb = P.dram_in("l3b_lamqb", [128, 8 * 128])
    d_lamk = P.dram_in("l3b_lamk", [128, 16])
    d_clam = P.dram_in("l3b_clam", [16, T])
    d_lamc = P.dram_in("l3b_lamc", [1, 16])
    b_kv1 = P.dram_tmp("b_kv1", [128, 512])
    b_kv2 = P.dram_tmp("b_kv2", [128, 512])
    groups = PAIRS[:n_pair // 2] if n_pair < 8 else PAIRS

    P.std_init()
    P.init_banks()
    identf_init(P)
    x = P.acq(8, (128, T), F32)
    emit_load_x(P, x, d_xin, T)
    xln = P.acq(8, (128, T), BF16)
    emit_ln(P, x, xln, T)
    P.rel(x)
    P.barrier()
    wt = load_wt(P, dWq, "wt", 8, D)
    s1 = P.acq1((128, 512), F32)
    qf = P.acq(8, (128, T), BF16)
    def silu_evac(dst):
        def f(oc, g, b):
            P.act(dst[oc][:, 512 * g:512 * (g + 1)], b[:, 0:512], AF.Silu)
        return f
    proj_fm(P, wt, xln, T, silu_evac(qf))
    P.barrier()
    for k in range(8):
        P.dma("qpool", wt[k][:, :], dWk.ap()[128 * k:128 * (k + 1), :])
    kf = P.acq(8, (128, T), BF16)
    proj_fm(P, wt, xln, T, silu_evac(kf))
    P.barrier()
    for k in range(8):
        P.dma("qpool", wt[k][:, :], dWv.ap()[128 * k:128 * (k + 1), :])
    vt = P.acq(NCH, (128, 1024), BF16)
    def vevac(ti, og, b):
        P.act(vt[ti][:, 512 * og:512 * (og + 1)], b[:, 0:512], AF.Silu)
    proj_tm(P, wt, xln, T, D, vevac)
    P.barrier()
    g1 = P.acq1((128, T), BF16)
    wg1 = P.acq(8, (128, 512), BF16)
    for k in range(8):
        P.dma("qpool", wg1[k][:, 0:64], dWg1.ap()[128 * k:128 * (k + 1), :])
    for g in range(T // 512):
        sl = slice(512 * g, 512 * (g + 1))
        b = P.bks[g % 2]
        for k in range(8):
            P.mm(b[0:64, 0:512], wg1[k][:, 0:64], xln[k][:, sl], start=(k == 0), stop=(k == 7))
        P.copy(g1[0:64, sl], b[0:64, 0:512])
    P.rel(wg1 + xln)
    P.barrier()
    at = P.acq(8, (128, T), BF16)
    lamk = P.sb("lamk", [128, 16], F32)
    P.dma("qsp", lamk[:, :], d_lamk.ap())
    lamqb = P.acq1((128, 1024), BF16)
    P.dma("qpool", lamqb[:, :], d_lamqb.ap())
    maskT = P.acq1((128, 2048), BF16)
    P.dma("qpool", maskT[:, :], d_maskT.ap())
    kvst = P.acq1((128, 512), F32)
    kvb = P.acq1((128, 512), BF16)
    P.memset(kvst[:, :], 0.0)
    P.memset(kvb[:, :], 0.0)
    lamc_r = P.sb("lamc_r", [1, 16], F32)
    P.dma("qsp", lamc_r[0:1, :], d_lamc.ap())
    lamc_c = P.sb("lamc_c", [128, 16], F32)
    blc = P.bks[5]
    P.mm(blc[:, 0:16], P.ones1[0:1, :], lamc_r[0:1, :])
    P.copy(lamc_c[:, :], blc[:, 0:16], eng="act")
    smt = P.acq1((128, 512), BF16)
    ktl = P.acq1((128, 512), BF16)
    qlam = P.acq1((128, 512), BF16)
    for h in range(H):
        hp, h2 = h // 2, h % 2
        ch, po = h // 2, 64 * (h % 2)
        pb = slice(po, po + 64)
        kvsl = kvst[64 * h2:64 * (h2 + 1), 64 * hp:64 * hp + 64]
        kvbsl = kvb[64 * h2:64 * (h2 + 1), 64 * hp:64 * hp + 64]
        for c in range(NCH):
            csl = slice(128 * c, 128 * (c + 1))
            bS = P.bks[0]
            P.mm(bS[:, 0:128], kf[ch][pb, csl], qf[ch][pb, csl], tile_position=(po, 0))
            P.tt(smt[:, 0:128], bS[:, 0:128], maskT[:, 128 * h:128 * (h + 1)], ALU.mult)
            bo_ = P.bks[1]
            P.mm(bo_[0:64, 0:128], vt[c][:, 64 * h:64 * h + 64], smt[:, 0:128],
                 start=True, stop=False)
            P.tt(qlam[pb, 0:128], qf[ch][pb, csl], lamqb[pb, 128 * hp:128 * (hp + 1)], ALU.mult)
            P.mm(bo_[0:64, 0:128], kvbsl, qlam[pb, 0:128],
                 start=False, stop=True, tile_position=(64 * h2, 0))
            P.copy(at[ch][pb, csl], bo_[0:64, 0:128], eng="act")
            tb = P.tb[c % 2]
            P.transp(tb[0:128, 0:64], kf[ch][pb, csl], P.identb[pb, pb])
            P.ts(ktl[:, 0:64], tb[0:128, 0:64], lamk[:, h:h + 1], None, ALU.mult)
            bK = P.bks[2 + (c % 2)]
            P.mm(bK[64 * h2:64 * (h2 + 1), 0:64], ktl[:, 0:64], vt[c][:, 64 * h:64 * h + 64],
                 tile_position=(0, 64 * h2))
            P.stt(kvsl, kvsl, lamc_c[64 * h2:64 * (h2 + 1), h:h + 1],
                  bK[64 * h2:64 * (h2 + 1), 0:64], ALU.mult, ALU.add)
            P.copy(kvbsl, kvsl, eng="act")
    P.barrier()
    P.dma("qpool", b_kv1.ap(), kvst[:, :])
    P.cc("AllReduce", ALU.add, groups, [b_kv1.ap()], [b_kv2.ap()])
    tot = P.acq1((128, 512), F32)
    P.dma("qpool", tot[:, :], b_kv2.ap())
    P.tt(tot[:, :], tot[:, :], kvst[:, :], ALU.subtract)
    kin = P.acq1((128, 512), BF16)
    P.copy(kin[:, :], tot[:, :], eng="act")
    P.rel([tot, kvst, kvb])
    clrow = P.acq1((1, T), F32)
    for h in range(H):
        hp, h2 = h // 2, h % 2
        ch, po = h // 2, 64 * (h % 2)
        pb = slice(po, po + 64)
        kinsl = kin[64 * h2:64 * (h2 + 1), 64 * hp:64 * hp + 64]
        P.dma("qsp", clrow[0:1, :], d_clam.ap()[h:h + 1, :])
        for c in range(NCH):
            csl = slice(128 * c, 128 * (c + 1))
            bb = P.bks[4 + (c % 2)]
            P.mm(bb[0:64, 0:128], P.ones1[0:1, 0:64], clrow[0:1, csl])
            P.tt(qlam[pb, 0:128], qf[ch][pb, csl], bb[0:64, 0:128], ALU.mult)
            bo_ = P.bks[c % 2]
            P.mm(bo_[0:64, 0:128], kinsl, qlam[pb, 0:128], tile_position=(64 * h2, 0))
            P.ts(smt[:, 0:128], bo_[0:64, 0:128], 1.0, None, ALU.mult) if False else None
            P.tt(at[ch][pb, csl], at[ch][pb, csl], bo_[0:64, 0:128], ALU.add)
    P.rel([kin, clrow, smt, ktl, qlam, lamqb, maskT] + qf + kf + vt)
    P.barrier()
    gcol = P.sb("gcol", [128, 8], F32)
    bcol = P.sb("bcol", [128, 8], F32)
    P.dma("qsp", gcol[:, :], d_gam.ap())
    P.dma("qsp", bcol[:, :], d_bet.ap())
    atf = P.acq(8, (128, T), F32)
    for ccc in range(8):
        P.copy(atf[ccc][:, :], at[ccc][:, :])
    P.rel(at)
    onrm = P.acq(8, (128, T), BF16)
    emit_ln(P, atf, onrm, T, pre="oln", gcol=gcol, bcol=bcol)
    P.rel(atf)
    wg2 = P.acq1((128, 1024), BF16)
    P.dma("qpool", wg2[0:64, :], dWg2.ap())
    gt = P.acq(8, (128, T), BF16)
    for oc in range(8):
        for g in range(T // 512):
            sl = slice(512 * g, 512 * (g + 1))
            b = P.bks[(oc + g) % 4]
            P.mm(b[:, 0:512], wg2[0:64, 128 * oc:128 * (oc + 1)], g1[0:64, sl])
            P.act(gt[oc][:, sl], b[:, 0:512], AF.Sigmoid)
    P.rel([g1, wg2])
    for oc in range(8):
        P.tt(onrm[oc][:, :], onrm[oc][:, :], gt[oc][:, :], ALU.mult)
    P.rel(gt)
    P.barrier()
    x2 = P.acq(8, (128, T), F32)
    emit_load_x(P, x2, d_xin, T)
    for k in range(8):
        P.dma("qpool", wt[k][:, :], dWo.ap()[128 * k:128 * (k + 1), :])
    def oevac(oc, g, b):
        sl = slice(512 * g, 512 * (g + 1))
        P.tt(x2[oc][:, sl], x2[oc][:, sl], b[:, 0:512], ALU.add)
    proj_fm(P, wt, onrm, T, oevac)
    P.barrier()
    emit_store_x(P, x2, d_xout, T)
    P.rel(x2 + onrm + wt + [s1])
    P.finish()



def emit_l1b(P, T=THALF, n_pair=2, d_xin=None, d_xout=None):
    d_xin = d_xin or P.dram_in("x_in", [D, T])
    d_xout = d_xout or P.dram_out("x_out", [D, T])
    dWomT = P.dram_in("l1b_womT", [D, D])
    d_bom = P.dram_in("l1b_bom", [128, 8])
    d_lsc = P.dram_in("l1b_lsc", [128, 8])
    d_isq = P.dram_in("l1b_isq", [1, T])
    d_ipos = P.dram_in("l1b_ipos", [1, T])
    d_gam = P.dram_in("l1b_gamcol", [128, 1])
    d_lng = P.dram_in("l1b_lngam", [128, 32])
    d_lnb = P.dram_in("l1b_lnbet", [128, 32])
    dW1T = P.dram_in("l1b_w1T", [4 * D, 2 * D])
    d_b1 = P.dram_in("l1b_b1", [128, 16])
    dW2T = P.dram_in("l1b_w2T", [2 * D, D])
    d_b2 = P.dram_in("l1b_b2", [128, 8])
    d_ctx = P.dram_tmp("d_ctx", [4 * D, T], BF16)
    d_slc = P.dram_tmp("d_slc", [2 * D, T], BF16)
    b_c1 = P.dram_tmp("b_c1", [128, 8])
    b_c2 = P.dram_tmp("b_c2", [128, 8])
    b_m1 = P.dram_tmp("b_m1", [128, 16])
    b_m2 = P.dram_tmp("b_m2", [128, 16])
    groups = PAIRS[:n_pair // 2] if n_pair < 8 else PAIRS
    TPI = 2.0 * math.pi
    MAGIC = 12582912.0

    P.std_init()
    P.init_banks()
    d_murs = P.dram_tmp("d_murs", [2, T])
    x = P.acq(8, (128, T), F32)
    emit_load_x(P, x, d_xin, T)
    xln = P.acq(8, (128, T), BF16)
    mu, rs = emit_ln(P, x, xln, T, keep_rows=True)
    P.dma("qsp", d_murs.ap()[0:1, :], mu[0:1, :])
    P.dma("qsp", d_murs.ap()[1:2, :], rs[0:1, :])
    P.rel([mu, rs] + x)
    P.barrier()
    bomc = P.sb("bomc", [128, 8], F32)
    lscc = P.sb("lscc", [128, 8], F32)
    gamc = P.sb("gamc", [128, 1], F32)
    P.dma("qsp", bomc[:, :], d_bom.ap())
    P.dma("qsp", lscc[:, :], d_lsc.ap())
    P.dma("qsp", gamc[:, :], d_gam.ap())
    els = P.sb("els", [128, 8], F32)
    P.act(els[:, :], lscc[:, :], AF.Exp)
    isqr = P.acq1((1, T), F32)
    P.dma("qsp", isqr[0:1, :], d_isq.ap())
    isqb = P.acq1((128, T), F32)
    bcast_to_sbuf(P, isqr[0:1, :], isqb, T)
    P.rel(isqr)
    s1 = P.acq1((128, 512), F32)
    wt = load_wt(P, dWomT, "wt", 8, D)
    om = P.acq(8, (128, T), F32)
    def omevac(oc, g, b):
        sl = slice(512 * g, 512 * (g + 1))
        P.ts(s1[:, :], b[:, 0:512], bomc[:, oc:oc + 1], els[:, oc:oc + 1], ALU.add, ALU.mult)
        P.tt(om[oc][:, sl], s1[:, :], isqb[:, sl], ALU.mult)
    proj_fm(P, wt, xln, T, omevac)
    P.rel(wt)
    onest = P.acq1((128, T), F32)
    P.memset(onest[:, :], 1.0)
    for oc in range(8):
        P.scan(om[oc][:, :], onest[:, :], om[oc][:, :], 0.0, ALU.mult, ALU.add)
    P.rel(onest)
    cc8 = P.acq1((128, 8), F32)
    for oc in range(8):
        P.copy(cc8[:, oc:oc + 1], om[oc][:, T - 1:T])
    P.dma("qpool", b_c1.ap(), cc8[:, :])
    P.cc("AllReduce", ALU.add, groups, [b_c1.ap()], [b_c2.ap()])
    tot8 = P.acq1((128, 8), F32)
    P.dma("qpool", tot8[:, :], b_c2.ap())
    P.tt(tot8[:, :], tot8[:, :], cc8[:, :], ALU.subtract)
    P.ts(tot8[:, :], tot8[:, :], gamc[:, 0:1], None, ALU.mult)
    for oc in range(8):
        P.ts(om[oc][:, :], om[oc][:, :], tot8[:, oc:oc + 1], None, ALU.add)
    P.barrier()
    cph = P.acq(8, (128, T), BF16)
    sph = P.acq(8, (128, T), BF16)
    rr = P.acq1((128, T), F32)
    cm = P.acq1((128, T), F32)
    mc16 = P.acq1((128, 16), F32)
    for oc in range(8):
        p = om[oc]
        P.ts(rr[:, :], p[:, :], 1.0 / TPI, MAGIC, ALU.mult, ALU.add)
        P.ts(rr[:, :], rr[:, :], MAGIC, None, ALU.subtract)
        P.stt(rr[:, :], rr[:, :], -TPI, p[:, :], ALU.mult, ALU.add)
        P.ts(rr[:, :], rr[:, :], 3.1415925, -3.1415925, ALU.min, ALU.max)
        P.act(sph[oc][:, :], rr[:, :], AF.Sin)
        P.ts(rr[:, :], p[:, :], 1.0 / TPI, 0.25, ALU.mult, ALU.add)
        P.ts(rr[:, :], rr[:, :], MAGIC, None, ALU.add)
        P.ts(rr[:, :], rr[:, :], MAGIC, None, ALU.subtract)
        P.ts(rr[:, :], rr[:, :], -0.25, None, ALU.add)
        P.stt(rr[:, :], rr[:, :], -TPI, p[:, :], ALU.mult, ALU.add)
        P.ts(rr[:, :], rr[:, :], 3.1415925, -3.1415925, ALU.min, ALU.max)
        P.act(cph[oc][:, :], rr[:, :], AF.Sin)
        for ri, trig in [(0, cph), (1, sph)]:
            P.tt(cm[:, :], xln[oc][:, :], trig[oc][:, :], ALU.mult)
            P.dma("qpool", d_ctx.ap()[128 * (8 * ri + oc):128 * (8 * ri + oc) + 128, :], cm[:, :])
            P.ts(p[:, :], p[:, :], 0.0, 1.0, ALU.mult, ALU.add)
            P.scan(cm[:, :], p[:, :], cm[:, :], 0.0, ALU.mult, ALU.add)
            P.copy(mc16[:, 8 * ri + oc:8 * ri + oc + 1], cm[:, T - 1:T])
            P.dma("qpool", d_slc.ap()[128 * (8 * ri + oc):128 * (8 * ri + oc) + 128, :], cm[:, :])
        P.rel([p, xln[oc]])
        om[oc] = None
        xln[oc] = None
    P.rel([rr, isqb])
    P.barrier()
    P.dma("qpool", b_m1.ap(), mc16[:, :])
    P.cc("AllReduce", ALU.add, groups, [b_m1.ap()], [b_m2.ap()])
    mtot = P.acq1((128, 16), F32)
    P.dma("qpool", mtot[:, :], b_m2.ap())
    P.tt(mtot[:, :], mtot[:, :], mc16[:, :], ALU.subtract)
    P.ts(mtot[:, :], mtot[:, :], gamc[:, 0:1], None, ALU.mult)
    iposr = P.acq1((1, T), F32)
    P.dma("qsp", iposr[0:1, :], d_ipos.ap())
    iposb = P.acq1((128, T), F32)
    bcast_to_sbuf(P, iposr[0:1, :], iposb, T)
    P.rel(iposr)
    mr = P.acq1((128, T), F32)
    mi = P.acq1((128, T), F32)
    rt = P.acq1((128, T), BF16)
    for oc in range(8):
        P.dma("qsp", mr.bitcast(BF16)[:, 0:T], d_slc.ap()[128 * oc:128 * oc + 128, :]) if False else None
        mrb = P.acq1((128, T), BF16)
        P.dma("qsp", mrb[:, :], d_slc.ap()[128 * oc:128 * oc + 128, :])
        P.ts(mr[:, :], mrb[:, :], mtot[:, oc:oc + 1], None, ALU.add)
        P.tt(mr[:, :], mr[:, :], iposb[:, :], ALU.mult)
        P.dma("qsp", mrb[:, :], d_slc.ap()[128 * (8 + oc):128 * (8 + oc) + 128, :])
        P.ts(mi[:, :], mrb[:, :], mtot[:, 8 + oc:8 + oc + 1], None, ALU.add)
        P.tt(mi[:, :], mi[:, :], iposb[:, :], ALU.mult)
        P.rel(mrb)
        P.tt(cm[:, :], mr[:, :], cph[oc][:, :], ALU.mult)
        P.tt(rt[:, :], mi[:, :], sph[oc][:, :], ALU.mult)
        P.tt(rt[:, :], cm[:, :], rt[:, :], ALU.add)
        P.dma("qsp", d_ctx.ap()[128 * (16 + oc):128 * (16 + oc) + 128, :], rt[:, :])
        P.tt(cm[:, :], mi[:, :], cph[oc][:, :], ALU.mult)
        P.tt(rt[:, :], mr[:, :], sph[oc][:, :], ALU.mult)
        P.tt(rt[:, :], cm[:, :], rt[:, :], ALU.subtract)
        P.dma("qsp", d_ctx.ap()[128 * (24 + oc):128 * (24 + oc) + 128, :], rt[:, :])
    P.rel([cm, mr, mi, rt, mc16, mtot, iposb] + cph + sph)
    P.barrier()
    cbuf = P.acq(2, (128, T), BF16)
    sq2 = P.acq1((128, 512), F32)
    cmu = P.acq1((1, T), F32)
    crs = P.acq1((1, T), F32)
    b_s, b_q = P.bks[0], P.bks[1]
    for g in range(T // 512):
        sl = slice(512 * g, 512 * (g + 1))
        for k in range(32):
            cb = cbuf[k % 2]
            if g == 0:
                pass
            P.dma("qpool", cb[:, sl], d_ctx.ap()[128 * k:128 * (k + 1), sl])
            P.mm(b_s[0:1, 0:512], P.ones128[:, 0:1], cb[:, sl], start=(k == 0), stop=(k == 31))
            P.act(sq2[:, :], cb[:, sl], AF.Square)
            P.mm(b_q[0:1, 0:512], P.onec[:, 0:1], sq2[:, :], start=(k == 0), stop=(k == 31))
        P.ts(cmu[0:1, sl], b_s[0:1, 0:512], 1.0 / 4096.0, None, ALU.mult)
        P.ts(crs[0:1, sl], b_q[0:1, 0:512], 1.0 / 4096.0, None, ALU.mult)
        P.tt(sq2[0:1, 0:512], cmu[0:1, sl], cmu[0:1, sl], ALU.mult)
        P.tt(crs[0:1, sl], crs[0:1, sl], sq2[0:1, 0:512], ALU.subtract)
        P.ts(crs[0:1, sl], crs[0:1, sl], 1e-5, None, ALU.add)
        P.act(crs[0:1, sl], crs[0:1, sl], AF.Sqrt)
        P.op("dve", lambda v, o=crs[0:1, sl], i=crs[0:1, sl]: v.reciprocal(o, i),
             reads=[crs], writes=[crs])
    P.rel([sq2])
    P.barrier()
    lngc = P.sb("lngc", [128, 32], F32)
    lnbc = P.sb("lnbc", [128, 32], F32)
    P.dma("qsp", lngc[:, :], d_lng.ap())
    P.dma("qsp", lnbc[:, :], d_lnb.ap())
    h1 = P.acq(16, (128, T), BF16)
    cbb = [P.acq(4, (128, T), BF16), P.acq(4, (128, T), BF16)]
    w1bb = [P.acq(4, (128, T), BF16), P.acq(4, (128, T), BF16)]
    mub = P.acq1((128, T), F32)
    rsb = P.acq1((128, T), F32)
    bcast_to_sbuf(P, cmu[0:1, :], mub, T)
    bcast_to_sbuf(P, crs[0:1, :], rsb, T)
    P.rel([cmu, crs])
    sA = P.acq1((128, T), F32)

    def load_dma(bi, kg):
        for kk in range(4):
            k = 4 * kg + kk
            P.dma("qpool", cbb[bi][kk][:, :], d_ctx.ap()[128 * k:128 * (k + 1), :])
            P.dma("qpool", w1bb[bi][kk][:, :], dW1T.ap()[128 * k:128 * (k + 1), :])

    def ln_apply(bi, kg):
        for kk in range(4):
            k = 4 * kg + kk
            cb = cbb[bi][kk]
            P.tt(sA[:, :], cb[:, :], mub[:, :], ALU.subtract)
            P.tt(sA[:, :], sA[:, :], rsb[:, :], ALU.mult)
            P.ts(cb[:, :], sA[:, :], lngc[:, k:k + 1], lnbc[:, k:k + 1], ALU.mult, ALU.add)

    load_dma(0, 0)
    ln_apply(0, 0)
    for kg in range(8):
        if kg + 1 < 8:
            load_dma((kg + 1) % 2, kg + 1)
        cb4, w14 = cbb[kg % 2], w1bb[kg % 2]
        for oc in range(16):
            for kk in range(4):
                for g in range(T // 512):
                    b = P.bks[g % 4]
                    P.mm(b[:, 0:512], w14[kk][:, 128 * oc:128 * (oc + 1)],
                         cb4[kk][:, 512 * g:512 * (g + 1)], start=(kk == 0), stop=(kk == 3))
            if oc == 7 and kg + 1 < 8:
                ln_apply((kg + 1) % 2, kg + 1)
            for g in range(T // 512):
                sl = slice(512 * g, 512 * (g + 1))
                b = P.bks[g % 4]
                if kg == 0:
                    P.copy(h1[oc][:, sl], b[:, 0:512])
                else:
                    P.tt(h1[oc][:, sl], h1[oc][:, sl], b[:, 0:512], ALU.add)
    P.rel(cbb[0] + cbb[1] + w1bb[0] + w1bb[1] + [cbuf[0], cbuf[1], mub, rsb, sA, cc8, tot8])
    P.barrier()
    b1c = P.sb("b1c", [128, 16], F32)
    P.dma("qsp", b1c[:, :], d_b1.ap())
    for oc in range(16):
        P.act(h1[oc][:, :], h1[oc][:, :], AF.Gelu, bias=b1c[:, oc:oc + 1], scale=1.0)
    P.barrier()
    b2c = P.sb("b2c", [128, 8], F32)
    P.dma("qsp", b2c[:, :], d_b2.ap())
    x2 = P.acq(2, (128, T), F32)
    w2c = P.acq(16, (128, 1024), BF16)
    s2 = P.acq1((128, 512), F32)
    s3 = P.acq1((128, 512), F32)
    for k in range(16):
        P.dma("qpool", w2c[k][:, :], dW2T.ap()[128 * k:128 * (k + 1), :])
    mu = P.acq1((1, T), F32)
    rs = P.acq1((1, T), F32)
    P.dma("qsp", mu[0:1, :], d_murs.ap()[0:1, :])
    P.dma("qsp", rs[0:1, :], d_murs.ap()[1:2, :])
    bmu, brs = P.bks[4], P.bks[5]
    for oc in range(8):
        xo = x2[oc % 2]
        P.dma("qsp", xo[:, :], d_xin.ap()[128 * oc:128 * (oc + 1), :])
        for k in range(16):
            for g in range(T // 512):
                b = P.bks[g % 4]
                P.mm(b[:, 0:512], w2c[k][:, 128 * oc:128 * (oc + 1)],
                     h1[k][:, 512 * g:512 * (g + 1)], start=(k == 0), stop=(k == 15))
        for g in range(T // 512):
            sl = slice(512 * g, 512 * (g + 1))
            b = P.bks[g % 4]
            P.mm(bmu[:, 0:512], P.ones1[0:1, :], mu[0:1, sl])
            P.mm(brs[:, 0:512], P.ones1[0:1, :], rs[0:1, sl])
            P.tt(s2[:, :], xo[:, sl], bmu[:, 0:512], ALU.subtract)
            P.tt(s3[:, :], s2[:, :], brs[:, 0:512], ALU.mult)
            P.tt(xo[:, sl], xo[:, sl], s3[:, :], ALU.add)
            P.ts(s2[:, :], b[:, 0:512], b2c[:, oc:oc + 1], None, ALU.add)
            P.tt(xo[:, sl], xo[:, sl], s2[:, :], ALU.add)
        P.dma("qsp", d_xout.ap()[128 * oc:128 * (oc + 1), :], xo[:, :])
    P.rel(x2 + h1 + w2c + [s2, s3, mu, rs, s1])
    P.finish()


def rowmul_scalar(P, outrow, inrow, scal_ap, T):
    for g in range((T + 511) // 512):
        sl = slice(512 * g, min(512 * (g + 1), T))
        n = sl.stop - sl.start
        b = P.bks[5]
        P.mm(b[0:1, 0:n], scal_ap, inrow[0:1, sl])
        P.copy(outrow[0:1, sl], b[0:1, 0:n])


MAGICF = 12582912.0


def pmax_to_row(P, acc, T, d_b, rowout, s1):
    strip = P.acq1((128, 16), F32)
    for b in range(T // 128):
        bk = P.bks[4 + (b % 2)]
        P.transp(bk[:, 0:128], acc[:, 128 * b:128 * (b + 1)], P.identf[:, :])
        P.red(strip[:, b:b + 1], bk[:, 0:128], ALU.max)
    bk = P.bks[4]
    P.transp(bk[0:16, 0:128], strip[:, 0:16], P.identf[:, :])
    P.copy(s1[0:16, 0:128], bk[0:16, 0:128])
    P.dma("qsp", d_b.ap(), s1[0:16, 0:128])
    P.dma("qsp", rowout[0:1, :].rearrange("x (a p) -> x a p", a=T // 128),
          d_b.ap()[0:T // 128, :].rearrange("a p -> a p").unsqueeze(0))
    P.rel(strip)



def emit_l2f(P, T=THALF, dbg=False, d_xin=None, d_xout=None, quant=True):
    INNER = 4096
    d_xin = d_xin or P.dram_in("x_in", [D, T])
    d_xout = d_xout or P.dram_out("x_out", [D, T])
    dWg = P.dram_in("l2f_wgT", [D, INNER])
    dWu = P.dram_in("l2f_wuT", [D, INNER])
    dWd = P.dram_in("l2f_wdT", [INNER, D])
    sc_save = getattr(P, "scope", "")
    P.push_scope("l2fq_")
    d_wqg = P.dram_tmp("d_wqg", [D, INNER], BF16)
    d_wqu = P.dram_tmp("d_wqu", [D, INNER], BF16)
    d_wqd = P.dram_tmp("d_wqd", [INNER, D], BF16)
    d_rw = P.dram_tmp("d_rw", [1, 4])
    P.push_scope(sc_save)
    d_p = P.dram_tmp("d_p", [INNER, T])
    d_b16 = P.dram_tmp("d_b16", [16, 128])
    P.std_init()
    P.init_banks()
    identf_init(P)
    x = P.acq(8, (128, T), F32)
    emit_load_x(P, x, d_xin, T)
    xln = P.acq(8, (128, T), F32)
    emit_ln(P, x, xln, T)
    P.rel(x)
    P.barrier()
    s1 = P.acq1((128, 512), F32)
    acc = P.acq1((128, T), F32)
    P.act(acc[:, :], xln[0][:, :], AF.Abs)
    absx = P.acq1((128, T), F32)
    for c in range(1, 8):
        P.act(absx[:, :], xln[c][:, :], AF.Abs)
        P.tt(acc[:, :], acc[:, :], absx[:, :], ALU.max)
    mrow = P.acq1((1, T), F32)
    pmax_to_row(P, acc, T, d_b16, mrow, s1)
    P.ts(mrow[0:1, :], mrow[0:1, :], 1e-5, None, ALU.max)
    rx = P.acq1((1, T), F32)
    P.ts(rx[0:1, :], mrow[0:1, :], 1.0 / 127.0, None, ALU.mult)
    sxb = P.acq1((128, T), F32)
    P.op("dve", lambda v, o=mrow[0:1, :], i=mrow[0:1, :]: v.reciprocal(o, i),
         reads=[mrow], writes=[mrow])
    P.ts(mrow[0:1, :], mrow[0:1, :], 127.0, None, ALU.mult)
    bcast_to_sbuf(P, mrow[0:1, :], sxb, T)
    xq = P.acq(8, (128, T), BF16)
    for c in range(8):
        P.tt(acc[:, :], xln[c][:, :], sxb[:, :], ALU.mult)
        P.ts(acc[:, :], acc[:, :], MAGICF, None, ALU.add)
        P.ts(acc[:, :], acc[:, :], MAGICF, 127.0, ALU.subtract, ALU.min)
        P.ts(xq[c][:, :], acc[:, :], -128.0, None, ALU.max)
    if dbg:
        d_dxq = P.dram_out("dbg_xq", [128, T])
        xqf = P.acq1((128, T), F32)
        P.copy(xqf[:, :], xq[0][:, :])
        P.dma("qsp", d_dxq.ap(), xqf[:, :])
        P.rel(xqf)
        d_dmr = P.dram_out("dbg_rw", [1, 4])
    P.rel(xln + [sxb])
    P.barrier()
    rwrow = P.acq1((1, 4), F32)
    WSPEC = [(dWg, d_wqg, 8, INNER), (dWu, d_wqu, 8, INNER), (dWd, d_wqd, 32, D)]
    if quant:
        wbuf = P.acq(2, (128, INNER), BF16)
        for wi, (dsrc, ddst, nr, nc_) in enumerate(WSPEC):
            bsum = P.bks[2]
            nmm = 0
            for k in range(nr):
                P.dma("qpool", wbuf[k % 2][:, 0:nc_], dsrc.ap()[128 * k:128 * (k + 1), :])
                for g in range(nc_ // 512):
                    P.act(s1[:, :], wbuf[k % 2][:, 512 * g:512 * (g + 1)], AF.Abs)
                    P.mm(bsum[0:1, 0:512], P.onec[:, 0:1], s1[:, :],
                         start=(nmm == 0), stop=(nmm == nr * (nc_ // 512) - 1))
                    nmm += 1
            P.red(rwrow[0:1, wi:wi + 1], bsum[0:1, 0:512], ALU.add)
            P.ts(rwrow[0:1, wi:wi + 1], rwrow[0:1, wi:wi + 1], 1.0 / (4096.0 * 1024.0), None, ALU.mult)
            P.ts(rwrow[0:1, wi:wi + 1], rwrow[0:1, wi:wi + 1], 1e-5, None, ALU.max)
        swc = P.acq1((128, 4), F32)
        sww = P.acq1((1, 4), F32)
        P.op("dve", lambda v, o=sww[0:1, :], i=rwrow[0:1, :]: v.reciprocal(o, i),
             reads=[rwrow], writes=[sww])
        bsw = P.bks[3]
        P.mm(bsw[:, 0:4], P.ones1[0:1, :], sww[0:1, :])
        P.copy(swc[:, :], bsw[:, 0:4], eng="act")
        wbf = P.acq(2, (128, 2048), F32)
        for wi, (dsrc, ddst, nr, nc_) in enumerate(WSPEC):
            for k in range(nr):
                for hf in range((nc_ + 2047) // 2048):
                    w = min(2048, nc_ - 2048 * hf)
                    hs = slice(2048 * hf, 2048 * hf + w)
                    wb = wbf[(2 * k + hf) % 2]
                    P.dma("qsp", wb[:, 0:w], dsrc.ap()[128 * k:128 * (k + 1), hs])
                    qt = P.acq1((128, 2048), BF16)
                    for g in range(w // 512):
                        sl = slice(512 * g, 512 * (g + 1))
                        P.ts(s1[:, :], wb[:, sl], swc[:, wi:wi + 1], None, ALU.mult)
                        P.ts(s1[:, :], s1[:, :], MAGICF, None, ALU.add)
                        P.ts(s1[:, :], s1[:, :], MAGICF, 1.0, ALU.subtract, ALU.min)
                        P.ts(qt[:, sl], s1[:, :], -1.0, None, ALU.max)
                    P.dma("qsp", ddst.ap()[128 * k:128 * (k + 1), hs], qt[:, 0:w])
                    P.rel(qt)
        P.dma("qsp", d_rw.ap(), rwrow[0:1, :])
        P.rel(wbuf + wbf + [swc, sww])
    else:
        P.dma("qsp", rwrow[0:1, :], d_rw.ap())
    P.rel([mrow])
    P.barrier()
    rxg = P.acq1((1, T), F32)
    rxu = P.acq1((1, T), F32)
    rowmul_scalar(P, rxg, rx, rwrow[0:1, 0:1], T)
    rowmul_scalar(P, rxu, rx, rwrow[0:1, 1:2], T)
    P.rel(rx)
    rxgb = P.acq1((128, T), F32)
    rxub = P.acq1((128, T), F32)
    bcast_to_sbuf(P, rxg[0:1, :], rxgb, T)
    bcast_to_sbuf(P, rxu[0:1, :], rxub, T)
    if dbg:
        d_drx = P.dram_out("dbg_rx", [3, T])
        P.dma("qsp", d_drx.ap()[0:1, :], rx[0:1, :])
        P.dma("qsp", d_drx.ap()[1:2, :], rxg[0:1, :])
        P.dma("qsp", d_drx.ap()[2:3, :], rxu[0:1, :])
    P.rel([rxg, rxu])
    pacc = P.acq1((128, T), F32)
    P.memset(pacc[:, :], 0.0)
    if dbg:
        dgate = P.acq1((128, T), F32)
        dup = P.acq1((128, T), F32)
        d_dg = P.dram_out("dbg_gate", [128, T])
        d_du = P.dram_out("dbg_up", [128, T])
    wgb = [P.acq(8, (128, 512), BF16), P.acq(8, (128, 512), BF16)]
    wub = [P.acq(8, (128, 512), BF16), P.acq(8, (128, 512), BF16)]
    sg = P.acq1((128, 512), F32)
    su = P.acq1((128, 512), F32)
    pt_ = P.acq1((128, T), F32)

    def load_gu2(i):
        cs2 = slice(512 * i, 512 * (i + 1))
        for k in range(8):
            P.dma("qpool", wgb[i % 2][k][:, :], d_wqg.ap()[128 * k:128 * (k + 1), cs2])
            P.dma("qpool", wub[i % 2][k][:, :], d_wqu.ap()[128 * k:128 * (k + 1), cs2])
    load_gu2(0)
    for ib in range(8):
        wg_s, wu_s = wgb[ib % 2], wub[ib % 2]
        if ib + 1 < 8:
            load_gu2(ib + 1)
        for ic in range(4):
            gic = 4 * ib + ic
            for g in range(T // 512):
                sl = slice(512 * g, 512 * (g + 1))
                bG = P.bks[g % 2]
                bU = P.bks[2 + (g % 2)]
                for k in range(8):
                    P.mm(bG[:, 0:512], wg_s[k][:, 128 * ic:128 * (ic + 1)], xq[k][:, sl],
                         start=(k == 0), stop=(k == 7))
                for k in range(8):
                    P.mm(bU[:, 0:512], wu_s[k][:, 128 * ic:128 * (ic + 1)], xq[k][:, sl],
                         start=(k == 0), stop=(k == 7))
                P.tt(sg[:, :], bG[:, 0:512], rxgb[:, sl], ALU.mult)
                P.act(sg[:, :], sg[:, :], AF.Silu)
                P.tt(su[:, :], bU[:, 0:512], rxub[:, sl], ALU.mult)
                P.tt(pt_[:, sl], sg[:, :], su[:, :], ALU.mult)
                if dbg and ib == 0 and ic == 0:
                    P.tt(dgate[:, sl], sg[:, :], sg[:, :], ALU.bypass)
                    P.tt(dup[:, sl], su[:, :], su[:, :], ALU.bypass)
            P.act(absx[:, :], pt_[:, :], AF.Abs)
            P.tt(pacc[:, :], pacc[:, :], absx[:, :], ALU.max)
            P.dma("qsp", d_p.ap()[128 * gic:128 * (gic + 1), :], pt_[:, :])
    if dbg:
        P.dma("qsp", d_dg.ap(), dgate[:, :])
        P.dma("qsp", d_du.ap(), dup[:, :])
        d_dp = P.dram_out("dbg_p", [128, T])
        pf = P.acq1((128, T), BF16)
        P.dma("qpool", pf[:, :], d_p.ap()[0:128, :])
        pf2 = P.acq1((128, T), F32)
        P.copy(pf2[:, :], pf[:, :])
        P.dma("qsp", d_dp.ap(), pf2[:, :])
        P.rel([pf, pf2])
    P.rel(wgb[0] + wgb[1] + wub[0] + wub[1] + [sg, su, pt_, rxgb, rxub] + xq)
    P.barrier()
    prow = P.acq1((1, T), F32)
    pmax_to_row(P, pacc, T, d_b16, prow, s1)
    P.ts(prow[0:1, :], prow[0:1, :], 1e-5, None, ALU.max)
    rpd = P.acq1((1, T), F32)
    P.ts(rpd[0:1, :], prow[0:1, :], 0.5 / 127.0, None, ALU.mult)
    rowmul_scalar(P, rpd, rpd, rwrow[0:1, 2:3], T)
    P.op("dve", lambda v, o=prow[0:1, :], i=prow[0:1, :]: v.reciprocal(o, i),
         reads=[prow], writes=[prow])
    P.ts(prow[0:1, :], prow[0:1, :], 127.0, None, ALU.mult)
    spb = P.acq1((128, T), F32)
    rpdb = P.acq1((128, T), F32)
    bcast_to_sbuf(P, prow[0:1, :], spb, T)
    bcast_to_sbuf(P, rpd[0:1, :], rpdb, T)
    oacc = P.acq(8, (128, T), F32)
    pqb = [P.acq(4, (128, T), BF16), P.acq(4, (128, T), BF16)]
    pf32 = P.acq(2, (128, T), F32)
    wdb = [P.acq(4, (128, 1024), BF16), P.acq(4, (128, 1024), BF16)]

    def load_wd(bi, kg):
        for kk in range(4):
            k = 4 * kg + kk
            P.dma("qpool", wdb[bi][kk][:, :], d_wqd.ap()[128 * k:128 * (k + 1), :])
    load_wd(0, 0)
    for kg in range(8):
        pq = pqb[kg % 2]
        for kk in range(4):
            k = 4 * kg + kk
            P.dma("qsp", pf32[kk % 2][:, :], d_p.ap()[128 * k:128 * (k + 1), :])
            P.tt(acc[:, :], pf32[kk % 2][:, :], spb[:, :], ALU.mult)
            P.ts(acc[:, :], acc[:, :], MAGICF, None, ALU.add)
            P.ts(acc[:, :], acc[:, :], MAGICF, 127.0, ALU.subtract, ALU.min)
            P.ts(pq[kk][:, :], acc[:, :], -128.0, None, ALU.max)
        if kg + 1 < 8:
            load_wd((kg + 1) % 2, kg + 1)
        wd_s = wdb[kg % 2]
        for oc in range(8):
            for kk in range(4):
                for g in range(T // 512):
                    b = P.bks[g % 4]
                    P.mm(b[:, 0:512], wd_s[kk][:, 128 * oc:128 * (oc + 1)],
                         pq[kk][:, 512 * g:512 * (g + 1)], start=(kk == 0), stop=(kk == 3))
            for g in range(T // 512):
                sl = slice(512 * g, 512 * (g + 1))
                b = P.bks[g % 4]
                if kg == 0:
                    P.copy(oacc[oc][:, sl], b[:, 0:512])
                else:
                    P.tt(oacc[oc][:, sl], oacc[oc][:, sl], b[:, 0:512], ALU.add)
    P.rel(pqb[0] + pqb[1] + wdb[0] + wdb[1] + [spb, pacc])
    P.barrier()
    x2 = P.acq(2, (128, T), F32)
    for oc in range(8):
        xo = x2[oc % 2]
        P.dma("qsp", xo[:, :], d_xin.ap()[128 * oc:128 * (oc + 1), :])
        P.tt(oacc[oc][:, :], oacc[oc][:, :], rpdb[:, :], ALU.mult)
        P.tt(xo[:, :], xo[:, :], oacc[oc][:, :], ALU.add)
        P.dma("qsp", d_xout.ap()[128 * oc:128 * (oc + 1), :], xo[:, :])
    P.rel(oacc + x2 + pf32 + [rpdb, prow, rpd, rwrow, s1, acc, absx])
    P.finish()



def emit_l0a1(P, T=THALF, n_pair=2, d_xin=None, d_s1=None):
    TF = 2 * T
    NHL = 8
    d_xin = d_xin or P.dram_in("x_in", [D, T])
    d_s1 = d_s1 or P.dram_out("s1_out", [D, T])
    d_xln = P.dram_tmp("d_xln", [D, T])
    d_kvdT = P.dram_in("l0_kvdT", [D, 256])
    d_qdT = P.dram_in("l0_qdT", [D, 256])
    d_ropekT = P.dram_in("l0_ropekT_mine", [D, 256])
    d_kup = P.dram_in("l0_kupermT", [256, 512])
    d_qup = P.dram_in("l0_qupermT", [256, 512])
    d_ropeqT = P.dram_in("l0_ropeqT_mine", [256, 256])
    d_vuT = P.dram_in("l0_vuT_mine", [256, 512])
    d_owT = P.dram_in("l0_owT", [D, D])
    d_bo_zero = P.dram_in("l0_ln1w", [128, 8])
    d_cos = P.dram_in("l0_cos", [128, TF])
    d_sin = P.dram_in("l0_sin", [128, TF])
    d_rotP = P.dram_in("l0_rotP", [128, 128])
    d_selR = P.dram_in("l0_selR", [2 * 128, 128])
    d_cmask = P.dram_in("l0_cmask", [128, 4 * 512])
    d_cmaskT = P.dram_in("l0_cmaskT", [128, 4 * 512])
    d_gidx = P.dram_in("l0_gidx", [128, 8], mybir.dt.int32)
    b_ag1 = P.dram_tmp("b_ag1", [D, T], BF16)
    b_ag1o = P.dram_tmp("b_ag1o", [2 * D, T], BF16)
    b_ag2 = P.dram_tmp("b_ag2", [512, TF], BF16)
    b_ag2o = P.dram_tmp("b_ag2o", [D, TF], BF16)
    d_mrow = P.dram_tmp("d_mrow", [32, 128])
    groups = PAIRS[:n_pair // 2] if n_pair < 8 else PAIRS

    P.std_init()
    P.init_banks()
    identf_init(P)
    x = P.acq(8, (128, T), F32)
    emit_load_x(P, x, d_xin, T)
    xln = P.acq(8, (128, T), F32)
    emit_ln(P, x, xln, T, pre="oln", xout32=None, gcol=None)
    P.rel(x)
    for c in range(8):
        P.dma("qsp", d_xln.ap()[128 * c:128 * (c + 1), :], xln[c][:, :])
    ln1w = P.sb("ln1w", [128, 8], F32)
    P.dma("qsp", ln1w[:, :], d_bo_zero.ap())
    y1 = P.acq(8, (128, T), BF16)
    emit_ln(P, xln, y1, T, pre="ln1", gcol=ln1w)
    P.rel(xln)
    for c in range(8):
        P.dma("qsp", b_ag1.ap()[128 * c:128 * (c + 1), :], y1[c][:, :])
    P.rel(y1)
    P.cc("AllGather", ALU.bypass, groups, [b_ag1.ap()], [b_ag1o.ap()])
    P.barrier()
    y1F_view = b_ag1o.ap().rearrange("(h f) t -> f h t", h=2)
    y1f = P.acq(8, (128, TF), BF16)
    for c in range(8):
        P.dma("qsp", y1f[c][:, :].rearrange("p (h t) -> p h t", h=2),
              y1F_view[128 * c:128 * (c + 1), :, :])
    def fm_proj_full(P, wT_tiles, src, noch, out, bk0=0):
        for oc in range(noch):
            for g in range(TF // 512):
                sl = slice(512 * g, 512 * (g + 1))
                b = P.bks[bk0 + ((oc * 8 + g) % 4)]
                for k in range(len(wT_tiles)):
                    P.mm(b[:, 0:512], wT_tiles[k][:, 128 * oc:128 * (oc + 1)], src[k][:, sl],
                         start=(k == 0), stop=(k == len(wT_tiles) - 1))
                P.copy(out[oc][:, sl], b[:, 0:512], eng="act" if (oc + g) % 2 else "dve")
    wt = load_wt(P, d_kvdT, "wt", 8, 256)
    kvl = P.acq(2, (128, TF), BF16)
    fm_proj_full(P, wt, y1f, 2, kvl)
    P.barrier()
    for k in range(8):
        P.dma("qpool", wt[k][:, 0:256], d_qdT.ap()[128 * k:128 * (k + 1), :])
    ql = P.acq(2, (128, TF), BF16)
    fm_proj_full(P, wt, y1f, 2, ql)
    P.barrier()
    for k in range(8):
        P.dma("qpool", wt[k][:, 0:256], d_ropekT.ap()[128 * k:128 * (k + 1), :])
    krp = P.acq(2, (128, TF), BF16)
    fm_proj_full(P, wt, y1f, 2, krp)
    P.rel(y1f + wt)
    P.barrier()
    cosb = P.acq1((128, TF), BF16)
    sinb = P.acq1((128, TF), BF16)
    P.dma("qpool", cosb[:, :], d_cos.ap())
    P.dma("qpool", sinb[:, :], d_sin.ap())
    rotP = P.acq1((128, 128), BF16)
    P.dma("qpool", rotP[:, :], d_rotP.ap())
    wu2 = P.acq(2, (128, 512), BF16)
    qrp = P.acq(2, (128, TF), BF16)
    for k in range(2):
        P.dma("qpool", wu2[k][:, 0:256], d_ropeqT.ap()[128 * k:128 * (k + 1), :])
    fm_proj_full(P, wu2, ql, 2, qrp)
    s1t = P.acq1((128, 512), F32)
    s2t = P.acq1((128, 512), F32)
    def rot_apply(rp):
        for c in range(2):
            for g in range(TF // 512):
                sl = slice(512 * g, 512 * (g + 1))
                b = P.bks[4 + (g % 2)]
                P.mm(b[:, 0:512], rotP[:, :], rp[c][:, sl])
                P.tt(s1t[:, :], rp[c][:, sl], cosb[:, sl], ALU.mult)
                P.tt(s2t[:, :], b[:, 0:512], sinb[:, sl], ALU.mult)
                P.tt(rp[c][:, sl], s1t[:, :], s2t[:, :], ALU.add)
    rot_apply(qrp)
    rot_apply(krp)
    P.rel([cosb, sinb, rotP])
    P.barrier()
    selR = P.acq(2, (128, 128), BF16)
    for c in range(2):
        P.dma("qpool", selR[c][:, :], d_selR.ap()[128 * c:128 * (c + 1), :])
    qh = P.acq(4, (128, TF), BF16)
    kh = P.acq(4, (128, TF), BF16)
    for (dst, dup_, rp) in [(qh, d_qup, qrp), (kh, d_kup, krp)]:
        for k in range(2):
            P.dma("qpool", wu2[k][:, :], dup_.ap()[128 * k:128 * (k + 1), :])
        for oc in range(4):
            for g in range(TF // 512):
                sl = slice(512 * g, 512 * (g + 1))
                b = P.bks[(oc * 8 + g) % 4]
                P.mm(b[:, 0:512], wu2[0][:, 128 * oc:128 * (oc + 1)], (ql if dst is qh else kvl)[0][:, sl],
                     start=True, stop=False)
                P.mm(b[:, 0:512], wu2[1][:, 128 * oc:128 * (oc + 1)], (ql if dst is qh else kvl)[1][:, sl],
                     start=False, stop=False)
                P.mm(b[:, 0:512], selR[oc % 2][:, :], rp[oc // 2][:, sl],
                     start=False, stop=True)
                P.copy(dst[oc][:, sl], b[:, 0:512], eng="act" if (oc + g) % 2 else "dve")
    P.rel(qrp + krp + ql + wu2 + selR)
    P.barrier()
    vu = P.acq(2, (128, 512), BF16)
    for k in range(2):
        P.dma("qpool", vu[k][:, :], d_vuT.ap()[128 * k:128 * (k + 1), :])
    vt = P.acq(TF // 128, (128, 1024), BF16)
    for ti in range(TF // 128):
        b = P.bks[ti % 4]
        for k in range(2):
            P.mm(b[:, 0:512], kvl[k][:, 128 * ti:128 * (ti + 1)], vu[k][:, :],
                 start=(k == 0), stop=(k == 1))
        P.memset(vt[ti][:, 0:520], 1.0)
        for h8 in range(8):
            P.copy(vt[ti][:, 65 * h8:65 * h8 + 64], b[:, 64 * h8:64 * h8 + 64])
    P.rel(kvl + vu)
    P.barrier()
    SC = 1.0 / 8.0
    cmaskT = P.acq1((128, 2048), BF16)
    P.dma("qpool", cmaskT[:, :], d_cmaskT.ap())
    at = P.acq(4, (128, TF), BF16)
    at2r = P.acq(4, (128, 512), BF16)
    col = P.acq(2, (128, 1), F32)
    nrm = P.acq1((1, 2048), F32)
    zrow = P.acq1((1, 512), F32)
    zrow2 = P.acq1((1, 512), F32)
    sq64 = P.acq(2, (128, 512), F32)
    for h in range(NHL):
        ch, po = h // 2, 64 * (h % 2)
        pb = slice(po, po + 64)
        for ri, src_ in [(0, qh), (1, kh)]:
            for g in range(TF // 512):
                s2_ = sq64[g % 2]
                P.act(s2_[pb, :], src_[ch][pb, 512 * g:512 * (g + 1)], AF.Square)
                bq2 = P.bks[4 + (g % 2)]
                P.mm(bq2[0:1, 0:512], P.onec[pb, 0:1], s2_[pb, :], tile_position=(po, 0))
                P.red(nrm[0:1, 16 + 8 * ri + g:17 + 8 * ri + g], bq2[0:1, 0:512], ALU.max)
        P.copy(nrm[0:1, 8:9], nrm[0:1, 24:25])
        for g in range(1, TF // 512):
            P.tt(nrm[0:1, 8 + g:9 + g], nrm[0:1, 7 + g:8 + g], nrm[0:1, 24 + g:25 + g], ALU.max)
        P.tt(nrm[0:1, 32:40], nrm[0:1, 16:24], nrm[0:1, 8:16], ALU.mult)
        P.act(nrm[0:1, 32:40], nrm[0:1, 32:40], AF.Sqrt)
        P.ts(nrm[0:1, 32:40], nrm[0:1, 32:40], -SC, None, ALU.mult)
        for qg in range(TF // 512):
            nkt = 4 * qg + 4
            bm = P.bks[4]
            P.mm(bm[:, 0:1], P.ones1[0:1, :], nrm[0:1, 32 + qg:33 + qg])
            P.copy(col[qg % 2][:, 0:1], bm[:, 0:1])
            bcol = col[qg % 2]
            bo_ = P.bks[4]

            def S_mm(kt):
                P.mm(P.bks[kt % 4][:, 0:512], kh[ch][pb, 128 * kt:128 * (kt + 1)],
                     qh[ch][pb, 512 * qg:512 * (qg + 1)], tile_position=(po, 0))
            S_mm(0)
            if nkt > 1:
                S_mm(1)
            for kt in range(nkt):
                b = P.bks[kt % 4]
                at2 = at2r[kt % 4]
                P.act(at2[:, :], b[:, 0:512], AF.Exp, bias=bcol[:, 0:1], scale=SC)
                if kt // 4 == qg:
                    dg = kt % 4
                    P.tt(at2[:, :], at2[:, :], cmaskT[:, 512 * dg:512 * dg + 512], ALU.mult)
                if kt + 2 < nkt:
                    S_mm(kt + 2)
                P.mm(bo_[0:65, 0:512], vt[kt][:, 65 * h:65 * h + 65], at2[:, :],
                     start=(kt == 0), stop=(kt == nkt - 1))
            zr = zrow if qg % 2 == 0 else zrow2
            P.copy(zr[0:1, 0:512], bo_[64:65, 0:512])
            P.op("dve", lambda v, o=zr[0:1, 0:512], i=zr[0:1, 0:512]: v.reciprocal(o, i),
                 reads=[zr], writes=[zr])
            bb = P.bks[5]
            P.mm(bb[0:64, 0:512], P.ones1[0:1, 0:64], zr[0:1, 0:512])
            P.copy(at[ch][pb, 512 * qg:512 * (qg + 1)], bo_[0:64, 0:512], eng="act")
            P.tt(at[ch][pb, 512 * qg:512 * (qg + 1)], at[ch][pb, 512 * qg:512 * (qg + 1)],
                 bb[0:64, 0:512], ALU.mult)
    P.rel(qh + kh + vt + [cmaskT, nrm, zrow, zrow2, s1t, s2t] + col + at2r + sq64)


    P.barrier()
    for c in range(4):
        P.dma("qsp", b_ag2.ap()[128 * c:128 * (c + 1), :], at[c][:, :])
    P.rel(at)
    P.cc("AllGather", ALU.bypass, groups, [b_ag2.ap()], [b_ag2o.ap()])
    P.barrier()
    gidx = P.sb("gidx", [128, 8], mybir.dt.int32)
    P.dma("qsp", gidx[:, :], d_gidx.ap())
    af = P.acq(8, (128, T), BF16)
    gv = b_ag2o.ap().rearrange("d (h t) -> (d h) t", h=2)
    for c in range(8):
        P.idma_gather(af[c][:, :], gv, gidx[:, c:c + 1])
    P.barrier()
    wt8 = load_wt(P, d_owT, "wt8", 8, D)
    x3 = P.acq(2, (128, T), F32)
    scr = P.acq1((128, 512), F32)
    for oc in range(8):
        xo = x3[oc % 2]
        P.dma("qsp", xo[:, :], d_xln.ap()[128 * oc:128 * (oc + 1), :])
        for k in range(8):
            for g in range(T // 512):
                b = P.bks[g % 4]
                P.mm(b[:, 0:512], wt8[k][:, 128 * oc:128 * (oc + 1)],
                     af[k][:, 512 * g:512 * (g + 1)], start=(k == 0), stop=(k == 7))
        for g in range(T // 512):
            sl = slice(512 * g, 512 * (g + 1))
            P.tt(xo[:, sl], xo[:, sl], P.bks[g % 4][:, 0:512], ALU.add)
        P.dma("qsp", d_s1.ap()[128 * oc:128 * (oc + 1), :], xo[:, :])
    P.rel(af + x3 + [scr] + wt8)
    P.finish()



def emit_l0a2(P, T=THALF, d_xin=None, d_s1=None, d_xout=None):
    NE = 7
    I2 = 2048
    d_xin = d_xin or P.dram_in("x_in", [D, T])
    d_s1 = d_s1 or P.dram_in("s1_in", [D, T])
    d_xout = d_xout or P.dram_out("x_out", [D, T])
    d_ln2w = P.dram_in("l0_ln2w", [128, 8])
    d_rwT = P.dram_in("l0_routerT", [D, NE])
    d_rb = P.dram_in("l0_rbias", [1, NE])
    d_guT = P.dram_in("l0_gucatT", [D, 8 * 4096])
    d_dnT = P.dram_in("l0_dncatT", [8 * I2, D])
    d_wscr = P.dram_tmp("wscr0", [128, 128])

    P.std_init()
    P.init_banks()
    identf_init(P)
    s1 = P.acq(8, (128, T), F32)
    for c in range(8):
        P.dma("qsp", s1[c][:, :], d_s1.ap()[128 * c:128 * (c + 1), :])
    ln2w = P.sb("ln2w", [128, 8], F32)
    P.dma("qsp", ln2w[:, :], d_ln2w.ap())
    y2 = P.acq(8, (128, T), BF16)
    mu, rs = emit_ln(P, s1, y2, T, pre="ln2", gcol=ln2w, keep_rows=True)
    P.barrier()
    rwpk = P.acq1((128, 512), F32)
    rwlnT = [rwpk[:, 7 * k:7 * (k + 1)] for k in range(8)]
    for k in range(8):
        P.dma("qsp", rwlnT[k], d_rwT.ap()[128 * k:128 * (k + 1), :])
        P.ts(rwlnT[k], rwlnT[k], ln2w[:, k:k + 1], None, ALU.mult)
    Brow = P.sb("Brow0", [1, NE], F32)
    bank = P.bks[5]
    for k in range(8):
        P.mm(bank[0:1, 0:NE], P.onec[:, 0:1], rwlnT[k], start=(k == 0), stop=(k == 7))
    P.copy(Brow[0:1, :], bank[0:1, 0:NE])
    rbr = P.sb("rbr", [1, NE], F32)
    P.dma("qsp", rbr[:, :], d_rb.ap())
    rb_b = P.sb("rb_b", [128, NE], F32)
    P.mm(bank[:, 0:NE], P.ones1[0:1, :], rbr[0:1, :])
    P.copy(rb_b[:, :], bank[:, 0:NE], eng="act")
    wbuf = P.sb("wbuf0", [128, 128], F32)
    P.memset(wbuf[:, :], 0.0)
    sc = P.acq1((128, 512), F32)
    t1 = P.acq1((128, 512), F32)
    t2 = P.acq1((128, 512), F32)
    col = P.acq1((128, 1), F32)
    for ti in range(T // 128):
        tsl = slice(128 * ti, 128 * (ti + 1))
        b = P.bks[ti % 2]
        for k in range(8):
            P.mm(b[:, 0:NE], s1[k][:, tsl], rwlnT[k], start=(k == 0), stop=(k == 7))
        bmb = P.bks[2 + (ti % 2)]
        P.mm(bmb[:, 0:NE], mu[0:1, tsl], Brow[0:1, :])
        P.mm(P.bks[4][:, 0:1], rs[0:1, tsl], P.ones1[0:1, 0:1])
        P.copy(col[:, 0:1], P.bks[4][:, 0:1])
        P.copy(sc[:, 0:NE], b[:, 0:NE], eng="act")
        P.tt(sc[:, 0:NE], sc[:, 0:NE], bmb[:, 0:NE], ALU.subtract)
        P.ts(sc[:, 0:NE], sc[:, 0:NE], col[:, 0:1], None, ALU.mult)
        P.tt(sc[:, 0:NE], sc[:, 0:NE], rb_b[:, 0:NE], ALU.add)
        P.act(sc[:, 0:NE], sc[:, 0:NE], AF.Sigmoid)
        P.red(col[:, 0:1], sc[:, 0:NE], ALU.max)
        P.ts(t1[:, 0:NE], sc[:, 0:NE], col[:, 0:1], None, ALU.is_ge)
        P.copy(t2[:, 8:9], col[:, 0:1])
        P.ts(t2[:, 0:NE], t1[:, 0:NE], -1e30, None, ALU.mult)
        P.tt(t2[:, 0:NE], t2[:, 0:NE], sc[:, 0:NE], ALU.add)
        P.red(col[:, 0:1], t2[:, 0:NE], ALU.max)
        P.ts(t2[:, 0:NE], t2[:, 0:NE], col[:, 0:1], None, ALU.is_ge)
        P.tt(t1[:, 0:NE], t1[:, 0:NE], t2[:, 0:NE], ALU.add)
        P.tt(col[:, 0:1], col[:, 0:1], t2[:, 8:9], ALU.add)
        P.op("dve", lambda v, o=col[:, 0:1], i=col[:, 0:1]: v.reciprocal(o, i),
             reads=[col], writes=[col])
        P.tt(t1[:, 0:NE], t1[:, 0:NE], sc[:, 0:NE], ALU.mult)
        P.ts(wbuf[:, 8 * ti:8 * ti + NE], t1[:, 0:NE], col[:, 0:1], None, ALU.mult)
    P.rel([sc, t1, t2, col, rwpk, mu, rs])
    bank = P.bks[4]
    P.transp(bank[:, 0:128], wbuf[:, :], P.identf[:, :])
    wtr = P.acq1((128, 512), F32)
    P.copy(wtr[:, 0:128], bank[:, 0:128])
    P.dma("qsp", d_wscr.ap(), wtr[:, 0:128])
    P.rel(wtr)
    x = []
    xts = [P.acq1((128, T), F32), P.acq1((128, T), F32)]
    for c in range(8):
        xt = xts[c % 2]
        P.dma("qsp", xt[:, :], d_xin.ap()[128 * c:128 * (c + 1), :])
        P.stt(s1[c][:, :], s1[c][:, :], 0.5, xt[:, :], ALU.mult, ALU.add)
        x.append(s1[c])
    P.rel(xts)
    P.barrier()
    v = d_wscr.ap().rearrange("(a e) p -> e a p", e=8)
    wrowb = P.acq1((1, T), F32)
    a = P.acq(16, (128, T // 2), BF16)
    gub = [P.acq(8, (128, 512), BF16), P.acq(8, (128, 512), BF16)]
    dnb = [P.acq(4, (128, 1024), BF16), P.acq(4, (128, 1024), BF16)]
    wgtb = P.acq1((128, T // 2), F32)

    def load_gu(buf, e, ib, which):
        col0 = 4096 * e + 2048 * which + 512 * ib
        for k in range(8):
            P.dma("qpool", buf[k][:, :], d_guT.ap()[128 * k:128 * (k + 1),
                  col0:col0 + 512])

    def load_dn(buf, e, q):
        for ic in range(4):
            gi = 16 * e + 4 * q + ic
            P.dma("qpool", buf[ic][:, :], d_dnT.ap()[128 * gi:128 * (gi + 1), :])

    gu_all = [(e, ib, w) for e in range(8) for ib in range(4) for w in (0, 1)] * 2
    dn_all = [(e, q) for e in range(8) for q in range(4)] * 2
    load_gu(gub[0], *gu_all[0])
    load_dn(dnb[0], *dn_all[0])
    gu_i = 0
    dn_i = 0
    for th in range(2):
        for e in range(8):
            if e < NE:
                P.dma("qsp", wrowb[0:1, :].rearrange("x (a p) -> x a p", a=T // 128),
                      v[e:e + 1, :, :])
                for g in range(T // 2 // 512):
                    bnk = P.bks[4 + (g % 2)]
                    P.mm(bnk[:, 0:512], P.ones1[0:1, :],
                         wrowb[0:1, T // 2 * th + 512 * g:T // 2 * th + 512 * (g + 1)])
                    P.copy(wgtb[:, 512 * g:512 * (g + 1)], bnk[:, 0:512], eng="act")
            for ib in range(4):
                gu_s = gub[gu_i % 2]
                if gu_i + 1 < len(gu_all):
                    load_gu(gub[(gu_i + 1) % 2], *gu_all[gu_i + 1])
                gu_i += 1
                for ic in range(4):
                    gic = 4 * ib + ic
                    for k in range(8):
                        for g in range(T // 2 // 512):
                            sl = slice(T // 2 * th + 512 * g, T // 2 * th + 512 * (g + 1))
                            bG = P.bks[(2 * ic + g) % 4]
                            P.mm(bG[:, 0:512], gu_s[k][:, 128 * ic:128 * (ic + 1)], y2[k][:, sl],
                                 start=(k == 0), stop=(k == 7))
                    for g in range(T // 2 // 512):
                        P.act(a[gic][:, 512 * g:512 * (g + 1)],
                              P.bks[(2 * ic + g) % 4][:, 0:512], AF.Silu)
                gu_s = gub[gu_i % 2]
                if gu_i + 1 < len(gu_all):
                    load_gu(gub[(gu_i + 1) % 2], *gu_all[gu_i + 1])
                gu_i += 1
                for ic in range(4):
                    gic = 4 * ib + ic
                    for k in range(8):
                        for g in range(T // 2 // 512):
                            sl = slice(T // 2 * th + 512 * g, T // 2 * th + 512 * (g + 1))
                            bU = P.bks[(2 * ic + g) % 4]
                            P.mm(bU[:, 0:512], gu_s[k][:, 128 * ic:128 * (ic + 1)], y2[k][:, sl],
                                 start=(k == 0), stop=(k == 7))
                    for g in range(T // 2 // 512):
                        lsl = slice(512 * g, 512 * (g + 1))
                        bU = P.bks[(2 * ic + g) % 4]
                        P.tt(a[gic][:, lsl], a[gic][:, lsl], bU[:, 0:512], ALU.mult)
                        if e < NE:
                            P.tt(a[gic][:, lsl], a[gic][:, lsl], wgtb[:, lsl], ALU.mult)
            for q in range(4):
                dn4 = dnb[dn_i % 2]
                if dn_i + 1 < len(dn_all):
                    load_dn(dnb[(dn_i + 1) % 2], *dn_all[dn_i + 1])
                dn_i += 1
                for oc in range(8):
                    for g in range(T // 2 // 512):
                        sl = slice(T // 2 * th + 512 * g, T // 2 * th + 512 * (g + 1))
                        lsl = slice(512 * g, 512 * (g + 1))
                        b = P.bks[4 + (g % 2)]
                        for ic in range(4):
                            P.mm(b[:, 0:512], dn4[ic][:, 128 * oc:128 * (oc + 1)],
                                 a[4 * q + ic][:, lsl], start=(ic == 0), stop=(ic == 3))
                        P.stt(x[oc][:, sl], b[:, 0:512], 0.5, x[oc][:, sl], ALU.mult, ALU.add)
    P.rel(a + gub[0] + gub[1] + dnb[0] + dnb[1] + [wgtb, wrowb] + y2)
    assert gu_i == len(gu_all) and dn_i == len(dn_all)

    P.barrier()
    emit_store_x(P, x, d_xout, T)
    P.rel(x)
    P.finish()



def _T32(a):
    return np.ascontiguousarray(np.asarray(a).T).astype(np.float32)


def _colN(b, n):
    return np.ascontiguousarray(np.asarray(b).reshape(n, 128).T).astype(np.float32)


def _prep(inputs):
    z = inputs
    f32 = np.float32
    g = {}
    g["l0_kvdT"] = _T32(z["L0_ffn_kv_d"])
    g["l0_qdT"] = _T32(z["L0_ffn_q_d"])
    g["l0_owT"] = _T32(z["L0_ffn_o_w"])
    g["l0_ln1w"] = _colN(z["L0_ffn_ln1_w"], 8)
    g["l0_ln2w"] = _colN(z["L0_ffn_ln2_w"], 8)
    g["l0_routerT"] = _T32(z["L0_ffn_router_w"])
    g["l0_rbias"] = np.asarray(z["L0_ffn_routing_bias"]).reshape(1, 7).astype(f32)
    gu = np.zeros((1024, 8 * 4096), f32)
    dn = np.zeros((8 * 2048, 1024), f32)
    for e in range(7):
        gu[:, 4096 * e:4096 * e + 2048] = np.asarray(z["L0_ffn_routed_gate"][e]).T
        gu[:, 4096 * e + 2048:4096 * (e + 1)] = np.asarray(z["L0_ffn_routed_up"][e]).T
        dn[2048 * e:2048 * (e + 1), :] = np.asarray(z["L0_ffn_routed_down"][e]).T
    gu[:, 7 * 4096:7 * 4096 + 2048] = np.asarray(z["L0_ffn_shared_gate"][0]).T
    gu[:, 7 * 4096 + 2048:] = np.asarray(z["L0_ffn_shared_up"][0]).T
    dn[7 * 2048:, :] = np.asarray(z["L0_ffn_shared_down"][0]).T
    g["l0_gucatT"] = gu
    g["l0_dncatT"] = dn
    RD = 32
    inv = 1.0 / (10000.0 ** (np.arange(0, RD, 2) / RD))
    fr = np.arange(4096)[:, None] * inv[None, :]
    emb = np.concatenate([fr, fr], -1)
    cos, sin = np.cos(emb), np.sin(emb)
    c32 = np.zeros((128, 4096), f32)
    s32 = np.zeros((128, 4096), f32)
    for b in range(4):
        c32[32 * b:32 * b + 32, :] = cos.T
        s32[32 * b:32 * b + 32, :] = sin.T
    g["l0_cos"] = c32
    g["l0_sin"] = s32
    rotP = np.zeros((128, 128), f32)
    for b in range(4):
        for d in range(16):
            rotP[32 * b + d + 16, 32 * b + d] = -1.0
        for d in range(16, 32):
            rotP[32 * b + d - 16, 32 * b + d] = 1.0
    g["l0_rotP"] = rotP
    selR = np.zeros((2, 128, 128), f32)
    for par in range(2):
        for hl2 in range(2):
            for j in range(32):
                selR[par, 64 * par + 32 * hl2 + j, 64 * hl2 + j] = 1.0
    g["l0_selR"] = selR.reshape(256, 128)
    cm1 = np.zeros((128, 2048), f32)
    cm2 = np.zeros((128, 2048), f32)
    i_ = np.arange(128)[:, None]
    j_ = np.arange(512)[None, :]
    for dg in range(4):
        cm1[:, 512 * dg:512 * dg + 512] = (j_ <= 128 * dg + i_)
        cm2[:, 512 * dg:512 * dg + 512] = (j_ >= 128 * dg + i_)
    g["l0_cmask"] = cm1
    g["l0_cmaskT"] = cm2
    g["l0b_wqT"] = _T32(z["L0_attn_Wq"]); g["l0b_wkT"] = _T32(z["L0_attn_Wk"])
    g["l0b_wvT"] = _T32(z["L0_attn_Wv"]); g["l0b_woT"] = _T32(z["L0_attn_Wo"])
    g["l0b_bq"] = _colN(z["L0_attn_bq"], 8); g["l0b_bo"] = _colN(z["L0_attn_bo"], 8)
    g["l0b_bkvrow"] = np.stack([np.asarray(z["L0_attn_bk"]), np.asarray(z["L0_attn_bv"])]).astype(f32)
    kk = np.arange(1024)
    ang = 2 * np.pi * ((np.outer(kk, kk) % 1024) / 1024.0)
    g["dftC"] = np.cos(ang).astype(f32)
    g["dftS"] = np.sin(ang).astype(f32)
    g["w_freq_real"] = np.asarray(z["L1_ffn_w_freq_real"]).astype(f32)
    g["w_freq_imag"] = np.asarray(z["L1_ffn_w_freq_imag"]).astype(f32)
    g["bias_chunks"] = _colN(z["L1_ffn_bias"], 8)
    g["l1b_womT"] = _T32(z["L1_attn_W_omega"])
    g["l1b_bom"] = _colN(z["L1_attn_b_omega"], 8)
    g["l1b_lsc"] = _colN(z["L1_attn_log_scale"], 8)
    g["l1b_lngam"] = _colN(z["L1_attn_ln_gamma"], 32)
    g["l1b_lnbet"] = _colN(z["L1_attn_ln_beta"], 32)
    g["l1b_w1T"] = _T32(z["L1_attn_W1"])
    g["l1b_b1"] = _colN(z["L1_attn_b1"], 16)
    g["l1b_w2T"] = _T32(z["L1_attn_W2"])
    g["l1b_b2"] = _colN(z["L1_attn_b2"], 8)
    g["l2f_wgT"] = _T32(z["L2_ffn_w_gate"]); g["l2f_wuT"] = _T32(z["L2_ffn_w_up"])
    g["l2f_wdT"] = _T32(z["L2_ffn_w_down"])
    g["l2b_wqT"] = _T32(z["L2_attn_Wq"]); g["l2b_wkT"] = _T32(z["L2_attn_Wk"])
    g["l2b_wvT"] = _T32(z["L2_attn_Wv"]); g["l2b_woT"] = _T32(z["L2_attn_Wo"])
    g["l2b_bq"] = _colN(z["L2_attn_bq"], 8); g["l2b_bk"] = _colN(z["L2_attn_bk"], 8)
    g["l2b_bo"] = _colN(z["L2_attn_bo"], 8)
    g["l2b_bvrow"] = np.asarray(z["L2_attn_bv"]).reshape(1, 1024).astype(f32)
    pj = np.asarray(z["L2_attn_proj"])
    g["l2b_projT"] = np.concatenate(
        [np.concatenate([pj[2 * i].T, pj[2 * i + 1].T], axis=0) for i in range(8)], axis=0).astype(f32)
    g["l3_gateT"] = _T32(z["L3_ffn_gate_w"])
    g["l3_ebias"] = np.asarray(z["L3_ffn_expert_bias"]).reshape(1, 8).astype(f32)
    wu3 = np.asarray(z["L3_ffn_w_up"]); wd3 = np.asarray(z["L3_ffn_w_down"])
    g["l3_upcatT"] = np.concatenate([wu3[e].T for e in range(8)] +
                                    [np.asarray(z["L3_ffn_shared_w_up"]).T], axis=1).astype(f32)
    g["l3_dncatT"] = np.concatenate([wd3[e].T for e in range(8)] +
                                    [np.asarray(z["L3_ffn_shared_w_down"]).T], axis=0).astype(f32)
    g["l3b_wq"] = np.asarray(z["L3_attn_Wq"]).astype(f32)
    g["l3b_wk"] = np.asarray(z["L3_attn_Wk"]).astype(f32)
    g["l3b_wv"] = np.asarray(z["L3_attn_Wv"]).astype(f32)
    g["l3b_wo"] = np.asarray(z["L3_attn_Wo"]).astype(f32)
    g["l3b_wg1"] = np.asarray(z["L3_attn_Wg1"]).astype(f32)
    g["l3b_wg2"] = np.asarray(z["L3_attn_Wg2"]).astype(f32)
    g["l3b_gamma"] = _colN(z["L3_attn_gamma"], 8)
    g["l3b_beta"] = _colN(z["L3_attn_beta"], 8)
    H, CH = 16, 128
    def build_slopes(n):
        def po2(m):
            start = 2.0 ** (-2.0 ** (-(np.log2(m) - 3)))
            return [start * start ** i for i in range(m)]
        if np.log2(n).is_integer():
            return po2(n)
        closest = 2 ** int(np.floor(np.log2(n)))
        return po2(closest) + build_slopes(2 * closest)[0::2][:n - closest]
    s3 = (np.array(build_slopes(H), dtype=np.float32) * np.float32(1.0 - 0 + 1e-5)).astype(np.float64)
    idx = np.arange(CH)
    maskT = np.zeros((128, 16 * 128), f32)
    lamqb = np.zeros((128, 8 * 128), f32)
    lamk = np.zeros((128, 16), f32)
    lamc = np.zeros((1, 16), f32)
    for h in range(H):
        m = np.where(idx[:, None] >= idx[None, :], np.exp(-s3[h] * (idx[:, None] - idx[None, :])), 0.0)
        maskT[:, 128 * h:128 * (h + 1)] = m.T
        lamk[:, h] = np.exp(-s3[h] * (CH - 1.0 - idx))
        lamc[0, h] = np.exp(-s3[h] * CH)
        hp, h2 = h // 2, h % 2
        lamqb[64 * h2:64 * (h2 + 1), 128 * hp:128 * (hp + 1)] = np.exp(-s3[h] * (idx + 1.0))[None, :]
    g["l3b_maskT"] = maskT; g["l3b_lamqb"] = lamqb
    g["l3b_lamk"] = lamk; g["l3b_lamc"] = lamc
    g["_l3_slopes"] = s3
    pc = []
    for c in range(N_CORES):
        par = c % 2
        d = {}
        h0 = 8 * par
        hsl = slice(h0, h0 + 8)
        qup = np.zeros((512, 256), f32)
        kup = np.zeros((512, 256), f32)
        qu_ = np.asarray(z["L0_ffn_q_u"]); ku_ = np.asarray(z["L0_ffn_k_u"])
        for hl in range(8):
            h = h0 + hl
            dst = 128 * (hl // 2) + 64 * (hl % 2) + 32
            qup[dst:dst + 32, :] = qu_[64 * h + 32:64 * h + 64, :]
            kup[dst:dst + 32, :] = ku_[64 * h + 32:64 * h + 64, :]
        d["l0_qupermT"] = _T32(qup)
        d["l0_kupermT"] = _T32(kup)
        d["l0_ropeqT_mine"] = _T32(np.asarray(z["L0_ffn_rope_q_w"]).reshape(16, 32, 256)[hsl].reshape(256, 256))
        d["l0_ropekT_mine"] = _T32(np.asarray(z["L0_ffn_rope_k_w"]).reshape(16, 32, 1024)[hsl].reshape(256, 1024))
        d["l0_vuT_mine"] = _T32(np.asarray(z["L0_ffn_v_u"]).reshape(16, 64, 256)[hsl].reshape(512, 256))
        gidx = np.zeros((128, 8), np.int32)
        for ch in range(8):
            gidx[:, ch] = 2 * (128 * ch + np.arange(128)) + par
        d["l0_gidx"] = gidx
        p0 = 2048 * par
        d["l1b_isq"] = (1.0 / np.sqrt(np.arange(p0 + 1, p0 + 2049))).reshape(1, -1).astype(f32)
        d["l1b_ipos"] = (1.0 / np.arange(p0 + 1, p0 + 2049)).reshape(1, -1).astype(f32)
        d["l1b_gamcol"] = np.full((128, 1), float(par), f32)
        clam = np.zeros((16, 2048), f32)
        if par == 1:
            for h in range(16):
                lq = np.exp(-s3[h] * (idx + 1.0))
                for cc in range(16):
                    clam[h, 128 * cc:128 * (cc + 1)] = lq * (np.exp(-s3[h] * CH) ** cc)
        d["l3b_clam"] = clam
        pc.append(d)
    return g, pc


_PROG_CACHE = {}


def _get_prog(name):
    if name not in _PROG_CACHE:
        P = Prog(name)
        if name == "full":
            emit_full(P)
        elif name == "l0a1":
            emit_l0a1(P, n_pair=8)
        elif name == "l0a2":
            emit_l0a2(P)
        elif name == "l0b":
            emit_l0b(P, n_pair=8)
        elif name == "l1f":
            emit_l1f(P)
        elif name == "l1b":
            emit_l1b(P, n_pair=8)
        elif name == "l2f":
            emit_l2f(P)
        elif name == "l2b":
            emit_l2b(P, n_pair=8)
        elif name == "l3f":
            emit_l3f(P)
        elif name == "l3b":
            emit_l3b(P, n_pair=8)
        nc = P.build()
        names = set(P.dram_tensors.keys())
        innames = [n for n, t in P.dram_tensors.items()]
        _PROG_CACHE[name] = (P, nc)
    return _PROG_CACHE[name]


def _run(name, xs, g, pc, extra=None):
    P, nc = _get_prog(name)
    need = [n for n, t in P.dram_tensors.items()]
    ims = []
    for c in range(N_CORES):
        im = {}
        for n in need:
            kind = getattr(P.dram_tensors[n], "kind", None)
            if n == "x_in":
                im[n] = xs[c]
            elif extra and n in extra:
                im[n] = extra[n][c]
            elif n in pc[c]:
                im[n] = pc[c][n]
            elif n in g:
                im[n] = g[n]
        ims.append(im)
    res = run_bass_kernel_spmd(nc, ims, core_ids=list(range(N_CORES)))
    return res.results


FUSED = True


def kernel(**inputs):
    g, pc = _prep(inputs)
    x0 = np.asarray(inputs["L0_attn_query"]).astype(np.float32)
    xs = []
    for c in range(N_CORES):
        sh = x0[c // 2, 2048 * (c % 2):2048 * (c % 2 + 1), :]
        xs.append(np.ascontiguousarray(sh.T))
    if FUSED:
        r = _run("full", xs, g, pc)
        out = np.zeros((4, 4096, 1024), np.float32)
        for c in range(N_CORES):
            out[c // 2, 2048 * (c % 2):2048 * (c % 2 + 1), :] = r[c]["x_out"].T
        return out
    def step_l0f():
        nonlocal xs
        r1 = _run("l0a1", xs, g, pc)
        s1 = {"s1_in": [r1[c]["s1_out"] for c in range(N_CORES)]}
        r2 = _run("l0a2", xs, g, pc, extra=s1)
        xs = [r2[c]["x_out"] for c in range(N_CORES)]
    def step(nm):
        nonlocal xs
        r = _run(nm, xs, g, pc)
        xs = [r[c]["x_out"] for c in range(N_CORES)]
    step_l0f()
    step("l0b")
    step_l0f()
    step("l1f")
    step("l1b")
    step("l1f")
    step("l2f")
    step("l2b")
    step("l2f")
    step("l3f")
    step("l3b")
    step("l3f")
    out = np.zeros((4, 4096, 1024), np.float32)
    for c in range(N_CORES):
        out[c // 2, 2048 * (c % 2):2048 * (c % 2 + 1), :] = xs[c].T
    return out


def emit_full(P, T=THALF):
    xin = P.dram_in("x_in", [D, T])
    xout = P.dram_out("x_out", [D, T])
    xA = P.dram_tmp("xstA", [D, T])
    xB = P.dram_tmp("xstB", [D, T])
    P.std_init()
    P.init_banks()
    identf_init(P)
    steps = ["l0f", "l0b", "l0f", "l1f", "l1b", "l1f",
             "l2f", "l2b", "l2f", "l3f", "l3b", "l3f"]
    cur = xin
    P.defer_finish = True
    slablog = []
    for i, stp in enumerate(steps):
        nxt = xout if i == len(steps) - 1 else (xA if i % 2 == 0 else xB)
        P.push_scope(f"s{i}_")
        if stp == "l0f":
            ds1 = P.dram_tmp("s1sc", [D, T])
            emit_l0a1(P, n_pair=8, d_xin=cur, d_s1=ds1)
            P.barrier()
            emit_l0a2(P, d_xin=cur, d_s1=ds1, d_xout=nxt)
        elif stp == "l0b":
            emit_l0b(P, n_pair=8, d_xin=cur, d_xout=nxt)
        elif stp == "l1f":
            emit_l1f(P, d_xin=cur, d_xout=nxt)
        elif stp == "l1b":
            emit_l1b(P, n_pair=8, d_xin=cur, d_xout=nxt)
        elif stp == "l2f":
            emit_l2f(P, d_xin=cur, d_xout=nxt, quant=(i == 6))
        elif stp == "l2b":
            emit_l2b(P, n_pair=8, d_xin=cur, d_xout=nxt)
        elif stp == "l3f":
            emit_l3f(P, d_xin=cur, d_xout=nxt)
        elif stp == "l3b":
            emit_l3b(P, n_pair=8, d_xin=cur, d_xout=nxt)
        P.pop_scope()
        P.barrier()
        cur = nxt
        live = len(getattr(P, "_viewmap", {}))
        slablog.append((stp, len(P._slabs), live))
        if live:
            print(f"step {i} {stp}: {live} leaked pool tiles")
    P.defer_finish = False
    P.finish()
    print("fused slab log:", slablog)


# version 07 (final)
# speedup vs baseline: 1.0111x; 1.0111x over previous
import math
from contextlib import ExitStack

import numpy as np
import ml_dtypes

import concourse.bass as bass
import concourse.mybir as mybir
from concourse.bass_utils import run_bass_kernel_spmd

F32 = mybir.dt.float32
BF16 = mybir.dt.bfloat16
AF = mybir.ActivationFunctionType
ALU = mybir.AluOpType
AX = mybir.AxisListType

ENGS = ("pe", "dve", "act", "pool", "sp")
BLK = {"pe": "tensor", "dve": "vector", "act": "scalar", "pool": "gpsimd", "sp": "sync"}
QUEUES = {"qsp": "sp", "qact": "act", "qpool": "pool"}
NSLOT = 6
QSLOTS = [f"{q}{i}" for q in QUEUES for i in range(NSLOT)]

N_CORES = 8
D = 1024
TFULL = 4096
THALF = 2048
NCHUNK = 8


class Prog:

    def __init__(self, name="prog"):
        self.nc = bass.Bass("TRN2", target_bir_lowering=False)
        self.es = ExitStack()
        self.name = name
        self.phases = []
        self.cur = {e: [] for e in ENGS}
        allkeys = list(ENGS) + QSLOTS + ["cc"]
        self.n = {k: 0 for k in allkeys}
        self.items = {e: [] for e in ENGS}
        self.clock = {e: {k: -1 for k in allkeys} for e in ENGS}
        self.acc = {}
        self.qrr = {q: 0 for q in QUEUES}
        self.sems = {}
        for k in allkeys:
            self.sems[k] = self.es.enter_context(self.nc.semaphore("sem_" + k))
        self.tagged = {e: [] for e in ENGS}
        self._built = False
        self.sb_tensors = {}
        self.dram_tensors = {}
        self._scratch_ctr = 0

    def push_scope(self, s):
        self.scope = s

    def pop_scope(self):
        self.scope = ""

    def sb(self, name, shape, dtype):
        name = getattr(self, "scope", "") + name
        t = self.es.enter_context(self.nc.sbuf_tensor(name, list(shape), dtype))
        self.sb_tensors[name] = t
        return t

    def psum(self, name, shape, dtype=F32):
        t = self.es.enter_context(self.nc.psum_tensor(name, list(shape), dtype))
        return t

    def dram_in(self, name, shape, dtype=F32):
        if name in self.dram_tensors:
            return self.dram_tensors[name]
        t = self.nc.dram_tensor(name, list(shape), dtype, kind="ExternalInput")
        self.dram_tensors[name] = t
        return t

    def dram_out(self, name, shape, dtype=F32):
        t = self.nc.dram_tensor(name, list(shape), dtype, kind="ExternalOutput")
        self.dram_tensors[name] = t
        return t

    def dram_tmp(self, name, shape, dtype=F32):
        name = getattr(self, "scope", "") + name
        if name in self.dram_tensors:
            return self.dram_tensors[name]
        t = self.nc.dram_tensor(name, list(shape), dtype)
        self.dram_tensors[name] = t
        return t

    def _res(self, ap):
        if ap is None:
            return None
        return ap.name if hasattr(ap, "name") else str(ap)

    def _rec(self, res):
        r = self.acc.get(res)
        if r is None:
            r = {"w": None, "r": {}}
            self.acc[res] = r
        return r

    def _need(self, consumer, key, idx):
        if key == consumer and key == "pe":
            return
        if self.clock[consumer][key] >= idx:
            return
        if key in ENGS:
            it = self.items[key][idx]
            if not it["tag"]:
                it["tag"] = True
                self.tagged[key].append(idx)
        self.cur[consumer].append({"kind": "wait", "key": key, "idx": idx})
        self.clock[consumer][key] = idx

    def _deps_and_record(self, me_key, consumer_eng, reads, writes):
        deps = []
        for ap in reads:
            res = self._res(ap)
            if res is None:
                continue
            rec = self._rec(res)
            if rec["w"] is not None:
                deps.append(rec["w"])
        for ap in writes:
            res = self._res(ap)
            if res is None:
                continue
            rec = self._rec(res)
            if rec["w"] is not None:
                deps.append(rec["w"])
            deps.extend(rec["r"].items())
        best = {}
        for k, idx in deps:
            best[k] = max(best.get(k, -1), idx)
        for k, idx in sorted(best.items()):
            self._need(consumer_eng, k, idx)
        idx_me = self.n[me_key]
        for ap in reads:
            res = self._res(ap)
            if res is not None:
                self._rec(res)["r"][me_key] = idx_me
        for ap in writes:
            res = self._res(ap)
            if res is not None:
                rec = self._rec(res)
                rec["w"] = (me_key, idx_me)
                rec["r"] = {}
        return idx_me

    def op(self, eng, fn, reads=(), writes=()):
        assert eng in ENGS
        self._deps_and_record(eng, eng, list(reads), list(writes))
        item = {"kind": "op", "fn": fn, "tag": False, "eng": eng}
        self.cur[eng].append(item)
        self.items[eng].append(item)
        self.n[eng] += 1

    def dma(self, q, out_ap, in_ap, **kw):
        assert q in QUEUES
        eng = QUEUES[q]
        slot = f"{q}{self.qrr[q]}"
        self.qrr[q] = (self.qrr[q] + 1) % NSLOT
        if self.n[slot] > 0:
            self._need(eng, slot, self.n[slot] - 1)
        self._deps_and_record(slot, eng, [in_ap], [out_ap])
        sem = self.sems[slot]
        fn = lambda e, o=out_ap, i=in_ap, kw=kw: e.dma_start(out=o, in_=i, **kw).then_inc(sem, 16)
        item = {"kind": "op", "fn": fn, "tag": False, "eng": eng}
        self.cur[eng].append(item)
        self.n[slot] += 1

    def idma_gather(self, out_ap, in_view, idx_ap):
        q = "qpool"
        eng = "pool"
        slot = f"{q}{self.qrr[q]}"
        self.qrr[q] = (self.qrr[q] + 1) % NSLOT
        if self.n[slot] > 0:
            self._need(eng, slot, self.n[slot] - 1)
        self._deps_and_record(slot, eng, [in_view, idx_ap], [out_ap])
        sem = self.sems[slot]
        fn = lambda e, o=out_ap, i=in_view, ix=idx_ap: \
            e.indirect_dma_start(o, None, i, bass.IndirectOffsetOnAxis(ix, 0)).then_inc(sem, 16)
        item = {"kind": "op", "fn": fn, "tag": False, "eng": eng}
        self.cur[eng].append(item)
        self.n[slot] += 1

    def cc(self, kind, op, groups, ins, outs):
        eng = "pool"
        self._deps_and_record("cc", eng, list(ins), list(outs))
        sem = self.sems["cc"]
        fn = lambda e, k=kind, o=op, g=groups, i=list(ins), u=list(outs): \
            e.collective_compute(k, o, g, ins=[a.opt() for a in i], outs=[a.opt() for a in u]).then_inc(sem, 1)
        item = {"kind": "op", "fn": fn, "tag": False, "eng": eng}
        self.cur[eng].append(item)
        self.n["cc"] += 1

    def barrier(self):
        if any(self.cur[e] for e in ENGS):
            self.phases.append(self.cur)
            self.cur = {e: [] for e in ENGS}
        for e in ENGS:
            for f in ENGS:
                self.clock[e][f] = self.n[f] - 1

    def mm(self, bank, lhsT, rhs, start=True, stop=True, tile_position=None):
        self.op("pe", lambda t, b=bank, l=lhsT, r=rhs, s=start, p=stop, tp=tile_position:
                t.matmul(b, l, r, start=s, stop=p, tile_position=tp),
                reads=[lhsT, rhs], writes=[bank])

    def transp(self, bank, in_, ident):
        self.op("pe", lambda t, b=bank, i=in_, d=ident: t.transpose(b, i, d),
                reads=[in_, ident], writes=[bank])

    def tt(self, out, a, b, aluop, eng="dve"):
        self.op(eng, lambda v, o=out, x=a, y=b, p=aluop: v.tensor_tensor(o, x, y, p),
                reads=[a, b], writes=[out])

    def ts(self, out, a, s1, s2, op0, op1=None, eng="dve", accum_out=None):
        rd = [a] + ([s1] if hasattr(s1, "name") else []) + ([s2] if hasattr(s2, "name") else [])
        wr = [out] + ([accum_out] if accum_out is not None else [])
        if op1 is None:
            self.op(eng, lambda v, o=out, x=a, u=s1, p=op0, ac=accum_out:
                    v.tensor_scalar(o, x, u, None, p, accum_out=ac), reads=rd, writes=wr)
        else:
            self.op(eng, lambda v, o=out, x=a, u=s1, w=s2, p=op0, q=op1, ac=accum_out:
                    v.tensor_scalar(o, x, u, w, p, q, accum_out=ac), reads=rd, writes=wr)

    def stt(self, out, in0, scalar, in1, op0, op1, eng="dve"):
        rd = [in0, in1] + ([scalar] if hasattr(scalar, "name") else [])
        self.op(eng, lambda v, o=out, a=in0, s=scalar, b=in1, p=op0, q=op1:
                v.scalar_tensor_tensor(o, a, s, b, p, q), reads=rd, writes=[out])

    def act(self, out, in_, func, bias=0.0, scale=1.0, accum_out=None, eng="act"):
        rd = [in_] + ([bias] if hasattr(bias, "name") else []) + ([scale] if hasattr(scale, "name") else [])
        wr = [out] + ([accum_out] if accum_out is not None else [])
        self.op(eng, lambda s, o=out, i=in_, f=func, b=bias, c=scale, ac=accum_out:
                s.activation(o, i, f, bias=b, scale=c, accum_out=ac), reads=rd, writes=wr)

    def red(self, out, in_, aluop=ALU.max, negate=False, eng="dve"):
        self.op(eng, lambda v, o=out, i=in_, p=aluop, n=negate:
                v.tensor_reduce(o, i, AX.X, p, negate=n), reads=[in_], writes=[out])

    def scan(self, out, d0, d1, init, op0, op1):
        self.op("dve", lambda v, o=out, a=d0, b=d1, i=init, p=op0, q=op1:
                v.tensor_tensor_scan(o, a, b, i, p, q), reads=[d0, d1], writes=[out])

    def copy(self, out, in_, eng="dve"):
        if eng == "act":
            self.act(out, in_, AF.Copy)
        else:
            self.op(eng, lambda v, o=out, i=in_: v.tensor_copy(o, i), reads=[in_], writes=[out])

    def memset(self, ap, c, eng="pool"):
        self.op(eng, lambda g, a=ap, x=c: g.memset(a, x), reads=[], writes=[ap])

    def iota(self, ap, pattern, base=0, channel_multiplier=0):
        self.op("pool", lambda g, a=ap, p=pattern, b=base, c=channel_multiplier:
                g.iota(a, p, base=b, channel_multiplier=c, allow_small_or_imprecise_dtypes=True),
                reads=[], writes=[ap])

    def affsel(self, out, in_, pattern, cmp, fill, base=0, channel_multiplier=0):
        self.op("pool", lambda g, o=out, i=in_, p=pattern, c=cmp, f=fill, b=base, m=channel_multiplier:
                g.affine_select(o, i, p, c, f, base=b, channel_multiplier=m),
                reads=[in_], writes=[out])

    def std_init(self):
        if hasattr(self, "ones1"):
            return
        sc = getattr(self, "scope", "")
        self.scope = ""
        self.ones1 = self.sb("std_ones1", [1, 128], F32)
        self.ones1b = self.sb("std_ones1b", [1, 128], BF16)
        self.onec = self.sb("std_onec", [128, 1], F32)
        self.ones128 = self.sb("std_ones128", [128, 1], BF16)
        self.identb = self.sb("std_identb", [128, 128], BF16)
        self.memset(self.ones1[:, :], 1.0)
        self.memset(self.ones1b[:, :], 1.0)
        self.memset(self.onec[:, :], 1.0)
        self.memset(self.ones128[:, :], 1.0)
        self.memset(self.identb[:, :], 1.0)
        self.affsel(self.identb[:, :], self.identb[:, :], [[1, 128]], ALU.is_equal,
                    0.0, base=0, channel_multiplier=-1)
        self.scope = sc

    def init_banks(self):
        if hasattr(self, "bks"):
            return
        self.bks = [self.psum(f"bank{i}", [128, 512], F32) for i in range(6)]
        self.tb = [self.psum(f"tbank{i}", [128, 1024], BF16) for i in range(2)]

    def _units(self, shape, dtype):
        p, c = shape
        bytes_per_part = c * (2 if dtype == BF16 else 4)
        u = max(1, -(-bytes_per_part // 2048))
        return 1 if u == 1 else (2 if u == 2 else 4)

    def acq(self, n, shape, dtype):
        if not hasattr(self, "_free"):
            self._free = {1: [], 2: [], 4: []}
            self._slabs = []
            self._viewmap = {}
        out = []
        for _ in range(n):
            u = self._units(shape, dtype)
            blk = self._take(u)
            slab, off = blk
            ap = self._slabs[slab][:, 512 * off: 512 * (off + u)]
            if dtype == BF16:
                ap = ap.bitcast(BF16)
            ap = ap[0:shape[0], 0:shape[1]]
            self._viewmap[id(ap)] = (blk, u)
            out.append(ap)
        return out

    def _take(self, u):
        if self._free[u]:
            return self._free[u].pop()
        if u < 4:
            big = self._take(2 * u)
            slab, off = big
            self._free[u].append((slab, off + u))
            return (slab, off)
        t = self.sb(f"slab{len(self._slabs)}", [128, 2048], F32)
        self._slabs.append(t)
        return (len(self._slabs) - 1, 0)

    def acq1(self, shape, dtype):
        return self.acq(1, shape, dtype)[0]

    def rel(self, tiles):
        if not isinstance(tiles, (list, tuple)):
            tiles = [tiles]
        for t in tiles:
            assert id(t) in self._viewmap, f"rel of unknown/already-released tile {t}"
            blk, u = self._viewmap.pop(id(t))
            self._free_blk(blk, u)

    def _free_blk(self, blk, u):
        if u < 4:
            slab, off = blk
            buddy = (slab, off ^ u)
            if buddy in self._free[u]:
                self._free[u].remove(buddy)
                self._free_blk((slab, min(off, off ^ u)), 2 * u)
                return
        self._free[u].append(blk)

    def constcol(self, value, dtype=F32):
        key = (float(value), dtype)
        if not hasattr(self, "_ccols"):
            self._ccols = {}
        if key not in self._ccols:
            t = self.sb(f"cc_{len(self._ccols)}", [128, 1], dtype)
            self.memset(t[:, :], float(value))
            self._ccols[key] = t
        return self._ccols[key][:, 0:1]

    def bcast_row(self, bank, row_ap, nrows=128, scale=None):
        self.mm(bank, self.ones1[0:1, 0:nrows], row_ap)

    def finish(self):
        if getattr(self, "defer_finish", False):
            self.barrier()
            return
        self.barrier()
        for q in QSLOTS:
            if self.n[q] > 0:
                cnt = self.n[q]
                self.cur["sp"].append({"kind": "rawwait", "key": q, "val": 16 * cnt})
        if self.n["cc"] > 0:
            self.cur["sp"].append({"kind": "rawwait", "key": "cc", "val": self.n["cc"]})
        self.phases.append(self.cur)
        self.cur = {e: [] for e in ENGS}

    def _resolve(self, it):
        key = it["key"]
        if it["kind"] == "rawwait":
            return self.sems[key], it["val"]
        idx = it["idx"]
        if key in ENGS:
            tg = self.tagged_sorted[key]
            import bisect
            v = bisect.bisect_right(tg, idx)
            assert v > 0
            return self.sems[key], v
        elif key != "cc":
            return self.sems[key], 16 * (idx + 1)
        else:
            return self.sems[key], idx + 1

    def build(self):
        assert not self._built
        self._built = True
        self.tagged_sorted = {e: sorted(self.tagged[e]) for e in ENGS}
        for ph in self.phases:
            with self.nc.Block() as block:
                for ename in ENGS:
                    items = ph[ename]
                    if not items:
                        continue
                    def replay(engobj, items=items, ename=ename):
                        for it in items:
                            if it["kind"] in ("wait", "rawwait"):
                                sem, v = self._resolve(it)
                                engobj.wait_ge(sem, v)
                            else:
                                ins = it["fn"](engobj)
                                if it["tag"]:
                                    ins.then_inc(self.sems[it["eng"]], 1)
                    getattr(block, BLK[ename])(replay)
        return self.nc



def load_wt(P, dram, name, kchunks, cols, dtype=BF16, q="qpool", rowoff=0):
    tiles = P.acq(kchunks, (128, cols), dtype)
    for k in range(kchunks):
        P.dma(q, tiles[k][:, :], dram.ap()[rowoff + 128 * k: rowoff + 128 * (k + 1), :])
    return tiles


def sb_tiles(P, name, n, shape, dtype):
    return [P.sb(f"{name}_{i}", list(shape), dtype) for i in range(n)]


def emit_load_x(P, x, xdram, T):
    for c in range(8):
        P.dma("qsp", x[c][:, :], xdram.ap()[128 * c:128 * (c + 1), :])


def emit_store_x(P, x, xdram, T):
    for c in range(8):
        P.dma("qsp", xdram.ap()[128 * c:128 * (c + 1), :], x[c][:, :])


def emit_ln(P, x, xout, T, pre="ln", eps=1e-5, xout32=None, gcol=None, bcol=None, keep_rows=False):
    sq = P.acq(2, (128, 512), F32)
    if keep_rows:
        mu = P.acq1((1, T), F32)
        rs = P.acq1((1, T), F32)
    else:
        mu_t = P.acq1((128, 512), F32)
        rs_t = P.acq1((128, 512), F32)
        mu, rs = mu_t, rs_t
    tmp = P.acq1((128, 512), F32)
    b_s, b_q, b_m, b_r = P.bks[0], P.bks[1], P.bks[2], P.bks[3]
    onec = P.onec
    for g in range(T // 512):
        sl = slice(512 * g, 512 * (g + 1))
        msl = sl if keep_rows else slice(0, 512)
        for c in range(8):
            P.mm(b_s[0:1, 0:512], onec[:, 0:1], x[c][:, sl], start=(c == 0), stop=(c == 7))
            P.act(sq[c % 2][:, :], x[c][:, sl], AF.Square)
            P.mm(b_q[0:1, 0:512], onec[:, 0:1], sq[c % 2][:, :], start=(c == 0), stop=(c == 7))
        P.ts(mu[0:1, msl], b_s[0:1, 0:512], 1.0 / 1024.0, None, ALU.mult)
        P.ts(rs[0:1, msl], b_q[0:1, 0:512], 1.0 / 1024.0, None, ALU.mult)
        P.tt(tmp[0:1, 0:512], mu[0:1, msl], mu[0:1, msl], ALU.mult)
        P.tt(rs[0:1, msl], rs[0:1, msl], tmp[0:1, 0:512], ALU.subtract)
        P.ts(rs[0:1, msl], rs[0:1, msl], eps, None, ALU.add)
        P.act(rs[0:1, msl], rs[0:1, msl], AF.Sqrt)
        P.op("dve", lambda v, o=rs[0:1, msl], i=rs[0:1, msl]: v.reciprocal(o, i),
             reads=[rs], writes=[rs])
        P.mm(b_m[:, 0:512], P.ones1[0:1, :], mu[0:1, msl])
        P.mm(b_r[:, 0:512], P.ones1[0:1, :], rs[0:1, msl])
        for c in range(8):
            P.tt(tmp[:, 0:512], x[c][:, sl], b_m[:, 0:512], ALU.subtract)
            if gcol is None:
                P.tt(xout[c][:, sl], tmp[:, 0:512], b_r[:, 0:512], ALU.mult)
                if xout32 is not None:
                    P.copy(xout32[c][:, sl], xout[c][:, sl], eng="act")
            else:
                P.tt(tmp[:, 0:512], tmp[:, 0:512], b_r[:, 0:512], ALU.mult)
                if bcol is None:
                    P.ts(xout[c][:, sl], tmp[:, 0:512], gcol[:, c:c + 1], None, ALU.mult)
                else:
                    P.ts(xout[c][:, sl], tmp[:, 0:512], gcol[:, c:c + 1], bcol[:, c:c + 1],
                         ALU.mult, ALU.add)
    P.rel(sq + [tmp])
    if not keep_rows:
        P.rel([mu_t, rs_t])
        return None, None
    return mu, rs


def gemm_to_dram(P, name, terms, out_dram, Mch, N, scale=1.0, tmp=None):
    own_tmp = tmp is None
    if own_tmp:
        tmp = P.acq1((128, N), BF16)
    for m in range(Mch):
        for g in range(N // 512):
            sl = slice(512 * g, 512 * (g + 1))
            bank = P.bks[4 + (g % 2)]
            first = True
            for ti, (lt, rt) in enumerate(terms):
                for k in range(len(lt)):
                    P.mm(bank[:, 0:512], lt[k][:, 128 * m:128 * (m + 1)], rt[k][:, sl],
                         start=first, stop=(k == len(lt) - 1 and ti == len(terms) - 1))
                    first = False
            if scale == 1.0:
                P.copy(tmp[:, sl], bank[:, 0:512], eng="act")
            else:
                P.act(tmp[:, sl], bank[:, 0:512], AF.Copy, scale=P.constcol(scale))
        P.dma("qsp", out_dram.ap()[128 * m:128 * (m + 1), :], tmp[:, 0:N])
        P.barrier()
    if own_tmp:
        P.rel(tmp)



def emit_l1f(P, T=THALF, d_xin=None, d_xout=None):
    d_xin = d_xin or P.dram_in("x_in", [D, T])
    d_xout = d_xout or P.dram_out("x_out", [D, T])
    dC = P.dram_in("dftC", [D, D])
    dS = P.dram_in("dftS", [D, D])
    dWR = P.dram_in("w_freq_real", [D, D])
    dWI = P.dram_in("w_freq_imag", [D, D])
    dbias = P.dram_in("bias_chunks", [128, 8])
    dB1 = P.dram_tmp("scrB1", [D, D], BF16)
    dB2 = P.dram_tmp("scrB2", [D, D], BF16)
    dM = P.dram_tmp("scrM", [D, D], BF16)

    P.std_init()
    P.init_banks()
    x = P.acq(8, (128, T), F32)
    emit_load_x(P, x, d_xin, T)
    xln = P.acq(8, (128, T), BF16)
    emit_ln(P, x, xln, T)
    P.barrier()
    tA = load_wt(P, dC, "tA", 8, D)
    tB = load_wt(P, dS, "tB", 8, D)
    tC_ = load_wt(P, dWR, "tC", 8, D)
    tD_ = load_wt(P, dWI, "tD", 8, D)
    P.barrier()
    gemm_to_dram(P, "gB1", [(tC_, tA), (tD_, tB)], dB1, 8, D)
    for k in range(8):
        P.act(tB[k][:, :], tB[k][:, :], AF.Copy, scale=P.constcol(-1.0))
    gemm_to_dram(P, "gB2", [(tD_, tA), (tC_, tB)], dB2, 8, D)
    P.barrier()
    for k in range(8):
        P.dma("qpool", tC_[k][:, :], dB1.ap()[128 * k:128 * (k + 1), :])
        P.dma("qpool", tD_[k][:, :], dB2.ap()[128 * k:128 * (k + 1), :])
    gemm_to_dram(P, "gM", [(tC_, tA), (tD_, tB)], dM, 8, D, scale=1.0 / 1024.0)
    P.barrier()
    for k in range(8):
        P.dma("qpool", tA[k][:, :], dM.ap()[128 * k:128 * (k + 1), :])
    bcol = P.sb("bcol", [128, 8], F32)
    P.dma("qsp", bcol[:, :], dbias.ap())
    hb = P.sb("hb", [128, 8], F32)
    P.act(hb[:, :], bcol[:, :], AF.Copy, scale=P.constcol(0.5))
    P.barrier()
    scr = P.acq(2, (128, 512), F32)
    for oc in range(8):
        for k in range(8):
            for g in range(T // 512):
                bank = P.bks[g % 4]
                P.mm(bank[:, 0:512], tA[k][:, 128 * oc:128 * (oc + 1)],
                     xln[k][:, 512 * g:512 * (g + 1)], start=(k == 0), stop=(k == 7))
        for g in range(T // 512):
            sl = slice(512 * g, 512 * (g + 1))
            bank = P.bks[g % 4]
            s = scr[g % 2]
            P.ts(s[:, :], bank[:, 0:512], 0.5, hb[:, oc:oc + 1], ALU.mult, ALU.add)
            P.tt(x[oc][:, sl], x[oc][:, sl], s[:, :], ALU.add)
    P.barrier()
    emit_store_x(P, x, d_xout, T)
    P.rel(x + xln + tA + tB + tC_ + tD_ + scr)
    P.finish()


def bcast_to_sbuf(P, row_ap, out_tile, W):
    for g in range((W + 511) // 512):
        sl = slice(512 * g, min(512 * (g + 1), W))
        n = sl.stop - sl.start
        bank = P.bks[4 + (g % 2)]
        P.mm(bank[:, 0:n], P.ones1[0:1, :], row_ap[0:1, sl])
        P.copy(out_tile[:, sl], bank[:, 0:n], eng="act")


def identf_init(P):
    if hasattr(P, "identf"):
        return
    sc = getattr(P, "scope", "")
    P.scope = ""
    P.identf = P.sb("std_identf", [128, 128], F32)
    P.memset(P.identf[:, :], 1.0)
    P.affsel(P.identf[:, :], P.identf[:, :], [[1, 128]], ALU.is_equal,
             0.0, base=0, channel_multiplier=-1)
    P.scope = sc



def emit_l3f(P, T=THALF, d_xin=None, d_xout=None):
    HID = 1408
    NE = 8
    d_xin = d_xin or P.dram_in("x_in", [D, T])
    d_xout = d_xout or P.dram_out("x_out", [D, T])
    d_gateT = P.dram_in("l3_gateT", [D, NE])
    d_ebias = P.dram_in("l3_ebias", [1, NE])
    d_upT = P.dram_in("l3_upcatT", [D, 9 * HID])
    d_dnT = P.dram_in("l3_dncatT", [9 * HID, D])
    d_wscr = P.dram_tmp("wscr", [128, 128])

    P.std_init()
    P.init_banks()
    identf_init(P)
    x = P.acq(8, (128, T), F32)
    emit_load_x(P, x, d_xin, T)
    xlnb = P.acq(8, (128, T), BF16)
    mu, rs = emit_ln(P, x, xlnb, T, keep_rows=True)
    P.barrier()
    gT = load_wt(P, d_gateT, "gT", 8, NE, dtype=F32, q="qsp")
    ebr = P.sb("ebr", [1, NE], F32)
    P.dma("qsp", ebr[:, :], d_ebias.ap())
    eb = P.sb("eb", [128, NE], F32)
    bank = P.bks[5]
    P.mm(bank[:, 0:NE], P.ones1[0:1, :], ebr[0:1, :])
    P.copy(eb[:, :], bank[:, 0:NE], eng="act")
    Brow = P.sb("Brow", [1, NE], F32)
    for k in range(8):
        P.mm(bank[0:1, 0:NE], P.onec[:, 0:1], gT[k][:, :], start=(k == 0), stop=(k == 7))
    P.copy(Brow[0:1, :], bank[0:1, 0:NE])
    wbuf = P.sb("wbuf", [128, 128], F32)
    sc = P.acq1((128, 512), F32)
    t1 = P.acq1((128, 512), F32)
    t2 = P.acq1((128, 512), F32)
    col = P.acq1((128, 1), F32)
    for ti in range(T // 128):
        tsl = slice(128 * ti, 128 * (ti + 1))
        b = P.bks[ti % 2]
        for k in range(8):
            P.mm(b[:, 0:NE], x[k][:, tsl], gT[k][:, :],
                 start=(k == 0), stop=(k == 7))
        bmb = P.bks[2 + (ti % 2)]
        P.mm(bmb[:, 0:NE], mu[0:1, tsl], Brow[0:1, :])
        P.mm(P.bks[4][:, 0:1], rs[0:1, tsl], P.ones1[0:1, 0:1])
        P.copy(col[:, 0:1], P.bks[4][:, 0:1])
        P.copy(sc[:, 0:NE], b[:, 0:NE], eng="act")
        P.tt(sc[:, 0:NE], sc[:, 0:NE], bmb[:, 0:NE], ALU.subtract)
        P.ts(sc[:, 0:NE], sc[:, 0:NE], col[:, 0:1], None, ALU.mult)
        P.act(sc[:, 0:NE], sc[:, 0:NE], AF.Sigmoid)
        P.tt(t1[:, 0:NE], sc[:, 0:NE], eb[:, 0:NE], ALU.add)
        P.red(col[:, 0:1], t1[:, 0:NE], ALU.max)
        P.ts(t2[:, 0:NE], t1[:, 0:NE], col[:, 0:1], None, ALU.is_ge)
        P.ts(t2[:, 8:8 + NE], t2[:, 0:NE], -1e30, None, ALU.mult)
        P.tt(t1[:, 0:NE], t1[:, 0:NE], t2[:, 8:8 + NE], ALU.add)
        P.red(col[:, 0:1], t1[:, 0:NE], ALU.max)
        P.ts(t1[:, 0:NE], t1[:, 0:NE], col[:, 0:1], None, ALU.is_ge)
        P.tt(t2[:, 0:NE], t2[:, 0:NE], t1[:, 0:NE], ALU.add)
        P.tt(t2[:, 0:NE], t2[:, 0:NE], sc[:, 0:NE], ALU.mult)
        P.tt(wbuf[:, 8 * ti:8 * ti + NE], t2[:, 0:NE], t2[:, 0:NE], ALU.mult)
    P.rel([sc, t1, t2, col])
    P.rel([mu, rs])
    P.rel(gT)
    wrowb = P.acq1((1, T), F32)
    bank = P.bks[4]
    P.transp(bank[:, 0:128], wbuf[:, :], P.identf[:, :])
    wtr = P.acq1((128, 512), F32)
    P.copy(wtr[:, 0:128], bank[:, 0:128])
    P.dma("qsp", d_wscr.ap(), wtr[:, 0:128])
    P.rel(wtr)
    P.barrier()
    v = d_wscr.ap().rearrange("(a e) p -> e a p", e=NE)
    wsqb = P.acq1((128, T), F32)
    hbuf = [P.acq(4, (128, T), BF16), P.acq(4, (128, T), BF16)]
    upb = [P.acq(8, (128, 512), BF16), P.acq(8, (128, 512), BF16)]
    dnb = [P.acq(4, (128, 1024), BF16), P.acq(4, (128, 1024), BF16)]
    scr = P.acq(2, (128, 512), F32)
    BLKS = [(0, 512), (512, 1024), (1024, 1408)]
    DGRP = [(0, 4), (4, 8), (8, 11)]

    def load_up(buf, e, blk):
        c0, c1 = BLKS[blk]
        for k in range(8):
            P.dma("qpool", buf[k][:, 0:c1 - c0],
                  d_upT.ap()[128 * k:128 * (k + 1), HID * e + c0:HID * e + c1])

    def load_dn(buf, e, gi):
        i0, i1 = DGRP[gi]
        for j in range(i1 - i0):
            gx = 11 * e + i0 + j
            P.dma("qpool", buf[j][:, :], d_dnT.ap()[128 * gx:128 * (gx + 1), :])

    up_all = [(e, blk) for e in range(9) for blk in range(3)]
    dn_all = [(e, gi) for e in range(9) for gi in range(3)]
    load_up(upb[0], *up_all[0])
    load_dn(dnb[0], *dn_all[0])
    up_i = dn_i = 0
    for e in range(9):
        if e < NE:
            P.dma("qsp", wrowb[0:1, :].rearrange("x (a p) -> x a p", a=T // 128),
                  v[e:e + 1, :, :])
            bcast_to_sbuf(P, wrowb[0:1, :], wsqb, T)
        for blk, (c0, c1) in enumerate(BLKS):
            w = c1 - c0
            nic = (w + 127) // 128
            hb = hbuf[up_i % 2]
            up_sl = upb[up_i % 2]
            if up_i + 1 < len(up_all):
                load_up(upb[(up_i + 1) % 2], *up_all[up_i + 1])
            up_i += 1
            for ic in range(nic):
                for k in range(8):
                    for g in range(T // 512):
                        b = P.bks[g % 4]
                        P.mm(b[:, 0:512], up_sl[k][:, 128 * ic:128 * (ic + 1)],
                             xlnb[k][:, 512 * g:512 * (g + 1)], start=(k == 0), stop=(k == 7))
                for g in range(T // 512):
                    sl = slice(512 * g, 512 * (g + 1))
                    b = P.bks[g % 4]
                    s = scr[g % 2]
                    P.ts(s[:, :], b[:, 0:512], 0.0, None, ALU.max)
                    P.act(s[:, :], s[:, :], AF.Square)
                    if e < NE:
                        P.tt(hb[ic][:, sl], s[:, :], wsqb[:, sl], ALU.mult)
                    else:
                        P.copy(hb[ic][:, sl], s[:, :])
            dn4 = dnb[dn_i % 2]
            if dn_i + 1 < len(dn_all):
                load_dn(dnb[(dn_i + 1) % 2], *dn_all[dn_i + 1])
            dn_i += 1
            for oc in range(8):
                for j in range(nic):
                    for g in range(T // 512):
                        b = P.bks[g % 4]
                        P.mm(b[:, 0:512], dn4[j][:, 128 * oc:128 * (oc + 1)],
                             hb[j][:, 512 * g:512 * (g + 1)],
                             start=(j == 0), stop=(j == nic - 1))
                for g in range(T // 512):
                    sl = slice(512 * g, 512 * (g + 1))
                    P.stt(x[oc][:, sl], P.bks[g % 4][:, 0:512], 0.5, x[oc][:, sl],
                          ALU.mult, ALU.add)
    assert up_i == len(up_all) and dn_i == len(dn_all)
    P.rel([wsqb, wrowb] + scr + upb[0] + upb[1] + dnb[0] + dnb[1] + hbuf[0] + hbuf[1])
    P.barrier()
    emit_store_x(P, x, d_xout, T)
    P.rel(x + xlnb)
    P.finish()


PAIRS = [[0, 1], [2, 3], [4, 5], [6, 7]]


def proj_fm(P, wT, xb, T, evac, ocs=range(8), bk0=0):
    ng = T // 512
    for i, oc in enumerate(ocs):
        for g0 in range(0, ng, 4):
            gs = range(g0, min(g0 + 4, ng))
            for k in range(8):
                for g in gs:
                    b = P.bks[bk0 + (g % 4)]
                    P.mm(b[:, 0:512], wT[k][:, 128 * oc:128 * (oc + 1)],
                         xb[k][:, 512 * g:512 * (g + 1)], start=(k == 0), stop=(k == 7))
            for g in gs:
                evac(oc, g, P.bks[bk0 + (g % 4)])


def proj_tm(P, wT, xb, T, NO, evac, bk0=0):
    ng = NO // 512
    for ti in range(T // 128):
        boff = bk0 + ((ti % 2) * ng if ng <= 2 else 0)
        for k in range(8):
            for og in range(ng):
                b = P.bks[boff + (og % 4)]
                P.mm(b[:, 0:512], xb[k][:, 128 * ti:128 * (ti + 1)],
                     wT[k][:, 512 * og:512 * (og + 1)], start=(k == 0), stop=(k == 7))
        for og in range(ng):
            evac(ti, og, P.bks[boff + (og % 4)])


def elu1_evac(P, dst_ap, src_ap, s1, s2, bias_col=None, bias_tile=None):
    if bias_col is not None:
        P.ts(s1[:, :], src_ap, bias_col, None, ALU.add)
    else:
        P.tt(s1[:, :], src_ap, bias_tile, ALU.add)
    P.ts(s2[:, :], s1[:, :], 0.0, None, ALU.min)
    P.act(s2[:, :], s2[:, :], AF.Exp)
    P.ts(s1[:, :], s1[:, :], 0.0, None, ALU.max)
    P.tt(dst_ap, s1[:, :], s2[:, :], ALU.add)



def emit_l0b(P, T=THALF, n_pair=2, d_xin=None, d_xout=None):
    d_xin = d_xin or P.dram_in("x_in", [D, T])
    d_xout = d_xout or P.dram_out("x_out", [D, T])
    wqT = P.dram_in("l0b_wqT", [D, D])
    wkT = P.dram_in("l0b_wkT", [D, D])
    wvT = P.dram_in("l0b_wvT", [D, D])
    woT = P.dram_in("l0b_woT", [D, D])
    d_bq = P.dram_in("l0b_bq", [128, 8])
    d_bo = P.dram_in("l0b_bo", [128, 8])
    d_bkv = P.dram_in("l0b_bkvrow", [2, D])
    b_kv1 = P.dram_tmp("b_kv1", [128, 17 * 64])
    b_kv2 = P.dram_tmp("b_kv2", [128, 17 * 64])

    P.std_init()
    P.init_banks()
    x = P.acq(8, (128, T), F32)
    emit_load_x(P, x, d_xin, T)
    xln = P.acq(8, (128, T), BF16)
    emit_ln(P, x, xln, T)
    P.rel(x)
    P.barrier()
    bqc = P.sb("bqc", [128, 8], F32)
    boc = P.sb("boc", [128, 8], F32)
    P.dma("qsp", bqc[:, :], d_bq.ap())
    P.dma("qsp", boc[:, :], d_bo.ap())
    bkr = P.acq1((1, D), F32)
    bvr = P.acq1((1, D), F32)
    P.dma("qsp", bkr[0:1, :], d_bkv.ap()[0:1, :])
    P.dma("qsp", bvr[0:1, :], d_bkv.ap()[1:2, :])
    bkt = P.acq1((128, 1024), F32)
    bvt = P.acq1((128, 1024), F32)
    bcast_to_sbuf(P, bkr[0:1, :], bkt, D)
    bcast_to_sbuf(P, bvr[0:1, :], bvt, D)
    s1 = P.acq1((128, 512), F32)
    s2 = P.acq1((128, 512), F32)
    wt = load_wt(P, wqT, "wq", 8, D)
    qp = P.acq(8, (128, T), BF16)
    proj_fm(P, wt, xln, T, lambda oc, g, b:
            elu1_evac(P, qp[oc][:, 512 * g:512 * (g + 1)], b[:, 0:512], s1, s2,
                      bias_col=bqc[:, oc:oc + 1]))
    P.barrier()
    for k in range(8):
        P.dma("qpool", wt[k][:, :], wkT.ap()[128 * k:128 * (k + 1), :])
    kp = P.acq(T // 128, (128, 1024), BF16)
    def kevac(ti, og, b):
        sl = slice(512 * og, 512 * (og + 1))
        elu1_evac(P, kp[ti][:, sl], b[:, 0:512], s1, s2, bias_tile=bkt[:, sl])
    proj_tm(P, wt, xln, T, D, kevac)
    P.barrier()
    for k in range(8):
        P.dma("qpool", wt[k][:, :], wvT.ap()[128 * k:128 * (k + 1), :])
    vt = P.acq(T // 128, (128, 1024), BF16)
    def vevac(ti, og, b):
        sl = slice(512 * og, 512 * (og + 1))
        P.tt(vt[ti][:, sl], b[:, 0:512], bvt[:, sl], ALU.add)
    proj_tm(P, wt, xln, T, D, vevac)
    P.rel(xln)
    P.barrier()
    kvs = P.acq1((128, 1088), F32)
    for hp in range(8):
        bkv = P.bks[hp % 4]
        bks_ = P.bks[4]
        for h2 in range(2):
            h = 2 * hp + h2
            hs = slice(64 * h, 64 * (h + 1))
            tp = (0, 64 * h2)
            for ti in range(T // 128):
                P.mm(bkv[64 * h2:64 * (h2 + 1), 0:64], kp[ti][:, hs], vt[ti][:, hs],
                     start=(ti == 0), stop=(ti == T // 128 - 1), tile_position=tp)
                P.mm(bks_[64 * h2:64 * (h2 + 1), hp:hp + 1], kp[ti][:, hs], P.ones128[:, 0:1],
                     start=(ti == 0), stop=(ti == T // 128 - 1), tile_position=tp)
            P.copy(kvs[64 * h2:64 * (h2 + 1), 136 * hp:136 * hp + 64], bkv[64 * h2:64 * (h2 + 1), 0:64])
            P.copy(kvs[64 * h2:64 * (h2 + 1), 136 * hp + 128 + h2:136 * hp + 129 + h2],
                   bks_[64 * h2:64 * (h2 + 1), hp:hp + 1])
    P.barrier()
    P.dma("qpool", b_kv1.ap(), kvs[:, :])
    P.cc("AllReduce", ALU.add, PAIRS[:n_pair // 2] if n_pair < 8 else PAIRS,
         [b_kv1.ap()], [b_kv2.ap()])
    P.dma("qpool", kvs[:, :], b_kv2.ap())
    kvb = P.acq1((128, 1088), BF16)
    P.copy(kvb[:, :], kvs[:, :], eng="act")
    P.barrier()
    at = P.acq(8, (128, T), BF16)
    zs = P.acq1((128, 512), F32)
    for h in range(16):
        hp, h2 = h // 2, h % 2
        ch, po = h // 2 % 4 * 0 + h // 2, 64 * (h % 2)
        ch = h // 2
        kvsl = kvb[64 * h2:64 * (h2 + 1), 136 * hp:136 * hp + 64]
        kssl = kvb[64 * h2:64 * (h2 + 1), 136 * hp + 128 + h2:136 * hp + 129 + h2]
        for g in range(T // 512):
            sl = slice(512 * g, 512 * (g + 1))
            qsl = qp[ch][po:po + 64, sl]
            bz = P.bks[4 + (g % 2)]
            P.mm(bz[0:1, 0:512], kssl, qsl)
            P.ts(zs[0:1, 0:512], bz[0:1, 0:512], 1e-6, None, ALU.add)
            P.op("dve", lambda v, o=zs[0:1, 0:512], i=zs[0:1, 0:512]: v.reciprocal(o, i),
                 reads=[zs], writes=[zs])
            bo_ = P.bks[g % 2]
            P.mm(bo_[po:po + 64, 0:512], kvsl, qsl, tile_position=(64 * h2, po))
            bb = P.bks[2 + (g % 2)]
            P.mm(bb[0:64, 0:512], P.ones1[0:1, 0:64], zs[0:1, 0:512])
            P.copy(at[ch][po:po + 64, sl], bo_[po:po + 64, 0:512], eng="act")
            P.tt(at[ch][po:po + 64, sl], at[ch][po:po + 64, sl], bb[0:64, 0:512], ALU.mult)
    P.rel([kvs, kvb, zs] + kp + vt + qp)
    P.barrier()
    x2 = P.acq(8, (128, T), F32)
    emit_load_x(P, x2, d_xin, T)
    for k in range(8):
        P.dma("qpool", wt[k][:, :], woT.ap()[128 * k:128 * (k + 1), :])
    def oevac(oc, g, b):
        sl = slice(512 * g, 512 * (g + 1))
        P.ts(s1[:, :], b[:, 0:512], boc[:, oc:oc + 1], None, ALU.add)
        P.tt(x2[oc][:, sl], x2[oc][:, sl], s1[:, :], ALU.add)
    proj_fm(P, wt, at, T, oevac)
    P.barrier()
    emit_store_x(P, x2, d_xout, T)
    P.rel(x2 + at + wt + [s1, s2, bkt, bvt, bkr, bvr])
    P.finish()



def emit_l2b(P, T=THALF, n_pair=2, d_xin=None, d_xout=None):
    H, J, Dh = 16, 64, 64
    d_xin = d_xin or P.dram_in("x_in", [D, T])
    d_xout = d_xout or P.dram_out("x_out", [D, T])
    wqT = P.dram_in("l2b_wqT", [D, D])
    wkT = P.dram_in("l2b_wkT", [D, D])
    wvT = P.dram_in("l2b_wvT", [D, D])
    woT = P.dram_in("l2b_woT", [D, D])
    bqc = P.dram_in("l2b_bq", [128, 8])
    bkc_d = P.dram_in("l2b_bk", [128, 8])
    boc_d = P.dram_in("l2b_bo", [128, 8])
    d_bv = P.dram_in("l2b_bvrow", [1, D])
    d_pjT = P.dram_in("l2b_projT", [8 * 128, J])
    b_st1 = P.dram_tmp("b_st1", [128, 16])
    b_st2 = P.dram_tmp("b_st2", [128, 16])
    b_str = P.dram_tmp("b_str", [16, 1])
    b_kv1 = P.dram_tmp("b_kv1", [128, 17 * 64])
    b_kv2 = P.dram_tmp("b_kv2", [128, 17 * 64])
    groups = PAIRS[:n_pair // 2] if n_pair < 8 else PAIRS
    nrm = float(Dh) ** (-0.25)
    ratio = float(J) ** (-0.5)
    EPSF = 1e-4

    P.std_init()
    P.init_banks()
    identf_init(P)
    x = P.acq(8, (128, T), F32)
    emit_load_x(P, x, d_xin, T)
    xln = P.acq(8, (128, T), BF16)
    emit_ln(P, x, xln, T)
    P.rel(x)
    P.barrier()
    bq = P.sb("bq", [128, 8], F32)
    bk = P.sb("bk", [128, 8], F32)
    bo = P.sb("bo", [128, 8], F32)
    P.dma("qsp", bq[:, :], bqc.ap())
    P.dma("qsp", bk[:, :], bkc_d.ap())
    P.dma("qsp", bo[:, :], boc_d.ap())
    bvr = P.acq1((1, D), F32)
    P.dma("qsp", bvr[0:1, :], d_bv.ap())
    bvt = P.acq1((128, 1024), F32)
    bcast_to_sbuf(P, bvr[0:1, :], bvt, D)
    s1 = P.acq1((128, 512), F32)
    s2 = P.acq1((128, 512), F32)
    pjT = load_wt(P, d_pjT, "pjT", 8, J)
    wt = load_wt(P, wqT, "wq", 8, D)
    qn = P.acq(8, (128, T), BF16)
    def nevac(dst, bias):
        def f(oc, g, b):
            P.ts(s1[:, :], b[:, 0:512], bias[:, oc:oc + 1], None, ALU.add)
            P.ts(dst[oc][:, 512 * g:512 * (g + 1)], s1[:, :], nrm, None, ALU.mult)
        return f
    proj_fm(P, wt, xln, T, nevac(qn, bq))
    P.barrier()
    for k in range(8):
        P.dma("qpool", wt[k][:, :], wkT.ap()[128 * k:128 * (k + 1), :])
    kn = P.acq(8, (128, T), BF16)
    proj_fm(P, wt, xln, T, nevac(kn, bk))
    P.barrier()
    for k in range(8):
        P.dma("qpool", wt[k][:, :], wvT.ap()[128 * k:128 * (k + 1), :])
    vt = P.acq(T // 128, (128, 1024), BF16)
    def vevac(ti, og, b):
        sl = slice(512 * og, 512 * (og + 1))
        P.tt(vt[ti][:, sl], b[:, 0:512], bvt[:, sl], ALU.add)
    proj_tm(P, wt, xln, T, D, vevac)
    P.rel(xln + [bvt, bvr])
    P.barrier()

    dra = P.acq1((1, T), F32)
    drb = P.acq1((1, T), F32)
    dca = P.acq1((128, 1), F32)
    def diag_rows(srcoc):
        for g in range(T // 512):
            sl = slice(512 * g, 512 * (g + 1))
            P.act(s1[:, :], srcoc[:, sl], AF.Square)
            ba = P.bks[4]
            P.mm(ba[0:1, 0:512], P.onec[0:64, 0:1], s1[0:64, :])
            P.ts(dra[0:1, sl], ba[0:1, 0:512], 0.5, None, ALU.mult)
            P.mm(ba[0:1, 0:512], P.onec[64:128, 0:1], s1[64:128, :], tile_position=(64, 0))
            P.ts(drb[0:1, sl], ba[0:1, 0:512], 0.5, None, ALU.mult)

    mxk = P.acq1((128, 16), F32)
    P.memset(mxk[:, :], -1e30)
    for h in range(H):
        oc, h2 = h // 2, h % 2
        pb = slice(64 * h2, 64 * (h2 + 1))
        for ti in range(T // 128):
            b = P.bks[ti % 4]
            P.mm(b[:, 0:64], kn[oc][pb, 128 * ti:128 * (ti + 1)], pjT[oc][pb, :],
                 tile_position=(64 * h2, 0))
            P.red(dca[:, 0:1], b[:, 0:64], ALU.max)
            P.tt(mxk[:, h:h + 1], mxk[:, h:h + 1], dca[:, 0:1], ALU.max)
    P.dma("qpool", b_st1.ap(), mxk[:, :])
    P.cc("AllReduce", ALU.max, groups, [b_st1.ap()], [b_st2.ap()])
    P.dma("qpool", mxk[:, :], b_st2.ap())
    bt = P.bks[4]
    P.transp(bt[0:16, 0:128], mxk[:, :], P.identf[:, :])
    P.copy(s1[0:16, 0:128], bt[0:16, 0:128])
    P.red(s2[0:16, 0:1], s1[0:16, 0:128], ALU.max)
    P.dma("qsp", b_str.ap(), s2[0:16, 0:1])
    strow = P.acq1((1, 16), F32)
    P.dma("qsp", strow[0:1, :], b_str.ap().rearrange("a b -> b a"))
    stabs = P.acq1((128, 16), F32)
    bt2 = P.bks[5]
    P.mm(bt2[:, 0:16], P.ones1[0:1, :], strow[0:1, :])
    P.copy(stabs[:, :], bt2[:, 0:16], eng="act")
    P.barrier()
    kp = P.acq(T // 128, (128, 1024), BF16)
    for h in range(H):
        oc, h2 = h // 2, h % 2
        if h2 == 0:
            diag_rows(kn[oc])
        drow = dra if h2 == 0 else drb
        pb = slice(64 * h2, 64 * (h2 + 1))
        for ti in range(T // 128):
            bd = P.bks[ti % 2]
            P.mm(bd[:, 0:64], kn[oc][pb, 128 * ti:128 * (ti + 1)], pjT[oc][pb, :],
                 tile_position=(64 * h2, 0))
            P.mm(P.bks[4][:, 0:1], drow[0:1, 128 * ti:128 * (ti + 1)], P.ones1[0:1, 0:1])
            P.copy(dca[:, 0:1], P.bks[4][:, 0:1])
            P.ts(s1[:, 0:64], bd[:, 0:64], dca[:, 0:1], stabs[:, h:h + 1],
                 ALU.subtract, ALU.subtract)
            P.act(s1[:, 0:64], s1[:, 0:64], AF.Exp)
            P.ts(kp[ti][:, 64 * h:64 * h + 64], s1[:, 0:64], ratio, EPSF, ALU.mult, ALU.add)
    P.rel(kn + [mxk, stabs, strow])
    P.barrier()
    qp = P.acq(8, (128, T), BF16)
    mq = P.acq1((128, 1), F32)
    for h in range(H):
        oc, h2 = h // 2, h % 2
        pb = slice(64 * h2, 64 * (h2 + 1))
        if h2 == 0:
            diag_rows(qn[oc])
        drow = dra if h2 == 0 else drb
        for ti in range(T // 128):
            b = P.bks[ti % 2]
            P.mm(b[:, 0:64], qn[oc][pb, 128 * ti:128 * (ti + 1)], pjT[oc][pb, :],
                 tile_position=(64 * h2, 0))
            P.mm(P.bks[4][:, 0:1], drow[0:1, 128 * ti:128 * (ti + 1)], P.ones1[0:1, 0:1])
            P.copy(dca[:, 0:1], P.bks[4][:, 0:1])
            P.red(mq[:, 0:1], b[:, 0:64], ALU.max)
            P.ts(s1[:, 0:64], b[:, 0:64], dca[:, 0:1], mq[:, 0:1], ALU.subtract, ALU.subtract)
            P.act(s1[:, 0:64], s1[:, 0:64], AF.Exp)
            qpt = P.acq1((128, 512), BF16)
            P.ts(qpt[:, 0:64], s1[:, 0:64], ratio, EPSF, ALU.mult, ALU.add)
            tb = P.tb[ti % 2]
            P.transp(tb[0:64, 0:128], qpt[:, 0:64], P.identb[:, :])
            P.copy(qp[oc][pb, 128 * ti:128 * (ti + 1)], tb[0:64, 0:128])
            P.rel(qpt)
    P.rel(qn + [dra, drb, dca, mq])
    P.barrier()
    kvs = P.acq1((128, 1088), F32)
    for hp in range(8):
        bkv = P.bks[hp % 4]
        bks_ = P.bks[4]
        for h2 in range(2):
            h = 2 * hp + h2
            hs = slice(64 * h, 64 * (h + 1))
            tp = (0, 64 * h2)
            for ti in range(T // 128):
                P.mm(bkv[64 * h2:64 * (h2 + 1), 0:64], kp[ti][:, hs], vt[ti][:, hs],
                     start=(ti == 0), stop=(ti == T // 128 - 1), tile_position=tp)
                P.mm(bks_[64 * h2:64 * (h2 + 1), hp:hp + 1], kp[ti][:, hs], P.ones128[:, 0:1],
                     start=(ti == 0), stop=(ti == T // 128 - 1), tile_position=tp)
            P.copy(kvs[64 * h2:64 * (h2 + 1), 136 * hp:136 * hp + 64], bkv[64 * h2:64 * (h2 + 1), 0:64])
            P.copy(kvs[64 * h2:64 * (h2 + 1), 136 * hp + 128 + h2:136 * hp + 129 + h2],
                   bks_[64 * h2:64 * (h2 + 1), hp:hp + 1])
    P.barrier()
    P.dma("qpool", b_kv1.ap(), kvs[:, :])
    P.cc("AllReduce", ALU.add, groups, [b_kv1.ap()], [b_kv2.ap()])
    P.dma("qpool", kvs[:, :], b_kv2.ap())
    kvb = P.acq1((128, 1088), BF16)
    P.copy(kvb[:, :], kvs[:, :], eng="act")
    P.barrier()
    at = P.acq(8, (128, T), BF16)
    zs = P.acq1((128, 512), F32)
    for h in range(16):
        hp, h2 = h // 2, h % 2
        ch, po = h // 2, 64 * (h % 2)
        kvsl = kvb[64 * h2:64 * (h2 + 1), 136 * hp:136 * hp + 64]
        kssl = kvb[64 * h2:64 * (h2 + 1), 136 * hp + 128 + h2:136 * hp + 129 + h2]
        for g in range(T // 512):
            sl = slice(512 * g, 512 * (g + 1))
            qsl = qp[ch][po:po + 64, sl]
            bz = P.bks[4 + (g % 2)]
            P.mm(bz[0:1, 0:512], kssl, qsl, tile_position=(64 * h2, 0))
            P.ts(zs[0:1, 0:512], bz[0:1, 0:512], 0.01, None, ALU.max)
            P.op("dve", lambda v, o=zs[0:1, 0:512], i=zs[0:1, 0:512]: v.reciprocal(o, i),
                 reads=[zs], writes=[zs])
            bo_ = P.bks[g % 2]
            P.mm(bo_[po:po + 64, 0:512], kvsl, qsl, tile_position=(64 * h2, po))
            bb = P.bks[2 + (g % 2)]
            P.mm(bb[0:64, 0:512], P.ones1[0:1, 0:64], zs[0:1, 0:512])
            P.copy(at[ch][po:po + 64, sl], bo_[po:po + 64, 0:512], eng="act")
            P.tt(at[ch][po:po + 64, sl], at[ch][po:po + 64, sl], bb[0:64, 0:512], ALU.mult)
    P.rel([kvs, kvb, zs] + kp + vt + qp)
    P.barrier()
    x2 = P.acq(8, (128, T), F32)
    emit_load_x(P, x2, d_xin, T)
    for k in range(8):
        P.dma("qpool", wt[k][:, :], woT.ap()[128 * k:128 * (k + 1), :])
    def oevac(oc, g, b):
        sl = slice(512 * g, 512 * (g + 1))
        P.ts(s1[:, :], b[:, 0:512], bo[:, oc:oc + 1], None, ALU.add)
        P.tt(x2[oc][:, sl], x2[oc][:, sl], s1[:, :], ALU.add)
    proj_fm(P, wt, at, T, oevac)
    P.barrier()
    emit_store_x(P, x2, d_xout, T)
    P.rel(x2 + at + wt + pjT + [s1, s2])
    P.finish()



def emit_l3b(P, T=THALF, n_pair=2, d_xin=None, d_xout=None):
    H, Dh, CH = 16, 64, 128
    NCH = T // CH
    d_xin = d_xin or P.dram_in("x_in", [D, T])
    d_xout = d_xout or P.dram_out("x_out", [D, T])
    dWq = P.dram_in("l3b_wq", [D, D])
    dWk = P.dram_in("l3b_wk", [D, D])
    dWv = P.dram_in("l3b_wv", [D, D])
    dWo = P.dram_in("l3b_wo", [D, D])
    dWg1 = P.dram_in("l3b_wg1", [D, 64])
    dWg2 = P.dram_in("l3b_wg2", [64, D])
    d_gam = P.dram_in("l3b_gamma", [128, 8])
    d_bet = P.dram_in("l3b_beta", [128, 8])
    d_maskT = P.dram_in("l3b_maskT", [128, 16 * 128])
    d_lamqb = P.dram_in("l3b_lamqb", [128, 8 * 128])
    d_lamk = P.dram_in("l3b_lamk", [128, 16])
    d_clam = P.dram_in("l3b_clam", [16, T])
    d_lamc = P.dram_in("l3b_lamc", [1, 16])
    b_kv1 = P.dram_tmp("b_kv1", [128, 512])
    b_kv2 = P.dram_tmp("b_kv2", [128, 512])
    groups = PAIRS[:n_pair // 2] if n_pair < 8 else PAIRS

    P.std_init()
    P.init_banks()
    identf_init(P)
    x = P.acq(8, (128, T), F32)
    emit_load_x(P, x, d_xin, T)
    xln = P.acq(8, (128, T), BF16)
    emit_ln(P, x, xln, T)
    P.rel(x)
    P.barrier()
    wt = load_wt(P, dWq, "wt", 8, D)
    s1 = P.acq1((128, 512), F32)
    qf = P.acq(8, (128, T), BF16)
    def silu_evac(dst):
        def f(oc, g, b):
            P.act(dst[oc][:, 512 * g:512 * (g + 1)], b[:, 0:512], AF.Silu)
        return f
    proj_fm(P, wt, xln, T, silu_evac(qf))
    P.barrier()
    for k in range(8):
        P.dma("qpool", wt[k][:, :], dWk.ap()[128 * k:128 * (k + 1), :])
    kf = P.acq(8, (128, T), BF16)
    proj_fm(P, wt, xln, T, silu_evac(kf))
    P.barrier()
    for k in range(8):
        P.dma("qpool", wt[k][:, :], dWv.ap()[128 * k:128 * (k + 1), :])
    vt = P.acq(NCH, (128, 1024), BF16)
    def vevac(ti, og, b):
        P.act(vt[ti][:, 512 * og:512 * (og + 1)], b[:, 0:512], AF.Silu)
    proj_tm(P, wt, xln, T, D, vevac)
    P.barrier()
    g1 = P.acq1((128, T), BF16)
    wg1 = P.acq(8, (128, 512), BF16)
    for k in range(8):
        P.dma("qpool", wg1[k][:, 0:64], dWg1.ap()[128 * k:128 * (k + 1), :])
    for g in range(T // 512):
        sl = slice(512 * g, 512 * (g + 1))
        b = P.bks[g % 2]
        for k in range(8):
            P.mm(b[0:64, 0:512], wg1[k][:, 0:64], xln[k][:, sl], start=(k == 0), stop=(k == 7))
        P.copy(g1[0:64, sl], b[0:64, 0:512])
    P.rel(wg1 + xln)
    P.barrier()
    at = P.acq(8, (128, T), BF16)
    lamk = P.sb("lamk", [128, 16], F32)
    P.dma("qsp", lamk[:, :], d_lamk.ap())
    lamqb = P.acq1((128, 1024), BF16)
    P.dma("qpool", lamqb[:, :], d_lamqb.ap())
    maskT = P.acq1((128, 2048), BF16)
    P.dma("qpool", maskT[:, :], d_maskT.ap())
    kvst = P.acq1((128, 512), F32)
    kvb = P.acq1((128, 512), BF16)
    P.memset(kvst[:, :], 0.0)
    P.memset(kvb[:, :], 0.0)
    lamc_r = P.sb("lamc_r", [1, 16], F32)
    P.dma("qsp", lamc_r[0:1, :], d_lamc.ap())
    lamc_c = P.sb("lamc_c", [128, 16], F32)
    blc = P.bks[5]
    P.mm(blc[:, 0:16], P.ones1[0:1, :], lamc_r[0:1, :])
    P.copy(lamc_c[:, :], blc[:, 0:16], eng="act")
    smt = P.acq1((128, 512), BF16)
    ktl = P.acq1((128, 512), BF16)
    qlam = P.acq1((128, 512), BF16)
    for h in range(H):
        hp, h2 = h // 2, h % 2
        ch, po = h // 2, 64 * (h % 2)
        pb = slice(po, po + 64)
        kvsl = kvst[64 * h2:64 * (h2 + 1), 64 * hp:64 * hp + 64]
        kvbsl = kvb[64 * h2:64 * (h2 + 1), 64 * hp:64 * hp + 64]
        for c in range(NCH):
            csl = slice(128 * c, 128 * (c + 1))
            bS = P.bks[0]
            P.mm(bS[:, 0:128], kf[ch][pb, csl], qf[ch][pb, csl], tile_position=(po, 0))
            P.tt(smt[:, 0:128], bS[:, 0:128], maskT[:, 128 * h:128 * (h + 1)], ALU.mult)
            bo_ = P.bks[1]
            P.mm(bo_[0:64, 0:128], vt[c][:, 64 * h:64 * h + 64], smt[:, 0:128],
                 start=True, stop=False)
            P.tt(qlam[pb, 0:128], qf[ch][pb, csl], lamqb[pb, 128 * hp:128 * (hp + 1)], ALU.mult)
            P.mm(bo_[0:64, 0:128], kvbsl, qlam[pb, 0:128],
                 start=False, stop=True, tile_position=(64 * h2, 0))
            P.copy(at[ch][pb, csl], bo_[0:64, 0:128], eng="act")
            tb = P.tb[c % 2]
            P.transp(tb[0:128, 0:64], kf[ch][pb, csl], P.identb[pb, pb])
            P.ts(ktl[:, 0:64], tb[0:128, 0:64], lamk[:, h:h + 1], None, ALU.mult)
            bK = P.bks[2 + (c % 2)]
            P.mm(bK[64 * h2:64 * (h2 + 1), 0:64], ktl[:, 0:64], vt[c][:, 64 * h:64 * h + 64],
                 tile_position=(0, 64 * h2))
            P.stt(kvsl, kvsl, lamc_c[64 * h2:64 * (h2 + 1), h:h + 1],
                  bK[64 * h2:64 * (h2 + 1), 0:64], ALU.mult, ALU.add)
            P.copy(kvbsl, kvsl, eng="act")
    P.barrier()
    P.dma("qpool", b_kv1.ap(), kvst[:, :])
    P.cc("AllReduce", ALU.add, groups, [b_kv1.ap()], [b_kv2.ap()])
    tot = P.acq1((128, 512), F32)
    P.dma("qpool", tot[:, :], b_kv2.ap())
    P.tt(tot[:, :], tot[:, :], kvst[:, :], ALU.subtract)
    kin = P.acq1((128, 512), BF16)
    P.copy(kin[:, :], tot[:, :], eng="act")
    P.rel([tot, kvst, kvb])
    clrow = P.acq1((1, T), F32)
    for h in range(H):
        hp, h2 = h // 2, h % 2
        ch, po = h // 2, 64 * (h % 2)
        pb = slice(po, po + 64)
        kinsl = kin[64 * h2:64 * (h2 + 1), 64 * hp:64 * hp + 64]
        P.dma("qsp", clrow[0:1, :], d_clam.ap()[h:h + 1, :])
        for c in range(NCH):
            csl = slice(128 * c, 128 * (c + 1))
            bb = P.bks[4 + (c % 2)]
            P.mm(bb[0:64, 0:128], P.ones1[0:1, 0:64], clrow[0:1, csl])
            P.tt(qlam[pb, 0:128], qf[ch][pb, csl], bb[0:64, 0:128], ALU.mult)
            bo_ = P.bks[c % 2]
            P.mm(bo_[0:64, 0:128], kinsl, qlam[pb, 0:128], tile_position=(64 * h2, 0))
            P.ts(smt[:, 0:128], bo_[0:64, 0:128], 1.0, None, ALU.mult) if False else None
            P.tt(at[ch][pb, csl], at[ch][pb, csl], bo_[0:64, 0:128], ALU.add)
    P.rel([kin, clrow, smt, ktl, qlam, lamqb, maskT] + qf + kf + vt)
    P.barrier()
    gcol = P.sb("gcol", [128, 8], F32)
    bcol = P.sb("bcol", [128, 8], F32)
    P.dma("qsp", gcol[:, :], d_gam.ap())
    P.dma("qsp", bcol[:, :], d_bet.ap())
    atf = P.acq(8, (128, T), F32)
    for ccc in range(8):
        P.copy(atf[ccc][:, :], at[ccc][:, :])
    P.rel(at)
    onrm = P.acq(8, (128, T), BF16)
    emit_ln(P, atf, onrm, T, pre="oln", gcol=gcol, bcol=bcol)
    P.rel(atf)
    wg2 = P.acq1((128, 1024), BF16)
    P.dma("qpool", wg2[0:64, :], dWg2.ap())
    gt = P.acq(8, (128, T), BF16)
    for oc in range(8):
        for g in range(T // 512):
            sl = slice(512 * g, 512 * (g + 1))
            b = P.bks[(oc + g) % 4]
            P.mm(b[:, 0:512], wg2[0:64, 128 * oc:128 * (oc + 1)], g1[0:64, sl])
            P.act(gt[oc][:, sl], b[:, 0:512], AF.Sigmoid)
    P.rel([g1, wg2])
    for oc in range(8):
        P.tt(onrm[oc][:, :], onrm[oc][:, :], gt[oc][:, :], ALU.mult)
    P.rel(gt)
    P.barrier()
    x2 = P.acq(8, (128, T), F32)
    emit_load_x(P, x2, d_xin, T)
    for k in range(8):
        P.dma("qpool", wt[k][:, :], dWo.ap()[128 * k:128 * (k + 1), :])
    def oevac(oc, g, b):
        sl = slice(512 * g, 512 * (g + 1))
        P.tt(x2[oc][:, sl], x2[oc][:, sl], b[:, 0:512], ALU.add)
    proj_fm(P, wt, onrm, T, oevac)
    P.barrier()
    emit_store_x(P, x2, d_xout, T)
    P.rel(x2 + onrm + wt + [s1])
    P.finish()



def emit_l1b(P, T=THALF, n_pair=2, d_xin=None, d_xout=None):
    d_xin = d_xin or P.dram_in("x_in", [D, T])
    d_xout = d_xout or P.dram_out("x_out", [D, T])
    dWomT = P.dram_in("l1b_womT", [D, D])
    d_bom = P.dram_in("l1b_bom", [128, 8])
    d_lsc = P.dram_in("l1b_lsc", [128, 8])
    d_isq = P.dram_in("l1b_isq", [1, T])
    d_ipos = P.dram_in("l1b_ipos", [1, T])
    d_gam = P.dram_in("l1b_gamcol", [128, 1])
    d_lng = P.dram_in("l1b_lngam", [128, 32])
    d_lnb = P.dram_in("l1b_lnbet", [128, 32])
    dW1T = P.dram_in("l1b_w1T", [4 * D, 2 * D])
    d_b1 = P.dram_in("l1b_b1", [128, 16])
    dW2T = P.dram_in("l1b_w2T", [2 * D, D])
    d_b2 = P.dram_in("l1b_b2", [128, 8])
    d_ctx = P.dram_tmp("d_ctx", [4 * D, T], BF16)
    d_slc = P.dram_tmp("d_slc", [2 * D, T], BF16)
    b_c1 = P.dram_tmp("b_c1", [128, 8])
    b_c2 = P.dram_tmp("b_c2", [128, 8])
    b_m1 = P.dram_tmp("b_m1", [128, 16])
    b_m2 = P.dram_tmp("b_m2", [128, 16])
    groups = PAIRS[:n_pair // 2] if n_pair < 8 else PAIRS
    TPI = 2.0 * math.pi
    MAGIC = 12582912.0

    P.std_init()
    P.init_banks()
    d_murs = P.dram_tmp("d_murs", [2, T])
    x = P.acq(8, (128, T), F32)
    emit_load_x(P, x, d_xin, T)
    xln = P.acq(8, (128, T), BF16)
    mu, rs = emit_ln(P, x, xln, T, keep_rows=True)
    P.dma("qsp", d_murs.ap()[0:1, :], mu[0:1, :])
    P.dma("qsp", d_murs.ap()[1:2, :], rs[0:1, :])
    P.rel([mu, rs] + x)
    P.barrier()
    bomc = P.sb("bomc", [128, 8], F32)
    lscc = P.sb("lscc", [128, 8], F32)
    gamc = P.sb("gamc", [128, 1], F32)
    P.dma("qsp", bomc[:, :], d_bom.ap())
    P.dma("qsp", lscc[:, :], d_lsc.ap())
    P.dma("qsp", gamc[:, :], d_gam.ap())
    els = P.sb("els", [128, 8], F32)
    P.act(els[:, :], lscc[:, :], AF.Exp)
    isqr = P.acq1((1, T), F32)
    P.dma("qsp", isqr[0:1, :], d_isq.ap())
    isqb = P.acq1((128, T), F32)
    bcast_to_sbuf(P, isqr[0:1, :], isqb, T)
    P.rel(isqr)
    s1 = P.acq1((128, 512), F32)
    wt = load_wt(P, dWomT, "wt", 8, D)
    om = P.acq(8, (128, T), F32)
    def omevac(oc, g, b):
        sl = slice(512 * g, 512 * (g + 1))
        P.ts(s1[:, :], b[:, 0:512], bomc[:, oc:oc + 1], els[:, oc:oc + 1], ALU.add, ALU.mult)
        P.tt(om[oc][:, sl], s1[:, :], isqb[:, sl], ALU.mult)
    proj_fm(P, wt, xln, T, omevac)
    P.rel(wt)
    onest = P.acq1((128, T), F32)
    P.memset(onest[:, :], 1.0)
    for oc in range(8):
        P.scan(om[oc][:, :], onest[:, :], om[oc][:, :], 0.0, ALU.mult, ALU.add)
    P.rel(onest)
    cc8 = P.acq1((128, 8), F32)
    for oc in range(8):
        P.copy(cc8[:, oc:oc + 1], om[oc][:, T - 1:T])
    P.dma("qpool", b_c1.ap(), cc8[:, :])
    P.cc("AllReduce", ALU.add, groups, [b_c1.ap()], [b_c2.ap()])
    tot8 = P.acq1((128, 8), F32)
    P.dma("qpool", tot8[:, :], b_c2.ap())
    P.tt(tot8[:, :], tot8[:, :], cc8[:, :], ALU.subtract)
    P.ts(tot8[:, :], tot8[:, :], gamc[:, 0:1], None, ALU.mult)
    for oc in range(8):
        P.ts(om[oc][:, :], om[oc][:, :], tot8[:, oc:oc + 1], None, ALU.add)
    P.barrier()
    cph = P.acq(8, (128, T), BF16)
    sph = P.acq(8, (128, T), BF16)
    rr = P.acq1((128, T), F32)
    cm = P.acq1((128, T), F32)
    mc16 = P.acq1((128, 16), F32)
    for oc in range(8):
        p = om[oc]
        P.ts(rr[:, :], p[:, :], 1.0 / TPI, MAGIC, ALU.mult, ALU.add)
        P.ts(rr[:, :], rr[:, :], MAGIC, None, ALU.subtract)
        P.stt(rr[:, :], rr[:, :], -TPI, p[:, :], ALU.mult, ALU.add)
        P.ts(rr[:, :], rr[:, :], 3.1415925, -3.1415925, ALU.min, ALU.max)
        P.act(sph[oc][:, :], rr[:, :], AF.Sin)
        P.ts(rr[:, :], p[:, :], 1.0 / TPI, 0.25, ALU.mult, ALU.add)
        P.ts(rr[:, :], rr[:, :], MAGIC, None, ALU.add)
        P.ts(rr[:, :], rr[:, :], MAGIC, None, ALU.subtract)
        P.ts(rr[:, :], rr[:, :], -0.25, None, ALU.add)
        P.stt(rr[:, :], rr[:, :], -TPI, p[:, :], ALU.mult, ALU.add)
        P.ts(rr[:, :], rr[:, :], 3.1415925, -3.1415925, ALU.min, ALU.max)
        P.act(cph[oc][:, :], rr[:, :], AF.Sin)
        for ri, trig in [(0, cph), (1, sph)]:
            P.tt(cm[:, :], xln[oc][:, :], trig[oc][:, :], ALU.mult)
            P.dma("qpool", d_ctx.ap()[128 * (8 * ri + oc):128 * (8 * ri + oc) + 128, :], cm[:, :])
            P.ts(p[:, :], p[:, :], 0.0, 1.0, ALU.mult, ALU.add)
            P.scan(cm[:, :], p[:, :], cm[:, :], 0.0, ALU.mult, ALU.add)
            P.copy(mc16[:, 8 * ri + oc:8 * ri + oc + 1], cm[:, T - 1:T])
            P.dma("qpool", d_slc.ap()[128 * (8 * ri + oc):128 * (8 * ri + oc) + 128, :], cm[:, :])
        P.rel([p, xln[oc]])
        om[oc] = None
        xln[oc] = None
    P.rel([rr, isqb])
    P.barrier()
    P.dma("qpool", b_m1.ap(), mc16[:, :])
    P.cc("AllReduce", ALU.add, groups, [b_m1.ap()], [b_m2.ap()])
    mtot = P.acq1((128, 16), F32)
    P.dma("qpool", mtot[:, :], b_m2.ap())
    P.tt(mtot[:, :], mtot[:, :], mc16[:, :], ALU.subtract)
    P.ts(mtot[:, :], mtot[:, :], gamc[:, 0:1], None, ALU.mult)
    iposr = P.acq1((1, T), F32)
    P.dma("qsp", iposr[0:1, :], d_ipos.ap())
    iposb = P.acq1((128, T), F32)
    bcast_to_sbuf(P, iposr[0:1, :], iposb, T)
    P.rel(iposr)
    mr = P.acq1((128, T), F32)
    mi = P.acq1((128, T), F32)
    rt = P.acq1((128, T), BF16)
    for oc in range(8):
        P.dma("qsp", mr.bitcast(BF16)[:, 0:T], d_slc.ap()[128 * oc:128 * oc + 128, :]) if False else None
        mrb = P.acq1((128, T), BF16)
        P.dma("qsp", mrb[:, :], d_slc.ap()[128 * oc:128 * oc + 128, :])
        P.ts(mr[:, :], mrb[:, :], mtot[:, oc:oc + 1], None, ALU.add)
        P.tt(mr[:, :], mr[:, :], iposb[:, :], ALU.mult)
        P.dma("qsp", mrb[:, :], d_slc.ap()[128 * (8 + oc):128 * (8 + oc) + 128, :])
        P.ts(mi[:, :], mrb[:, :], mtot[:, 8 + oc:8 + oc + 1], None, ALU.add)
        P.tt(mi[:, :], mi[:, :], iposb[:, :], ALU.mult)
        P.rel(mrb)
        P.tt(cm[:, :], mr[:, :], cph[oc][:, :], ALU.mult)
        P.tt(rt[:, :], mi[:, :], sph[oc][:, :], ALU.mult)
        P.tt(rt[:, :], cm[:, :], rt[:, :], ALU.add)
        P.dma("qsp", d_ctx.ap()[128 * (16 + oc):128 * (16 + oc) + 128, :], rt[:, :])
        P.tt(cm[:, :], mi[:, :], cph[oc][:, :], ALU.mult)
        P.tt(rt[:, :], mr[:, :], sph[oc][:, :], ALU.mult)
        P.tt(rt[:, :], cm[:, :], rt[:, :], ALU.subtract)
        P.dma("qsp", d_ctx.ap()[128 * (24 + oc):128 * (24 + oc) + 128, :], rt[:, :])
    P.rel([cm, mr, mi, rt, mc16, mtot, iposb] + cph + sph)
    P.barrier()
    cbuf = P.acq(2, (128, T), BF16)
    sq2 = P.acq1((128, 512), F32)
    cmu = P.acq1((1, T), F32)
    crs = P.acq1((1, T), F32)
    b_s, b_q = P.bks[0], P.bks[1]
    for g in range(T // 512):
        sl = slice(512 * g, 512 * (g + 1))
        for k in range(32):
            cb = cbuf[k % 2]
            if g == 0:
                pass
            P.dma("qpool", cb[:, sl], d_ctx.ap()[128 * k:128 * (k + 1), sl])
            P.mm(b_s[0:1, 0:512], P.ones128[:, 0:1], cb[:, sl], start=(k == 0), stop=(k == 31))
            P.act(sq2[:, :], cb[:, sl], AF.Square)
            P.mm(b_q[0:1, 0:512], P.onec[:, 0:1], sq2[:, :], start=(k == 0), stop=(k == 31))
        P.ts(cmu[0:1, sl], b_s[0:1, 0:512], 1.0 / 4096.0, None, ALU.mult)
        P.ts(crs[0:1, sl], b_q[0:1, 0:512], 1.0 / 4096.0, None, ALU.mult)
        P.tt(sq2[0:1, 0:512], cmu[0:1, sl], cmu[0:1, sl], ALU.mult)
        P.tt(crs[0:1, sl], crs[0:1, sl], sq2[0:1, 0:512], ALU.subtract)
        P.ts(crs[0:1, sl], crs[0:1, sl], 1e-5, None, ALU.add)
        P.act(crs[0:1, sl], crs[0:1, sl], AF.Sqrt)
        P.op("dve", lambda v, o=crs[0:1, sl], i=crs[0:1, sl]: v.reciprocal(o, i),
             reads=[crs], writes=[crs])
    P.rel([sq2])
    P.barrier()
    lngc = P.sb("lngc", [128, 32], F32)
    lnbc = P.sb("lnbc", [128, 32], F32)
    P.dma("qsp", lngc[:, :], d_lng.ap())
    P.dma("qsp", lnbc[:, :], d_lnb.ap())
    h1 = P.acq(16, (128, T), BF16)
    cb8 = P.acq(8, (128, T), BF16)
    w1b = P.acq(8, (128, T), BF16)
    mub = P.acq1((128, T), F32)
    rsb = P.acq1((128, T), F32)
    bcast_to_sbuf(P, cmu[0:1, :], mub, T)
    bcast_to_sbuf(P, crs[0:1, :], rsb, T)
    P.rel([cmu, crs])
    sA = P.acq1((128, T), F32)
    for kg in range(4):
        for kk in range(8):
            k = 8 * kg + kk
            cb = cb8[kk]
            P.dma("qpool", cb[:, :], d_ctx.ap()[128 * k:128 * (k + 1), :])
            P.tt(sA[:, :], cb[:, :], mub[:, :], ALU.subtract)
            P.tt(sA[:, :], sA[:, :], rsb[:, :], ALU.mult)
            P.ts(cb[:, :], sA[:, :], lngc[:, k:k + 1], lnbc[:, k:k + 1], ALU.mult, ALU.add)
            P.dma("qpool", w1b[kk][:, :], dW1T.ap()[128 * k:128 * (k + 1), :])
        for oc in range(16):
            for kk in range(8):
                for g in range(T // 512):
                    b = P.bks[g % 4]
                    P.mm(b[:, 0:512], w1b[kk][:, 128 * oc:128 * (oc + 1)],
                         cb8[kk][:, 512 * g:512 * (g + 1)], start=(kk == 0), stop=(kk == 7))
            for g in range(T // 512):
                sl = slice(512 * g, 512 * (g + 1))
                b = P.bks[g % 4]
                if kg == 0:
                    P.copy(h1[oc][:, sl], b[:, 0:512])
                else:
                    P.tt(h1[oc][:, sl], h1[oc][:, sl], b[:, 0:512], ALU.add)
    P.rel(cb8 + w1b + [cbuf[0], cbuf[1], mub, rsb, sA, cc8, tot8])
    P.barrier()
    b1c = P.sb("b1c", [128, 16], F32)
    P.dma("qsp", b1c[:, :], d_b1.ap())
    for oc in range(16):
        P.act(h1[oc][:, :], h1[oc][:, :], AF.Gelu, bias=b1c[:, oc:oc + 1], scale=1.0)
    P.barrier()
    b2c = P.sb("b2c", [128, 8], F32)
    P.dma("qsp", b2c[:, :], d_b2.ap())
    x2 = P.acq(2, (128, T), F32)
    w2c = P.acq(16, (128, 1024), BF16)
    s2 = P.acq1((128, 512), F32)
    s3 = P.acq1((128, 512), F32)
    for k in range(16):
        P.dma("qpool", w2c[k][:, :], dW2T.ap()[128 * k:128 * (k + 1), :])
    mu = P.acq1((1, T), F32)
    rs = P.acq1((1, T), F32)
    P.dma("qsp", mu[0:1, :], d_murs.ap()[0:1, :])
    P.dma("qsp", rs[0:1, :], d_murs.ap()[1:2, :])
    bmu, brs = P.bks[4], P.bks[5]
    for oc in range(8):
        xo = x2[oc % 2]
        P.dma("qsp", xo[:, :], d_xin.ap()[128 * oc:128 * (oc + 1), :])
        for k in range(16):
            for g in range(T // 512):
                b = P.bks[g % 4]
                P.mm(b[:, 0:512], w2c[k][:, 128 * oc:128 * (oc + 1)],
                     h1[k][:, 512 * g:512 * (g + 1)], start=(k == 0), stop=(k == 15))
        for g in range(T // 512):
            sl = slice(512 * g, 512 * (g + 1))
            b = P.bks[g % 4]
            P.mm(bmu[:, 0:512], P.ones1[0:1, :], mu[0:1, sl])
            P.mm(brs[:, 0:512], P.ones1[0:1, :], rs[0:1, sl])
            P.tt(s2[:, :], xo[:, sl], bmu[:, 0:512], ALU.subtract)
            P.tt(s3[:, :], s2[:, :], brs[:, 0:512], ALU.mult)
            P.tt(xo[:, sl], xo[:, sl], s3[:, :], ALU.add)
            P.ts(s2[:, :], b[:, 0:512], b2c[:, oc:oc + 1], None, ALU.add)
            P.tt(xo[:, sl], xo[:, sl], s2[:, :], ALU.add)
        P.dma("qsp", d_xout.ap()[128 * oc:128 * (oc + 1), :], xo[:, :])
    P.rel(x2 + h1 + w2c + [s2, s3, mu, rs, s1])
    P.finish()


def rowmul_scalar(P, outrow, inrow, scal_ap, T):
    for g in range((T + 511) // 512):
        sl = slice(512 * g, min(512 * (g + 1), T))
        n = sl.stop - sl.start
        b = P.bks[5]
        P.mm(b[0:1, 0:n], scal_ap, inrow[0:1, sl])
        P.copy(outrow[0:1, sl], b[0:1, 0:n])


MAGICF = 12582912.0


def pmax_to_row(P, acc, T, d_b, rowout, s1):
    strip = P.acq1((128, 16), F32)
    for b in range(T // 128):
        bk = P.bks[4 + (b % 2)]
        P.transp(bk[:, 0:128], acc[:, 128 * b:128 * (b + 1)], P.identf[:, :])
        P.red(strip[:, b:b + 1], bk[:, 0:128], ALU.max)
    bk = P.bks[4]
    P.transp(bk[0:16, 0:128], strip[:, 0:16], P.identf[:, :])
    P.copy(s1[0:16, 0:128], bk[0:16, 0:128])
    P.dma("qsp", d_b.ap(), s1[0:16, 0:128])
    P.dma("qsp", rowout[0:1, :].rearrange("x (a p) -> x a p", a=T // 128),
          d_b.ap()[0:T // 128, :].rearrange("a p -> a p").unsqueeze(0))
    P.rel(strip)



def emit_l2f(P, T=THALF, dbg=False, d_xin=None, d_xout=None, quant=True):
    INNER = 4096
    d_xin = d_xin or P.dram_in("x_in", [D, T])
    d_xout = d_xout or P.dram_out("x_out", [D, T])
    dWg = P.dram_in("l2f_wgT", [D, INNER])
    dWu = P.dram_in("l2f_wuT", [D, INNER])
    dWd = P.dram_in("l2f_wdT", [INNER, D])
    sc_save = getattr(P, "scope", "")
    P.push_scope("l2fq_")
    d_wqg = P.dram_tmp("d_wqg", [D, INNER], BF16)
    d_wqu = P.dram_tmp("d_wqu", [D, INNER], BF16)
    d_wqd = P.dram_tmp("d_wqd", [INNER, D], BF16)
    d_rw = P.dram_tmp("d_rw", [1, 4])
    P.push_scope(sc_save)
    d_p = P.dram_tmp("d_p", [INNER, T])
    d_b16 = P.dram_tmp("d_b16", [16, 128])
    P.std_init()
    P.init_banks()
    identf_init(P)
    x = P.acq(8, (128, T), F32)
    emit_load_x(P, x, d_xin, T)
    xln = P.acq(8, (128, T), F32)
    emit_ln(P, x, xln, T)
    P.rel(x)
    P.barrier()
    s1 = P.acq1((128, 512), F32)
    acc = P.acq1((128, T), F32)
    P.act(acc[:, :], xln[0][:, :], AF.Abs)
    absx = P.acq1((128, T), F32)
    for c in range(1, 8):
        P.act(absx[:, :], xln[c][:, :], AF.Abs)
        P.tt(acc[:, :], acc[:, :], absx[:, :], ALU.max)
    mrow = P.acq1((1, T), F32)
    pmax_to_row(P, acc, T, d_b16, mrow, s1)
    P.ts(mrow[0:1, :], mrow[0:1, :], 1e-5, None, ALU.max)
    rx = P.acq1((1, T), F32)
    P.ts(rx[0:1, :], mrow[0:1, :], 1.0 / 127.0, None, ALU.mult)
    sxb = P.acq1((128, T), F32)
    P.op("dve", lambda v, o=mrow[0:1, :], i=mrow[0:1, :]: v.reciprocal(o, i),
         reads=[mrow], writes=[mrow])
    P.ts(mrow[0:1, :], mrow[0:1, :], 127.0, None, ALU.mult)
    bcast_to_sbuf(P, mrow[0:1, :], sxb, T)
    xq = P.acq(8, (128, T), BF16)
    for c in range(8):
        P.tt(acc[:, :], xln[c][:, :], sxb[:, :], ALU.mult)
        P.ts(acc[:, :], acc[:, :], MAGICF, None, ALU.add)
        P.ts(acc[:, :], acc[:, :], MAGICF, 127.0, ALU.subtract, ALU.min)
        P.ts(xq[c][:, :], acc[:, :], -128.0, None, ALU.max)
    if dbg:
        d_dxq = P.dram_out("dbg_xq", [128, T])
        xqf = P.acq1((128, T), F32)
        P.copy(xqf[:, :], xq[0][:, :])
        P.dma("qsp", d_dxq.ap(), xqf[:, :])
        P.rel(xqf)
        d_dmr = P.dram_out("dbg_rw", [1, 4])
    P.rel(xln + [sxb])
    P.barrier()
    rwrow = P.acq1((1, 4), F32)
    WSPEC = [(dWg, d_wqg, 8, INNER), (dWu, d_wqu, 8, INNER), (dWd, d_wqd, 32, D)]
    if quant:
        wbuf = P.acq(2, (128, INNER), BF16)
        for wi, (dsrc, ddst, nr, nc_) in enumerate(WSPEC):
            bsum = P.bks[2]
            nmm = 0
            for k in range(nr):
                P.dma("qpool", wbuf[k % 2][:, 0:nc_], dsrc.ap()[128 * k:128 * (k + 1), :])
                for g in range(nc_ // 512):
                    P.act(s1[:, :], wbuf[k % 2][:, 512 * g:512 * (g + 1)], AF.Abs)
                    P.mm(bsum[0:1, 0:512], P.onec[:, 0:1], s1[:, :],
                         start=(nmm == 0), stop=(nmm == nr * (nc_ // 512) - 1))
                    nmm += 1
            P.red(rwrow[0:1, wi:wi + 1], bsum[0:1, 0:512], ALU.add)
            P.ts(rwrow[0:1, wi:wi + 1], rwrow[0:1, wi:wi + 1], 1.0 / (4096.0 * 1024.0), None, ALU.mult)
            P.ts(rwrow[0:1, wi:wi + 1], rwrow[0:1, wi:wi + 1], 1e-5, None, ALU.max)
        swc = P.acq1((128, 4), F32)
        sww = P.acq1((1, 4), F32)
        P.op("dve", lambda v, o=sww[0:1, :], i=rwrow[0:1, :]: v.reciprocal(o, i),
             reads=[rwrow], writes=[sww])
        bsw = P.bks[3]
        P.mm(bsw[:, 0:4], P.ones1[0:1, :], sww[0:1, :])
        P.copy(swc[:, :], bsw[:, 0:4], eng="act")
        wbf = P.acq(2, (128, 2048), F32)
        for wi, (dsrc, ddst, nr, nc_) in enumerate(WSPEC):
            for k in range(nr):
                for hf in range((nc_ + 2047) // 2048):
                    w = min(2048, nc_ - 2048 * hf)
                    hs = slice(2048 * hf, 2048 * hf + w)
                    wb = wbf[(2 * k + hf) % 2]
                    P.dma("qsp", wb[:, 0:w], dsrc.ap()[128 * k:128 * (k + 1), hs])
                    qt = P.acq1((128, 2048), BF16)
                    for g in range(w // 512):
                        sl = slice(512 * g, 512 * (g + 1))
                        P.ts(s1[:, :], wb[:, sl], swc[:, wi:wi + 1], None, ALU.mult)
                        P.ts(s1[:, :], s1[:, :], MAGICF, None, ALU.add)
                        P.ts(s1[:, :], s1[:, :], MAGICF, 1.0, ALU.subtract, ALU.min)
                        P.ts(qt[:, sl], s1[:, :], -1.0, None, ALU.max)
                    P.dma("qsp", ddst.ap()[128 * k:128 * (k + 1), hs], qt[:, 0:w])
                    P.rel(qt)
        P.dma("qsp", d_rw.ap(), rwrow[0:1, :])
        P.rel(wbuf + wbf + [swc, sww])
    else:
        P.dma("qsp", rwrow[0:1, :], d_rw.ap())
    P.rel([mrow])
    P.barrier()
    rxg = P.acq1((1, T), F32)
    rxu = P.acq1((1, T), F32)
    rowmul_scalar(P, rxg, rx, rwrow[0:1, 0:1], T)
    rowmul_scalar(P, rxu, rx, rwrow[0:1, 1:2], T)
    P.rel(rx)
    rxgb = P.acq1((128, T), F32)
    rxub = P.acq1((128, T), F32)
    bcast_to_sbuf(P, rxg[0:1, :], rxgb, T)
    bcast_to_sbuf(P, rxu[0:1, :], rxub, T)
    if dbg:
        d_drx = P.dram_out("dbg_rx", [3, T])
        P.dma("qsp", d_drx.ap()[0:1, :], rx[0:1, :])
        P.dma("qsp", d_drx.ap()[1:2, :], rxg[0:1, :])
        P.dma("qsp", d_drx.ap()[2:3, :], rxu[0:1, :])
    P.rel([rxg, rxu])
    pacc = P.acq1((128, T), F32)
    P.memset(pacc[:, :], 0.0)
    if dbg:
        dgate = P.acq1((128, T), F32)
        dup = P.acq1((128, T), F32)
        d_dg = P.dram_out("dbg_gate", [128, T])
        d_du = P.dram_out("dbg_up", [128, T])
    wgb = [P.acq(8, (128, 512), BF16), P.acq(8, (128, 512), BF16)]
    wub = [P.acq(8, (128, 512), BF16), P.acq(8, (128, 512), BF16)]
    sg = P.acq1((128, 512), F32)
    su = P.acq1((128, 512), F32)
    pt_ = P.acq1((128, T), F32)

    def load_gu2(i):
        cs2 = slice(512 * i, 512 * (i + 1))
        for k in range(8):
            P.dma("qpool", wgb[i % 2][k][:, :], d_wqg.ap()[128 * k:128 * (k + 1), cs2])
            P.dma("qpool", wub[i % 2][k][:, :], d_wqu.ap()[128 * k:128 * (k + 1), cs2])
    load_gu2(0)
    for ib in range(8):
        wg_s, wu_s = wgb[ib % 2], wub[ib % 2]
        if ib + 1 < 8:
            load_gu2(ib + 1)
        for ic in range(4):
            gic = 4 * ib + ic
            for g in range(T // 512):
                sl = slice(512 * g, 512 * (g + 1))
                bG = P.bks[g % 2]
                bU = P.bks[2 + (g % 2)]
                for k in range(8):
                    P.mm(bG[:, 0:512], wg_s[k][:, 128 * ic:128 * (ic + 1)], xq[k][:, sl],
                         start=(k == 0), stop=(k == 7))
                for k in range(8):
                    P.mm(bU[:, 0:512], wu_s[k][:, 128 * ic:128 * (ic + 1)], xq[k][:, sl],
                         start=(k == 0), stop=(k == 7))
                P.tt(sg[:, :], bG[:, 0:512], rxgb[:, sl], ALU.mult)
                P.act(sg[:, :], sg[:, :], AF.Silu)
                P.tt(su[:, :], bU[:, 0:512], rxub[:, sl], ALU.mult)
                P.tt(pt_[:, sl], sg[:, :], su[:, :], ALU.mult)
                if dbg and ib == 0 and ic == 0:
                    P.tt(dgate[:, sl], sg[:, :], sg[:, :], ALU.bypass)
                    P.tt(dup[:, sl], su[:, :], su[:, :], ALU.bypass)
            P.act(absx[:, :], pt_[:, :], AF.Abs)
            P.tt(pacc[:, :], pacc[:, :], absx[:, :], ALU.max)
            P.dma("qsp", d_p.ap()[128 * gic:128 * (gic + 1), :], pt_[:, :])
    if dbg:
        P.dma("qsp", d_dg.ap(), dgate[:, :])
        P.dma("qsp", d_du.ap(), dup[:, :])
        d_dp = P.dram_out("dbg_p", [128, T])
        pf = P.acq1((128, T), BF16)
        P.dma("qpool", pf[:, :], d_p.ap()[0:128, :])
        pf2 = P.acq1((128, T), F32)
        P.copy(pf2[:, :], pf[:, :])
        P.dma("qsp", d_dp.ap(), pf2[:, :])
        P.rel([pf, pf2])
    P.rel(wgb[0] + wgb[1] + wub[0] + wub[1] + [sg, su, pt_, rxgb, rxub] + xq)
    P.barrier()
    prow = P.acq1((1, T), F32)
    pmax_to_row(P, pacc, T, d_b16, prow, s1)
    P.ts(prow[0:1, :], prow[0:1, :], 1e-5, None, ALU.max)
    rpd = P.acq1((1, T), F32)
    P.ts(rpd[0:1, :], prow[0:1, :], 0.5 / 127.0, None, ALU.mult)
    rowmul_scalar(P, rpd, rpd, rwrow[0:1, 2:3], T)
    P.op("dve", lambda v, o=prow[0:1, :], i=prow[0:1, :]: v.reciprocal(o, i),
         reads=[prow], writes=[prow])
    P.ts(prow[0:1, :], prow[0:1, :], 127.0, None, ALU.mult)
    spb = P.acq1((128, T), F32)
    rpdb = P.acq1((128, T), F32)
    bcast_to_sbuf(P, prow[0:1, :], spb, T)
    bcast_to_sbuf(P, rpd[0:1, :], rpdb, T)
    oacc = P.acq(8, (128, T), F32)
    pqb = [P.acq(4, (128, T), BF16), P.acq(4, (128, T), BF16)]
    pf32 = P.acq(2, (128, T), F32)
    wdb = [P.acq(4, (128, 1024), BF16), P.acq(4, (128, 1024), BF16)]

    def load_wd(bi, kg):
        for kk in range(4):
            k = 4 * kg + kk
            P.dma("qpool", wdb[bi][kk][:, :], d_wqd.ap()[128 * k:128 * (k + 1), :])
    load_wd(0, 0)
    for kg in range(8):
        pq = pqb[kg % 2]
        for kk in range(4):
            k = 4 * kg + kk
            P.dma("qsp", pf32[kk % 2][:, :], d_p.ap()[128 * k:128 * (k + 1), :])
            P.tt(acc[:, :], pf32[kk % 2][:, :], spb[:, :], ALU.mult)
            P.ts(acc[:, :], acc[:, :], MAGICF, None, ALU.add)
            P.ts(acc[:, :], acc[:, :], MAGICF, 127.0, ALU.subtract, ALU.min)
            P.ts(pq[kk][:, :], acc[:, :], -128.0, None, ALU.max)
        if kg + 1 < 8:
            load_wd((kg + 1) % 2, kg + 1)
        wd_s = wdb[kg % 2]
        for oc in range(8):
            for kk in range(4):
                for g in range(T // 512):
                    b = P.bks[g % 4]
                    P.mm(b[:, 0:512], wd_s[kk][:, 128 * oc:128 * (oc + 1)],
                         pq[kk][:, 512 * g:512 * (g + 1)], start=(kk == 0), stop=(kk == 3))
            for g in range(T // 512):
                sl = slice(512 * g, 512 * (g + 1))
                b = P.bks[g % 4]
                if kg == 0:
                    P.copy(oacc[oc][:, sl], b[:, 0:512])
                else:
                    P.tt(oacc[oc][:, sl], oacc[oc][:, sl], b[:, 0:512], ALU.add)
    P.rel(pqb[0] + pqb[1] + wdb[0] + wdb[1] + [spb, pacc])
    P.barrier()
    x2 = P.acq(2, (128, T), F32)
    for oc in range(8):
        xo = x2[oc % 2]
        P.dma("qsp", xo[:, :], d_xin.ap()[128 * oc:128 * (oc + 1), :])
        P.tt(oacc[oc][:, :], oacc[oc][:, :], rpdb[:, :], ALU.mult)
        P.tt(xo[:, :], xo[:, :], oacc[oc][:, :], ALU.add)
        P.dma("qsp", d_xout.ap()[128 * oc:128 * (oc + 1), :], xo[:, :])
    P.rel(oacc + x2 + pf32 + [rpdb, prow, rpd, rwrow, s1, acc, absx])
    P.finish()



def emit_l0a1(P, T=THALF, n_pair=2, d_xin=None, d_s1=None):
    TF = 2 * T
    NHL = 8
    d_xin = d_xin or P.dram_in("x_in", [D, T])
    d_s1 = d_s1 or P.dram_out("s1_out", [D, T])
    d_xln = P.dram_tmp("d_xln", [D, T])
    d_kvdT = P.dram_in("l0_kvdT", [D, 256])
    d_qdT = P.dram_in("l0_qdT", [D, 256])
    d_ropekT = P.dram_in("l0_ropekT_mine", [D, 256])
    d_kup = P.dram_in("l0_kupermT", [256, 512])
    d_qup = P.dram_in("l0_qupermT", [256, 512])
    d_ropeqT = P.dram_in("l0_ropeqT_mine", [256, 256])
    d_vuT = P.dram_in("l0_vuT_mine", [256, 512])
    d_owT = P.dram_in("l0_owT", [D, D])
    d_bo_zero = P.dram_in("l0_ln1w", [128, 8])
    d_cos = P.dram_in("l0_cos", [128, TF])
    d_sin = P.dram_in("l0_sin", [128, TF])
    d_rotP = P.dram_in("l0_rotP", [128, 128])
    d_selR = P.dram_in("l0_selR", [2 * 128, 128])
    d_cmask = P.dram_in("l0_cmask", [128, 4 * 512])
    d_cmaskT = P.dram_in("l0_cmaskT", [128, 4 * 512])
    d_gidx = P.dram_in("l0_gidx", [128, 8], mybir.dt.int32)
    b_ag1 = P.dram_tmp("b_ag1", [D, T], BF16)
    b_ag1o = P.dram_tmp("b_ag1o", [2 * D, T], BF16)
    b_ag2 = P.dram_tmp("b_ag2", [512, TF], BF16)
    b_ag2o = P.dram_tmp("b_ag2o", [D, TF], BF16)
    d_mrow = P.dram_tmp("d_mrow", [32, 128])
    groups = PAIRS[:n_pair // 2] if n_pair < 8 else PAIRS

    P.std_init()
    P.init_banks()
    identf_init(P)
    x = P.acq(8, (128, T), F32)
    emit_load_x(P, x, d_xin, T)
    xln = P.acq(8, (128, T), F32)
    emit_ln(P, x, xln, T, pre="oln", xout32=None, gcol=None)
    P.rel(x)
    for c in range(8):
        P.dma("qsp", d_xln.ap()[128 * c:128 * (c + 1), :], xln[c][:, :])
    ln1w = P.sb("ln1w", [128, 8], F32)
    P.dma("qsp", ln1w[:, :], d_bo_zero.ap())
    y1 = P.acq(8, (128, T), BF16)
    emit_ln(P, xln, y1, T, pre="ln1", gcol=ln1w)
    P.rel(xln)
    for c in range(8):
        P.dma("qsp", b_ag1.ap()[128 * c:128 * (c + 1), :], y1[c][:, :])
    P.rel(y1)
    P.cc("AllGather", ALU.bypass, groups, [b_ag1.ap()], [b_ag1o.ap()])
    P.barrier()
    y1F_view = b_ag1o.ap().rearrange("(h f) t -> f h t", h=2)
    y1f = P.acq(8, (128, TF), BF16)
    for c in range(8):
        P.dma("qsp", y1f[c][:, :].rearrange("p (h t) -> p h t", h=2),
              y1F_view[128 * c:128 * (c + 1), :, :])
    def fm_proj_full(P, wT_tiles, src, noch, out, bk0=0):
        for oc in range(noch):
            for g in range(TF // 512):
                sl = slice(512 * g, 512 * (g + 1))
                b = P.bks[bk0 + ((oc * 8 + g) % 4)]
                for k in range(len(wT_tiles)):
                    P.mm(b[:, 0:512], wT_tiles[k][:, 128 * oc:128 * (oc + 1)], src[k][:, sl],
                         start=(k == 0), stop=(k == len(wT_tiles) - 1))
                P.copy(out[oc][:, sl], b[:, 0:512], eng="act" if (oc + g) % 2 else "dve")
    wt = load_wt(P, d_kvdT, "wt", 8, 256)
    kvl = P.acq(2, (128, TF), BF16)
    fm_proj_full(P, wt, y1f, 2, kvl)
    P.barrier()
    for k in range(8):
        P.dma("qpool", wt[k][:, 0:256], d_qdT.ap()[128 * k:128 * (k + 1), :])
    ql = P.acq(2, (128, TF), BF16)
    fm_proj_full(P, wt, y1f, 2, ql)
    P.barrier()
    for k in range(8):
        P.dma("qpool", wt[k][:, 0:256], d_ropekT.ap()[128 * k:128 * (k + 1), :])
    krp = P.acq(2, (128, TF), BF16)
    fm_proj_full(P, wt, y1f, 2, krp)
    P.rel(y1f + wt)
    P.barrier()
    cosb = P.acq1((128, TF), BF16)
    sinb = P.acq1((128, TF), BF16)
    P.dma("qpool", cosb[:, :], d_cos.ap())
    P.dma("qpool", sinb[:, :], d_sin.ap())
    rotP = P.acq1((128, 128), BF16)
    P.dma("qpool", rotP[:, :], d_rotP.ap())
    wu2 = P.acq(2, (128, 512), BF16)
    qrp = P.acq(2, (128, TF), BF16)
    for k in range(2):
        P.dma("qpool", wu2[k][:, 0:256], d_ropeqT.ap()[128 * k:128 * (k + 1), :])
    fm_proj_full(P, wu2, ql, 2, qrp)
    s1t = P.acq1((128, 512), F32)
    s2t = P.acq1((128, 512), F32)
    def rot_apply(rp):
        for c in range(2):
            for g in range(TF // 512):
                sl = slice(512 * g, 512 * (g + 1))
                b = P.bks[4 + (g % 2)]
                P.mm(b[:, 0:512], rotP[:, :], rp[c][:, sl])
                P.tt(s1t[:, :], rp[c][:, sl], cosb[:, sl], ALU.mult)
                P.tt(s2t[:, :], b[:, 0:512], sinb[:, sl], ALU.mult)
                P.tt(rp[c][:, sl], s1t[:, :], s2t[:, :], ALU.add)
    rot_apply(qrp)
    rot_apply(krp)
    P.rel([cosb, sinb, rotP])
    P.barrier()
    selR = P.acq(2, (128, 128), BF16)
    for c in range(2):
        P.dma("qpool", selR[c][:, :], d_selR.ap()[128 * c:128 * (c + 1), :])
    qh = P.acq(4, (128, TF), BF16)
    kh = P.acq(4, (128, TF), BF16)
    for (dst, dup_, rp) in [(qh, d_qup, qrp), (kh, d_kup, krp)]:
        for k in range(2):
            P.dma("qpool", wu2[k][:, :], dup_.ap()[128 * k:128 * (k + 1), :])
        for oc in range(4):
            for g in range(TF // 512):
                sl = slice(512 * g, 512 * (g + 1))
                b = P.bks[(oc * 8 + g) % 4]
                P.mm(b[:, 0:512], wu2[0][:, 128 * oc:128 * (oc + 1)], (ql if dst is qh else kvl)[0][:, sl],
                     start=True, stop=False)
                P.mm(b[:, 0:512], wu2[1][:, 128 * oc:128 * (oc + 1)], (ql if dst is qh else kvl)[1][:, sl],
                     start=False, stop=False)
                P.mm(b[:, 0:512], selR[oc % 2][:, :], rp[oc // 2][:, sl],
                     start=False, stop=True)
                P.copy(dst[oc][:, sl], b[:, 0:512], eng="act" if (oc + g) % 2 else "dve")
    P.rel(qrp + krp + ql + wu2 + selR)
    P.barrier()
    vu = P.acq(2, (128, 512), BF16)
    for k in range(2):
        P.dma("qpool", vu[k][:, :], d_vuT.ap()[128 * k:128 * (k + 1), :])
    vt = P.acq(TF // 128, (128, 1024), BF16)
    for ti in range(TF // 128):
        b = P.bks[ti % 4]
        for k in range(2):
            P.mm(b[:, 0:512], kvl[k][:, 128 * ti:128 * (ti + 1)], vu[k][:, :],
                 start=(k == 0), stop=(k == 1))
        P.memset(vt[ti][:, 0:520], 1.0)
        for h8 in range(8):
            P.copy(vt[ti][:, 65 * h8:65 * h8 + 64], b[:, 64 * h8:64 * h8 + 64])
    P.rel(kvl + vu)
    P.barrier()
    SC = 1.0 / 8.0
    cmaskT = P.acq1((128, 2048), BF16)
    P.dma("qpool", cmaskT[:, :], d_cmaskT.ap())
    at = P.acq(4, (128, TF), BF16)
    at2r = P.acq(4, (128, 512), BF16)
    bcall = P.acq1((128, 8), F32)
    nrm = P.acq1((1, 2048), F32)
    zrow = P.acq1((1, 512), F32)
    zrow2 = P.acq1((1, 512), F32)
    sq64 = P.acq(2, (128, 512), F32)
    for h in range(NHL):
        ch, po = h // 2, 64 * (h % 2)
        pb = slice(po, po + 64)
        for ri, src_ in [(0, qh), (1, kh)]:
            for g in range(TF // 512):
                s2_ = sq64[g % 2]
                P.act(s2_[pb, :], src_[ch][pb, 512 * g:512 * (g + 1)], AF.Square)
                bq2 = P.bks[4 + (g % 2)]
                P.mm(bq2[0:1, 0:512], P.onec[pb, 0:1], s2_[pb, :], tile_position=(po, 0))
                P.red(nrm[0:1, 16 + 8 * ri + g:17 + 8 * ri + g], bq2[0:1, 0:512], ALU.max)
        P.copy(nrm[0:1, 8:9], nrm[0:1, 24:25])
        for g in range(1, TF // 512):
            P.tt(nrm[0:1, 8 + g:9 + g], nrm[0:1, 7 + g:8 + g], nrm[0:1, 24 + g:25 + g], ALU.max)
        P.tt(nrm[0:1, 32:40], nrm[0:1, 16:24], nrm[0:1, 8:16], ALU.mult)
        P.act(nrm[0:1, 32:40], nrm[0:1, 32:40], AF.Sqrt)
        P.ts(nrm[0:1, 32:40], nrm[0:1, 32:40], -SC, None, ALU.mult)
        bmh = P.bks[4]
        P.mm(bmh[:, 0:8], P.ones1[0:1, :], nrm[0:1, 32:40])
        P.copy(bcall[:, 0:8], bmh[:, 0:8])
        for qg in range(TF // 512):
            nkt = 4 * qg + 4
            bcol = bcall[:, qg:qg + 1]
            bo_ = P.bks[4]

            def S_mm(kt):
                P.mm(P.bks[kt % 4][:, 0:512], kh[ch][pb, 128 * kt:128 * (kt + 1)],
                     qh[ch][pb, 512 * qg:512 * (qg + 1)], tile_position=(po, 0))
            S_mm(0)
            if nkt > 1:
                S_mm(1)
            for kt in range(nkt):
                b = P.bks[kt % 4]
                at2 = at2r[kt % 4]
                P.act(at2[:, :], b[:, 0:512], AF.Exp, bias=bcol[:, 0:1], scale=SC)
                if kt // 4 == qg:
                    dg = kt % 4
                    P.tt(at2[:, :], at2[:, :], cmaskT[:, 512 * dg:512 * dg + 512], ALU.mult)
                if kt + 2 < nkt:
                    S_mm(kt + 2)
                P.mm(bo_[0:65, 0:512], vt[kt][:, 65 * h:65 * h + 65], at2[:, :],
                     start=(kt == 0), stop=(kt == nkt - 1))
            zr = zrow if qg % 2 == 0 else zrow2
            P.copy(zr[0:1, 0:512], bo_[64:65, 0:512])
            P.op("dve", lambda v, o=zr[0:1, 0:512], i=zr[0:1, 0:512]: v.reciprocal(o, i),
                 reads=[zr], writes=[zr])
            bb = P.bks[5]
            P.mm(bb[0:64, 0:512], P.ones1[0:1, 0:64], zr[0:1, 0:512])
            P.copy(at[ch][pb, 512 * qg:512 * (qg + 1)], bo_[0:64, 0:512])
            P.tt(at[ch][pb, 512 * qg:512 * (qg + 1)], at[ch][pb, 512 * qg:512 * (qg + 1)],
                 bb[0:64, 0:512], ALU.mult)
    P.rel(qh + kh + vt + [cmaskT, nrm, zrow, zrow2, s1t, s2t, bcall] + at2r + sq64)


    P.barrier()
    for c in range(4):
        P.dma("qsp", b_ag2.ap()[128 * c:128 * (c + 1), :], at[c][:, :])
    P.rel(at)
    P.cc("AllGather", ALU.bypass, groups, [b_ag2.ap()], [b_ag2o.ap()])
    P.barrier()
    gidx = P.sb("gidx", [128, 8], mybir.dt.int32)
    P.dma("qsp", gidx[:, :], d_gidx.ap())
    af = P.acq(8, (128, T), BF16)
    gv = b_ag2o.ap().rearrange("d (h t) -> (d h) t", h=2)
    for c in range(8):
        P.idma_gather(af[c][:, :], gv, gidx[:, c:c + 1])
    P.barrier()
    wt8 = load_wt(P, d_owT, "wt8", 8, D)
    x3 = P.acq(2, (128, T), F32)
    scr = P.acq1((128, 512), F32)
    for oc in range(8):
        xo = x3[oc % 2]
        P.dma("qsp", xo[:, :], d_xln.ap()[128 * oc:128 * (oc + 1), :])
        for k in range(8):
            for g in range(T // 512):
                b = P.bks[g % 4]
                P.mm(b[:, 0:512], wt8[k][:, 128 * oc:128 * (oc + 1)],
                     af[k][:, 512 * g:512 * (g + 1)], start=(k == 0), stop=(k == 7))
        for g in range(T // 512):
            sl = slice(512 * g, 512 * (g + 1))
            P.tt(xo[:, sl], xo[:, sl], P.bks[g % 4][:, 0:512], ALU.add)
        P.dma("qsp", d_s1.ap()[128 * oc:128 * (oc + 1), :], xo[:, :])
    P.rel(af + x3 + [scr] + wt8)
    P.finish()



def emit_l0a2(P, T=THALF, d_xin=None, d_s1=None, d_xout=None):
    NE = 7
    I2 = 2048
    d_xin = d_xin or P.dram_in("x_in", [D, T])
    d_s1 = d_s1 or P.dram_in("s1_in", [D, T])
    d_xout = d_xout or P.dram_out("x_out", [D, T])
    d_ln2w = P.dram_in("l0_ln2w", [128, 8])
    d_rwT = P.dram_in("l0_routerT", [D, NE])
    d_rb = P.dram_in("l0_rbias", [1, NE])
    d_guT = P.dram_in("l0_gucatT", [D, 8 * 4096])
    d_dnT = P.dram_in("l0_dncatT", [8 * I2, D])
    d_wscr = P.dram_tmp("wscr0", [128, 128])

    P.std_init()
    P.init_banks()
    identf_init(P)
    s1 = P.acq(8, (128, T), F32)
    for c in range(8):
        P.dma("qsp", s1[c][:, :], d_s1.ap()[128 * c:128 * (c + 1), :])
    ln2w = P.sb("ln2w", [128, 8], F32)
    P.dma("qsp", ln2w[:, :], d_ln2w.ap())
    y2 = P.acq(8, (128, T), BF16)
    mu, rs = emit_ln(P, s1, y2, T, pre="ln2", gcol=ln2w, keep_rows=True)
    P.barrier()
    rwpk = P.acq1((128, 512), F32)
    rwlnT = [rwpk[:, 7 * k:7 * (k + 1)] for k in range(8)]
    for k in range(8):
        P.dma("qsp", rwlnT[k], d_rwT.ap()[128 * k:128 * (k + 1), :])
        P.ts(rwlnT[k], rwlnT[k], ln2w[:, k:k + 1], None, ALU.mult)
    Brow = P.sb("Brow0", [1, NE], F32)
    bank = P.bks[5]
    for k in range(8):
        P.mm(bank[0:1, 0:NE], P.onec[:, 0:1], rwlnT[k], start=(k == 0), stop=(k == 7))
    P.copy(Brow[0:1, :], bank[0:1, 0:NE])
    rbr = P.sb("rbr", [1, NE], F32)
    P.dma("qsp", rbr[:, :], d_rb.ap())
    rb_b = P.sb("rb_b", [128, NE], F32)
    P.mm(bank[:, 0:NE], P.ones1[0:1, :], rbr[0:1, :])
    P.copy(rb_b[:, :], bank[:, 0:NE], eng="act")
    wbuf = P.sb("wbuf0", [128, 128], F32)
    P.memset(wbuf[:, :], 0.0)
    sc = P.acq1((128, 512), F32)
    t1 = P.acq1((128, 512), F32)
    t2 = P.acq1((128, 512), F32)
    col = P.acq1((128, 1), F32)
    for ti in range(T // 128):
        tsl = slice(128 * ti, 128 * (ti + 1))
        b = P.bks[ti % 2]
        for k in range(8):
            P.mm(b[:, 0:NE], s1[k][:, tsl], rwlnT[k], start=(k == 0), stop=(k == 7))
        bmb = P.bks[2 + (ti % 2)]
        P.mm(bmb[:, 0:NE], mu[0:1, tsl], Brow[0:1, :])
        P.mm(P.bks[4][:, 0:1], rs[0:1, tsl], P.ones1[0:1, 0:1])
        P.copy(col[:, 0:1], P.bks[4][:, 0:1])
        P.copy(sc[:, 0:NE], b[:, 0:NE], eng="act")
        P.tt(sc[:, 0:NE], sc[:, 0:NE], bmb[:, 0:NE], ALU.subtract)
        P.ts(sc[:, 0:NE], sc[:, 0:NE], col[:, 0:1], None, ALU.mult)
        P.tt(sc[:, 0:NE], sc[:, 0:NE], rb_b[:, 0:NE], ALU.add)
        P.act(sc[:, 0:NE], sc[:, 0:NE], AF.Sigmoid)
        P.red(col[:, 0:1], sc[:, 0:NE], ALU.max)
        P.ts(t1[:, 0:NE], sc[:, 0:NE], col[:, 0:1], None, ALU.is_ge)
        P.copy(t2[:, 8:9], col[:, 0:1])
        P.ts(t2[:, 0:NE], t1[:, 0:NE], -1e30, None, ALU.mult)
        P.tt(t2[:, 0:NE], t2[:, 0:NE], sc[:, 0:NE], ALU.add)
        P.red(col[:, 0:1], t2[:, 0:NE], ALU.max)
        P.ts(t2[:, 0:NE], t2[:, 0:NE], col[:, 0:1], None, ALU.is_ge)
        P.tt(t1[:, 0:NE], t1[:, 0:NE], t2[:, 0:NE], ALU.add)
        P.tt(col[:, 0:1], col[:, 0:1], t2[:, 8:9], ALU.add)
        P.op("dve", lambda v, o=col[:, 0:1], i=col[:, 0:1]: v.reciprocal(o, i),
             reads=[col], writes=[col])
        P.tt(t1[:, 0:NE], t1[:, 0:NE], sc[:, 0:NE], ALU.mult)
        P.ts(wbuf[:, 8 * ti:8 * ti + NE], t1[:, 0:NE], col[:, 0:1], None, ALU.mult)
    P.rel([sc, t1, t2, col, rwpk, mu, rs])
    bank = P.bks[4]
    P.transp(bank[:, 0:128], wbuf[:, :], P.identf[:, :])
    wtr = P.acq1((128, 512), F32)
    P.copy(wtr[:, 0:128], bank[:, 0:128])
    P.dma("qsp", d_wscr.ap(), wtr[:, 0:128])
    P.rel(wtr)
    x = []
    xts = [P.acq1((128, T), F32), P.acq1((128, T), F32)]
    for c in range(8):
        xt = xts[c % 2]
        P.dma("qsp", xt[:, :], d_xin.ap()[128 * c:128 * (c + 1), :])
        P.stt(s1[c][:, :], s1[c][:, :], 0.5, xt[:, :], ALU.mult, ALU.add)
        x.append(s1[c])
    P.rel(xts)
    P.barrier()
    v = d_wscr.ap().rearrange("(a e) p -> e a p", e=8)
    wrowb = P.acq1((1, T), F32)
    a = P.acq(16, (128, T // 2), BF16)
    gub = [P.acq(8, (128, 512), BF16), P.acq(8, (128, 512), BF16)]
    dnb = [P.acq(4, (128, 1024), BF16), P.acq(4, (128, 1024), BF16)]
    wgtb = P.acq1((128, T // 2), F32)

    def load_gu(buf, e, ib, which):
        col0 = 4096 * e + 2048 * which + 512 * ib
        for k in range(8):
            P.dma("qpool", buf[k][:, :], d_guT.ap()[128 * k:128 * (k + 1),
                  col0:col0 + 512])

    def load_dn(buf, e, q):
        for ic in range(4):
            gi = 16 * e + 4 * q + ic
            P.dma("qpool", buf[ic][:, :], d_dnT.ap()[128 * gi:128 * (gi + 1), :])

    gu_all = [(e, ib, w) for e in range(8) for ib in range(4) for w in (0, 1)] * 2
    dn_all = [(e, q) for e in range(8) for q in range(4)] * 2
    load_gu(gub[0], *gu_all[0])
    load_dn(dnb[0], *dn_all[0])
    gu_i = 0
    dn_i = 0
    for th in range(2):
        for e in range(8):
            if e < NE:
                P.dma("qsp", wrowb[0:1, :].rearrange("x (a p) -> x a p", a=T // 128),
                      v[e:e + 1, :, :])
                for g in range(T // 2 // 512):
                    bnk = P.bks[4 + (g % 2)]
                    P.mm(bnk[:, 0:512], P.ones1[0:1, :],
                         wrowb[0:1, T // 2 * th + 512 * g:T // 2 * th + 512 * (g + 1)])
                    P.copy(wgtb[:, 512 * g:512 * (g + 1)], bnk[:, 0:512], eng="act")
            for ib in range(4):
                gu_s = gub[gu_i % 2]
                if gu_i + 1 < len(gu_all):
                    load_gu(gub[(gu_i + 1) % 2], *gu_all[gu_i + 1])
                gu_i += 1
                for ic in range(4):
                    gic = 4 * ib + ic
                    for k in range(8):
                        for g in range(T // 2 // 512):
                            sl = slice(T // 2 * th + 512 * g, T // 2 * th + 512 * (g + 1))
                            bG = P.bks[(2 * ic + g) % 4]
                            P.mm(bG[:, 0:512], gu_s[k][:, 128 * ic:128 * (ic + 1)], y2[k][:, sl],
                                 start=(k == 0), stop=(k == 7))
                    for g in range(T // 2 // 512):
                        P.act(a[gic][:, 512 * g:512 * (g + 1)],
                              P.bks[(2 * ic + g) % 4][:, 0:512], AF.Silu)
                gu_s = gub[gu_i % 2]
                if gu_i + 1 < len(gu_all):
                    load_gu(gub[(gu_i + 1) % 2], *gu_all[gu_i + 1])
                gu_i += 1
                for ic in range(4):
                    gic = 4 * ib + ic
                    for k in range(8):
                        for g in range(T // 2 // 512):
                            sl = slice(T // 2 * th + 512 * g, T // 2 * th + 512 * (g + 1))
                            bU = P.bks[(2 * ic + g) % 4]
                            P.mm(bU[:, 0:512], gu_s[k][:, 128 * ic:128 * (ic + 1)], y2[k][:, sl],
                                 start=(k == 0), stop=(k == 7))
                    for g in range(T // 2 // 512):
                        lsl = slice(512 * g, 512 * (g + 1))
                        bU = P.bks[(2 * ic + g) % 4]
                        P.tt(a[gic][:, lsl], a[gic][:, lsl], bU[:, 0:512], ALU.mult)
                        if e < NE:
                            P.tt(a[gic][:, lsl], a[gic][:, lsl], wgtb[:, lsl], ALU.mult)
            for q in range(4):
                dn4 = dnb[dn_i % 2]
                if dn_i + 1 < len(dn_all):
                    load_dn(dnb[(dn_i + 1) % 2], *dn_all[dn_i + 1])
                dn_i += 1
                for oc in range(8):
                    for g in range(T // 2 // 512):
                        sl = slice(T // 2 * th + 512 * g, T // 2 * th + 512 * (g + 1))
                        lsl = slice(512 * g, 512 * (g + 1))
                        b = P.bks[4 + (g % 2)]
                        for ic in range(4):
                            P.mm(b[:, 0:512], dn4[ic][:, 128 * oc:128 * (oc + 1)],
                                 a[4 * q + ic][:, lsl], start=(ic == 0), stop=(ic == 3))
                        P.stt(x[oc][:, sl], b[:, 0:512], 0.5, x[oc][:, sl], ALU.mult, ALU.add)
    P.rel(a + gub[0] + gub[1] + dnb[0] + dnb[1] + [wgtb, wrowb] + y2)
    assert gu_i == len(gu_all) and dn_i == len(dn_all)

    P.barrier()
    emit_store_x(P, x, d_xout, T)
    P.rel(x)
    P.finish()



def _T32(a):
    return np.ascontiguousarray(np.asarray(a).T).astype(np.float32)


def _colN(b, n):
    return np.ascontiguousarray(np.asarray(b).reshape(n, 128).T).astype(np.float32)


def _prep(inputs):
    z = inputs
    f32 = np.float32
    g = {}
    g["l0_kvdT"] = _T32(z["L0_ffn_kv_d"])
    g["l0_qdT"] = _T32(z["L0_ffn_q_d"])
    g["l0_owT"] = _T32(z["L0_ffn_o_w"])
    g["l0_ln1w"] = _colN(z["L0_ffn_ln1_w"], 8)
    g["l0_ln2w"] = _colN(z["L0_ffn_ln2_w"], 8)
    g["l0_routerT"] = _T32(z["L0_ffn_router_w"])
    g["l0_rbias"] = np.asarray(z["L0_ffn_routing_bias"]).reshape(1, 7).astype(f32)
    gu = np.zeros((1024, 8 * 4096), f32)
    dn = np.zeros((8 * 2048, 1024), f32)
    for e in range(7):
        gu[:, 4096 * e:4096 * e + 2048] = np.asarray(z["L0_ffn_routed_gate"][e]).T
        gu[:, 4096 * e + 2048:4096 * (e + 1)] = np.asarray(z["L0_ffn_routed_up"][e]).T
        dn[2048 * e:2048 * (e + 1), :] = np.asarray(z["L0_ffn_routed_down"][e]).T
    gu[:, 7 * 4096:7 * 4096 + 2048] = np.asarray(z["L0_ffn_shared_gate"][0]).T
    gu[:, 7 * 4096 + 2048:] = np.asarray(z["L0_ffn_shared_up"][0]).T
    dn[7 * 2048:, :] = np.asarray(z["L0_ffn_shared_down"][0]).T
    g["l0_gucatT"] = gu
    g["l0_dncatT"] = dn
    RD = 32
    inv = 1.0 / (10000.0 ** (np.arange(0, RD, 2) / RD))
    fr = np.arange(4096)[:, None] * inv[None, :]
    emb = np.concatenate([fr, fr], -1)
    cos, sin = np.cos(emb), np.sin(emb)
    c32 = np.zeros((128, 4096), f32)
    s32 = np.zeros((128, 4096), f32)
    for b in range(4):
        c32[32 * b:32 * b + 32, :] = cos.T
        s32[32 * b:32 * b + 32, :] = sin.T
    g["l0_cos"] = c32
    g["l0_sin"] = s32
    rotP = np.zeros((128, 128), f32)
    for b in range(4):
        for d in range(16):
            rotP[32 * b + d + 16, 32 * b + d] = -1.0
        for d in range(16, 32):
            rotP[32 * b + d - 16, 32 * b + d] = 1.0
    g["l0_rotP"] = rotP
    selR = np.zeros((2, 128, 128), f32)
    for par in range(2):
        for hl2 in range(2):
            for j in range(32):
                selR[par, 64 * par + 32 * hl2 + j, 64 * hl2 + j] = 1.0
    g["l0_selR"] = selR.reshape(256, 128)
    cm1 = np.zeros((128, 2048), f32)
    cm2 = np.zeros((128, 2048), f32)
    i_ = np.arange(128)[:, None]
    j_ = np.arange(512)[None, :]
    for dg in range(4):
        cm1[:, 512 * dg:512 * dg + 512] = (j_ <= 128 * dg + i_)
        cm2[:, 512 * dg:512 * dg + 512] = (j_ >= 128 * dg + i_)
    g["l0_cmask"] = cm1
    g["l0_cmaskT"] = cm2
    g["l0b_wqT"] = _T32(z["L0_attn_Wq"]); g["l0b_wkT"] = _T32(z["L0_attn_Wk"])
    g["l0b_wvT"] = _T32(z["L0_attn_Wv"]); g["l0b_woT"] = _T32(z["L0_attn_Wo"])
    g["l0b_bq"] = _colN(z["L0_attn_bq"], 8); g["l0b_bo"] = _colN(z["L0_attn_bo"], 8)
    g["l0b_bkvrow"] = np.stack([np.asarray(z["L0_attn_bk"]), np.asarray(z["L0_attn_bv"])]).astype(f32)
    kk = np.arange(1024)
    ang = 2 * np.pi * ((np.outer(kk, kk) % 1024) / 1024.0)
    g["dftC"] = np.cos(ang).astype(f32)
    g["dftS"] = np.sin(ang).astype(f32)
    g["w_freq_real"] = np.asarray(z["L1_ffn_w_freq_real"]).astype(f32)
    g["w_freq_imag"] = np.asarray(z["L1_ffn_w_freq_imag"]).astype(f32)
    g["bias_chunks"] = _colN(z["L1_ffn_bias"], 8)
    g["l1b_womT"] = _T32(z["L1_attn_W_omega"])
    g["l1b_bom"] = _colN(z["L1_attn_b_omega"], 8)
    g["l1b_lsc"] = _colN(z["L1_attn_log_scale"], 8)
    g["l1b_lngam"] = _colN(z["L1_attn_ln_gamma"], 32)
    g["l1b_lnbet"] = _colN(z["L1_attn_ln_beta"], 32)
    g["l1b_w1T"] = _T32(z["L1_attn_W1"])
    g["l1b_b1"] = _colN(z["L1_attn_b1"], 16)
    g["l1b_w2T"] = _T32(z["L1_attn_W2"])
    g["l1b_b2"] = _colN(z["L1_attn_b2"], 8)
    g["l2f_wgT"] = _T32(z["L2_ffn_w_gate"]); g["l2f_wuT"] = _T32(z["L2_ffn_w_up"])
    g["l2f_wdT"] = _T32(z["L2_ffn_w_down"])
    g["l2b_wqT"] = _T32(z["L2_attn_Wq"]); g["l2b_wkT"] = _T32(z["L2_attn_Wk"])
    g["l2b_wvT"] = _T32(z["L2_attn_Wv"]); g["l2b_woT"] = _T32(z["L2_attn_Wo"])
    g["l2b_bq"] = _colN(z["L2_attn_bq"], 8); g["l2b_bk"] = _colN(z["L2_attn_bk"], 8)
    g["l2b_bo"] = _colN(z["L2_attn_bo"], 8)
    g["l2b_bvrow"] = np.asarray(z["L2_attn_bv"]).reshape(1, 1024).astype(f32)
    pj = np.asarray(z["L2_attn_proj"])
    g["l2b_projT"] = np.concatenate(
        [np.concatenate([pj[2 * i].T, pj[2 * i + 1].T], axis=0) for i in range(8)], axis=0).astype(f32)
    g["l3_gateT"] = _T32(z["L3_ffn_gate_w"])
    g["l3_ebias"] = np.asarray(z["L3_ffn_expert_bias"]).reshape(1, 8).astype(f32)
    wu3 = np.asarray(z["L3_ffn_w_up"]); wd3 = np.asarray(z["L3_ffn_w_down"])
    g["l3_upcatT"] = np.concatenate([wu3[e].T for e in range(8)] +
                                    [np.asarray(z["L3_ffn_shared_w_up"]).T], axis=1).astype(f32)
    g["l3_dncatT"] = np.concatenate([wd3[e].T for e in range(8)] +
                                    [np.asarray(z["L3_ffn_shared_w_down"]).T], axis=0).astype(f32)
    g["l3b_wq"] = np.asarray(z["L3_attn_Wq"]).astype(f32)
    g["l3b_wk"] = np.asarray(z["L3_attn_Wk"]).astype(f32)
    g["l3b_wv"] = np.asarray(z["L3_attn_Wv"]).astype(f32)
    g["l3b_wo"] = np.asarray(z["L3_attn_Wo"]).astype(f32)
    g["l3b_wg1"] = np.asarray(z["L3_attn_Wg1"]).astype(f32)
    g["l3b_wg2"] = np.asarray(z["L3_attn_Wg2"]).astype(f32)
    g["l3b_gamma"] = _colN(z["L3_attn_gamma"], 8)
    g["l3b_beta"] = _colN(z["L3_attn_beta"], 8)
    H, CH = 16, 128
    def build_slopes(n):
        def po2(m):
            start = 2.0 ** (-2.0 ** (-(np.log2(m) - 3)))
            return [start * start ** i for i in range(m)]
        if np.log2(n).is_integer():
            return po2(n)
        closest = 2 ** int(np.floor(np.log2(n)))
        return po2(closest) + build_slopes(2 * closest)[0::2][:n - closest]
    s3 = (np.array(build_slopes(H), dtype=np.float32) * np.float32(1.0 - 0 + 1e-5)).astype(np.float64)
    idx = np.arange(CH)
    maskT = np.zeros((128, 16 * 128), f32)
    lamqb = np.zeros((128, 8 * 128), f32)
    lamk = np.zeros((128, 16), f32)
    lamc = np.zeros((1, 16), f32)
    for h in range(H):
        m = np.where(idx[:, None] >= idx[None, :], np.exp(-s3[h] * (idx[:, None] - idx[None, :])), 0.0)
        maskT[:, 128 * h:128 * (h + 1)] = m.T
        lamk[:, h] = np.exp(-s3[h] * (CH - 1.0 - idx))
        lamc[0, h] = np.exp(-s3[h] * CH)
        hp, h2 = h // 2, h % 2
        lamqb[64 * h2:64 * (h2 + 1), 128 * hp:128 * (hp + 1)] = np.exp(-s3[h] * (idx + 1.0))[None, :]
    g["l3b_maskT"] = maskT; g["l3b_lamqb"] = lamqb
    g["l3b_lamk"] = lamk; g["l3b_lamc"] = lamc
    g["_l3_slopes"] = s3
    pc = []
    for c in range(N_CORES):
        par = c % 2
        d = {}
        h0 = 8 * par
        hsl = slice(h0, h0 + 8)
        qup = np.zeros((512, 256), f32)
        kup = np.zeros((512, 256), f32)
        qu_ = np.asarray(z["L0_ffn_q_u"]); ku_ = np.asarray(z["L0_ffn_k_u"])
        for hl in range(8):
            h = h0 + hl
            dst = 128 * (hl // 2) + 64 * (hl % 2) + 32
            qup[dst:dst + 32, :] = qu_[64 * h + 32:64 * h + 64, :]
            kup[dst:dst + 32, :] = ku_[64 * h + 32:64 * h + 64, :]
        d["l0_qupermT"] = _T32(qup)
        d["l0_kupermT"] = _T32(kup)
        d["l0_ropeqT_mine"] = _T32(np.asarray(z["L0_ffn_rope_q_w"]).reshape(16, 32, 256)[hsl].reshape(256, 256))
        d["l0_ropekT_mine"] = _T32(np.asarray(z["L0_ffn_rope_k_w"]).reshape(16, 32, 1024)[hsl].reshape(256, 1024))
        d["l0_vuT_mine"] = _T32(np.asarray(z["L0_ffn_v_u"]).reshape(16, 64, 256)[hsl].reshape(512, 256))
        gidx = np.zeros((128, 8), np.int32)
        for ch in range(8):
            gidx[:, ch] = 2 * (128 * ch + np.arange(128)) + par
        d["l0_gidx"] = gidx
        p0 = 2048 * par
        d["l1b_isq"] = (1.0 / np.sqrt(np.arange(p0 + 1, p0 + 2049))).reshape(1, -1).astype(f32)
        d["l1b_ipos"] = (1.0 / np.arange(p0 + 1, p0 + 2049)).reshape(1, -1).astype(f32)
        d["l1b_gamcol"] = np.full((128, 1), float(par), f32)
        clam = np.zeros((16, 2048), f32)
        if par == 1:
            for h in range(16):
                lq = np.exp(-s3[h] * (idx + 1.0))
                for cc in range(16):
                    clam[h, 128 * cc:128 * (cc + 1)] = lq * (np.exp(-s3[h] * CH) ** cc)
        d["l3b_clam"] = clam
        pc.append(d)
    return g, pc


_PROG_CACHE = {}


def _get_prog(name):
    if name not in _PROG_CACHE:
        P = Prog(name)
        if name == "full":
            emit_full(P)
        elif name == "l0a1":
            emit_l0a1(P, n_pair=8)
        elif name == "l0a2":
            emit_l0a2(P)
        elif name == "l0b":
            emit_l0b(P, n_pair=8)
        elif name == "l1f":
            emit_l1f(P)
        elif name == "l1b":
            emit_l1b(P, n_pair=8)
        elif name == "l2f":
            emit_l2f(P)
        elif name == "l2b":
            emit_l2b(P, n_pair=8)
        elif name == "l3f":
            emit_l3f(P)
        elif name == "l3b":
            emit_l3b(P, n_pair=8)
        nc = P.build()
        names = set(P.dram_tensors.keys())
        innames = [n for n, t in P.dram_tensors.items()]
        _PROG_CACHE[name] = (P, nc)
    return _PROG_CACHE[name]


def _run(name, xs, g, pc, extra=None):
    P, nc = _get_prog(name)
    need = [n for n, t in P.dram_tensors.items()]
    ims = []
    for c in range(N_CORES):
        im = {}
        for n in need:
            kind = getattr(P.dram_tensors[n], "kind", None)
            if n == "x_in":
                im[n] = xs[c]
            elif extra and n in extra:
                im[n] = extra[n][c]
            elif n in pc[c]:
                im[n] = pc[c][n]
            elif n in g:
                im[n] = g[n]
        ims.append(im)
    res = run_bass_kernel_spmd(nc, ims, core_ids=list(range(N_CORES)))
    return res.results


FUSED = True


def kernel(**inputs):
    g, pc = _prep(inputs)
    x0 = np.asarray(inputs["L0_attn_query"]).astype(np.float32)
    xs = []
    for c in range(N_CORES):
        sh = x0[c // 2, 2048 * (c % 2):2048 * (c % 2 + 1), :]
        xs.append(np.ascontiguousarray(sh.T))
    if FUSED:
        r = _run("full", xs, g, pc)
        out = np.zeros((4, 4096, 1024), np.float32)
        for c in range(N_CORES):
            out[c // 2, 2048 * (c % 2):2048 * (c % 2 + 1), :] = r[c]["x_out"].T
        return out
    def step_l0f():
        nonlocal xs
        r1 = _run("l0a1", xs, g, pc)
        s1 = {"s1_in": [r1[c]["s1_out"] for c in range(N_CORES)]}
        r2 = _run("l0a2", xs, g, pc, extra=s1)
        xs = [r2[c]["x_out"] for c in range(N_CORES)]
    def step(nm):
        nonlocal xs
        r = _run(nm, xs, g, pc)
        xs = [r[c]["x_out"] for c in range(N_CORES)]
    step_l0f()
    step("l0b")
    step_l0f()
    step("l1f")
    step("l1b")
    step("l1f")
    step("l2f")
    step("l2b")
    step("l2f")
    step("l3f")
    step("l3b")
    step("l3f")
    out = np.zeros((4, 4096, 1024), np.float32)
    for c in range(N_CORES):
        out[c // 2, 2048 * (c % 2):2048 * (c % 2 + 1), :] = xs[c].T
    return out


def emit_full(P, T=THALF):
    xin = P.dram_in("x_in", [D, T])
    xout = P.dram_out("x_out", [D, T])
    xA = P.dram_tmp("xstA", [D, T])
    xB = P.dram_tmp("xstB", [D, T])
    P.std_init()
    P.init_banks()
    identf_init(P)
    steps = ["l0f", "l0b", "l0f", "l1f", "l1b", "l1f",
             "l2f", "l2b", "l2f", "l3f", "l3b", "l3f"]
    cur = xin
    P.defer_finish = True
    slablog = []
    for i, stp in enumerate(steps):
        nxt = xout if i == len(steps) - 1 else (xA if i % 2 == 0 else xB)
        P.push_scope(f"s{i}_")
        if stp == "l0f":
            ds1 = P.dram_tmp("s1sc", [D, T])
            emit_l0a1(P, n_pair=8, d_xin=cur, d_s1=ds1)
            P.barrier()
            emit_l0a2(P, d_xin=cur, d_s1=ds1, d_xout=nxt)
        elif stp == "l0b":
            emit_l0b(P, n_pair=8, d_xin=cur, d_xout=nxt)
        elif stp == "l1f":
            emit_l1f(P, d_xin=cur, d_xout=nxt)
        elif stp == "l1b":
            emit_l1b(P, n_pair=8, d_xin=cur, d_xout=nxt)
        elif stp == "l2f":
            emit_l2f(P, d_xin=cur, d_xout=nxt, quant=(i == 6))
        elif stp == "l2b":
            emit_l2b(P, n_pair=8, d_xin=cur, d_xout=nxt)
        elif stp == "l3f":
            emit_l3f(P, d_xin=cur, d_xout=nxt)
        elif stp == "l3b":
            emit_l3b(P, n_pair=8, d_xin=cur, d_xout=nxt)
        P.pop_scope()
        P.barrier()
        cur = nxt
        live = len(getattr(P, "_viewmap", {}))
        slablog.append((stp, len(P._slabs), live))
        if live:
            print(f"step {i} {stp}: {live} leaked pool tiles")
    P.defer_finish = False
    P.finish()
    print("fused slab log:", slablog)
